# version 12
# baseline (speedup 1.0000x reference)
import numpy as np
import concourse.bass as bass
import concourse.mybir as mybir
from concourse.bass_utils import run_bass_kernel_spmd

F32 = mybir.dt.float32
BF16 = mybir.dt.bfloat16
I8 = mybir.dt.int8
U8 = mybir.dt.uint8
AT = mybir.ActivationFunctionType
ALU = mybir.AluOpType

HID = 2048
IFF = 8192
IM = 4096
NH = 64
CS = 256
NTOK = 2048
NCH = NTOK // CS
T = CS
FCX = 16
NIB = 64
EPS = 1e-5
XBCW = 259

RACE_CHECK = False


class Ctr:
    def __init__(self, nc, name):
        self.h = nc.alloc_semaphore(name)
        self.v = 0


def build_nc(n_chunks=NCH, do_ffn1=True, do_mamba=True, do_ffn2=True):
    nc = bass.Bass("TRN2", target_bir_lowering=False,
                   detect_race_conditions=RACE_CHECK)

    xt = nc.dram_tensor("xt", [HID, NTOK], F32, kind="ExternalInput")
    yt = nc.dram_tensor("yt", [HID, NTOK], F32, kind="ExternalOutput")
    wipm = nc.dram_tensor("wipm", [66, 128, 16, 128], F32, kind="ExternalInput")
    wipd = nc.dram_tensor("wipd", [128, 16, 64], F32, kind="ExternalInput")
    wop = nc.dram_tensor("wop", [16, 2, 128, 16, 128], F32, kind="ExternalInput")
    wg8 = nc.dram_tensor("wg8", [64, 128, 16, 128], I8, kind="ExternalInput")
    wu8 = nc.dram_tensor("wu8", [64, 128, 16, 128], I8, kind="ExternalInput")
    wd8 = nc.dram_tensor("wd8", [16, 128, 64, 128], I8, kind="ExternalInput")
    sgate_d = nc.dram_tensor("sgate_d", [128, 64], F32, kind="ExternalInput")
    sup_d = nc.dram_tensor("sup_d", [128, 64], F32, kind="ExternalInput")
    sdownh_d = nc.dram_tensor("sdownh_d", [128, 16], F32, kind="ExternalInput")
    nw_d = nc.dram_tensor("nw_d", [128, 32], F32, kind="ExternalInput")
    convb_d = nc.dram_tensor("convb_d", [128, 34], F32, kind="ExternalInput")
    convw_d = nc.dram_tensor("convw_d", [128, 136], F32, kind="ExternalInput")
    dtb_d = nc.dram_tensor("dtb_d", [64, 1], F32, kind="ExternalInput")
    alog_d = nc.dram_tensor("alog_d", [64, 1], F32, kind="ExternalInput")
    dvec_d = nc.dram_tensor("dvec_d", [1, 64], F32, kind="ExternalInput")
    wgbf = nc.dram_tensor("wgbf", [64, 128, 16, 128], BF16, kind="Internal")
    wubf = nc.dram_tensor("wubf", [64, 128, 16, 128], BF16, kind="Internal")
    wdbf = nc.dram_tensor("wdbf", [16, 128, 64, 128], BF16, kind="Internal")

    ident_d = nc.inline_tensor(np.eye(128, dtype=np.float32), name="ident_c")
    onesr_d = nc.inline_tensor(np.ones((1, 128), dtype=np.float32), name="onesr_c")
    onesc_d = nc.inline_tensor(np.ones((128, 1), dtype=np.float32), name="onesc_c")
    dg0 = np.zeros((128, 256), dtype=np.float32)
    dg0[np.arange(128), np.arange(128)] = 1.0
    dg1 = np.zeros((128, 256), dtype=np.float32)
    dg1[np.arange(128), 128 + np.arange(128)] = 1.0
    diag0_d = nc.inline_tensor(dg0, name="diag0_c")
    diag1_d = nc.inline_tensor(dg1, name="diag1_c")
    zeros_d = nc.inline_tensor(np.zeros((128, 256), dtype=np.float32), name="zeros_c")

    A = nc.alloc_sbuf_tensor
    AT_ = nc.alloc_sbuf_tensor_at
    x_fm = A("x_fm", [128, FCX * T], F32)
    s_all = A("s_all", [128, 4096], F32)

    r_xny = A("r_xny", [128, 32768], U8)
    bx = nc.lookup_mloc(r_xny).addr
    xn32 = AT_("xn32", [128, 4096], F32, offset=bx)
    xnp = AT_("xnp", [128, 8192], BF16, offset=bx + 16384)
    y_fm = AT_("y_fm", [128, 8192], F32, offset=bx)

    r_big = A("r_big", [128, 68160], U8)
    bb = nc.lookup_mloc(r_big).addr
    fusep = AT_("fusep", [128, 32768], BF16, offset=bb)
    gate_fm = AT_("gate_fm", [128, 8192], F32, offset=bb)
    lnsq = AT_("lnsq", [128, 4096], F32, offset=bx)
    xbc = AT_("xbc", [128, 34 * XBCW], F32, offset=bb + 32768)
    ygp = AT_("ygp", [128, 16384], BF16, offset=bb + 32768)

    r_wb = A("r_wb", [128, 32768], U8)
    bw = nc.lookup_mloc(r_wb).addr
    wgb = [AT_("wgb%d" % i, [128, 4096], BF16, offset=bw + i * 16384) for i in range(2)]
    wub = [AT_("wub%d" % i, [128, 4096], BF16, offset=bw + 8192 + i * 16384) for i in range(2)]
    wdb = [AT_("wdb%d" % i, [128, 8192], BF16, offset=bw + i * 16384) for i in range(2)]
    proj32 = [AT_("proj32%d" % i, [128, 2048], F32, offset=bw + i * 8192) for i in range(2)]
    projh = [AT_("projh%d" % i, [128, 2048], BF16, offset=bw + 16384 + i * 4096) for i in range(2)]
    projl = [AT_("projl%d" % i, [128, 2048], BF16, offset=bw + 24576 + i * 4096) for i in range(2)]

    ea_p0 = A("ea_p0", [1, 2048], F32)
    halosave = A("halosave", [128, 102], F32)
    tg = [A("tg%d" % i, [128, 256], F32) for i in range(2)]
    tu = [A("tu%d" % i, [128, 256], F32) for i in range(2)]
    cbt = A("cbt", [128, 512], F32)
    ldec = [A("ldec%d" % i, [128, 512], F32) for i in range(2)]
    sdo = [A("sdo%d" % i, [128, 256], F32) for i in range(2)]
    b_tm = A("b_tm", [128, 256], F32)
    bd_tm = [A("bd_tm%d" % i, [128, 256], F32) for i in range(2)]
    xdt = [A("xdt%d" % i, [128, 128], F32) for i in range(2)]
    dt32 = A("dt32", [64, 256], F32)
    escr = A("escr", [64, 256], F32)
    ea_fm = A("ea_fm", [64, 256], F32)
    dt_tm = A("dt_tm", [128, 128], F32)
    rowA = A("rowA", [1, 256], F32)
    rowB = A("rowB", [1, 256], F32)
    rowC = A("rowC", [1, 256], F32)

    ident = A("ident", [128, 128], F32)
    onesr = A("onesr", [1, 128], F32)
    onesc = A("onesc", [128, 1], F32)
    diag0 = A("diag0", [128, 256], F32)
    diag1 = A("diag1", [128, 256], F32)
    zeros = A("zeros", [128, 256], F32)
    sg_sb = A("sg_sb", [128, 64], F32)
    su_sb = A("su_sb", [128, 64], F32)
    sdh_sb = A("sdh_sb", [128, 16], F32)
    nw_sb = A("nw_sb", [128, 32], F32)
    convb_sb = A("convb_sb", [128, 34], F32)
    convw_sb = A("convw_sb", [128, 136], F32)
    dtb_sb = A("dtb_sb", [64, 1], F32)
    negA = A("negA", [64, 1], F32)
    dexpb = A("dexpb", [128, 64], F32)
    d_p0 = A("d_p0", [1, 64], F32)
    epscol = A("epscol", [128, 1], F32)
    lnacc = A("lnacc", [128, 512], F32)
    sqt = A("sqt", [128, 256], F32)

    pb = [nc.alloc_psum_tensor("pb%d" % i, [128, 512], F32) for i in range(8)]

    c_hw = Ctr(nc, "c_hw")
    c_sw = Ctr(nc, "c_sw")
    c_pe = Ctr(nc, "c_pe")
    c_act = Ctr(nc, "c_act")
    c_vec = Ctr(nc, "c_vec")
    c_gp = Ctr(nc, "c_gp")
    c_st = Ctr(nc, "c_st")

    hiwater = {}

    dry_box = [False]

    def W(eng, key, ctr, val):
        if callable(val):
            val = val()
        if dry_box[0] or val <= 0:
            return
        k = (key, ctr.h.num)
        if hiwater.get(k, 0) < val:
            eng.wait_ge(ctr.h, val)
            hiwater[k] = val

    def I(instr, ctr, k=1):
        if instr is not None:
            instr.then_inc(ctr.h, k)
        ctr.v += k
        return ctr.v

    class _Dry:
        def __getattr__(self, name):
            return lambda *a, **kw: None

    class _PhaseCollector:
        def __init__(self):
            self.items = []

        def _deco(self, key):
            def d(fn):
                self.items.append((key, fn))
            return d

        @property
        def sync(self):
            return self._deco("sync")

        @property
        def gpsimd(self):
            return self._deco("gpsimd")

        @property
        def scalar(self):
            return self._deco("scalar")

        @property
        def tensor(self):
            return self._deco("tensor")

        @property
        def vector(self):
            return self._deco("vector")

    _ctrs = None

    from contextlib import contextmanager

    @contextmanager
    def phase(pname="ph"):
        ph = _PhaseCollector()
        yield ph
        ctrs = (c_hw, c_sw, c_pe, c_act, c_vec, c_gp, c_st)
        snap = [c.v for c in ctrs]
        dry_box[0] = True
        dg = _Dry()
        for _k, fn in ph.items:
            fn(dg)
        dry_box[0] = False
        finals = [c.v for c in ctrs]
        for c, s in zip(ctrs, snap):
            c.v = s
        with nc.named_scope(pname), nc.Block() as blk:
            for key in ("sync", "gpsimd", "scalar", "tensor", "vector"):
                fns = [f for k2, f in ph.items if k2 == key]
                if fns:
                    getattr(blk, key)(lambda eng, fns=fns: [f(eng) for f in fns])
        for c, f in zip(ctrs, finals):
            assert c.v == f, "phase counter mismatch (dry=%d real=%d)" % (f, c.v)

    with nc.Block() as blk:
        marks = {}

        @blk.sync
        def _(sync):
            for dst, src in (
                (ident, ident_d), (onesr, onesr_d), (onesc, onesc_d),
                (diag0, diag0_d), (diag1, diag1_d), (zeros, zeros_d),
                (sg_sb, sgate_d), (su_sb, sup_d), (sdh_sb, sdownh_d),
                (nw_sb, nw_d), (convb_sb, convb_d), (convw_sb, convw_d),
                (dtb_sb, dtb_d), (negA, alog_d), (d_p0, dvec_d),
            ):
                I(sync.dma_start(dst[:, :], src.ap()), c_hw, 16)
            marks["pre"] = c_hw.v

        @blk.gpsimd
        def _(g):
            g.memset(s_all[:, :], 0.0)
            g.memset(ea_p0[:, :], 0.0)
            g.memset(halosave[:, :], 0.0)
            g.memset(epscol[:, :], EPS)

        @blk.scalar
        def _(s):
            W(s, "act", c_hw, marks["pre"])
            s.activation(negA[:, :], negA[:, :], AT.Exp)
            s.mul(negA[:, :], negA[:, :], -1.0)

        @blk.tensor
        def _(t):
            W(t, "pe", c_hw, marks["pre"])
            m = t.matmul(pb[7][:, 0:64], onesr[0:1, :], d_p0[0:1, :],
                         start=True, stop=True)
            marks["dexp"] = I(m, c_pe)

        @blk.vector
        def _(v):
            W(v, "vec", c_pe, marks["dexp"])
            v.tensor_copy(dexpb[:, :], pb[7][:, 0:64])

    castq = {}
    with phase("wcast") as blk:
        @blk.gpsimd
        def _(g):
            for q in range(4):
                i = g.dma_start(wgbf.ap()[16 * q:16 * q + 16],
                                wg8.ap()[16 * q:16 * q + 16])
                castq["wg%d" % q] = I(i, c_sw, 16)
            for q in range(4):
                i = g.dma_start(wubf.ap()[16 * q:16 * q + 16],
                                wu8.ap()[16 * q:16 * q + 16])
                castq["wu%d" % q] = I(i, c_sw, 16)
            for q in range(2):
                i = g.dma_start(wdbf.ap()[8 * q:8 * q + 8],
                                wd8.ap()[8 * q:8 * q + 8])
                castq["wd%d" % q] = I(i, c_sw, 16)


    def emit_ln(blk, pre=(), fc_wait=None, pe_pre=()):
        if True:
            m = {}

            def fcw(eng, key, fc):
                if fc_wait is not None:
                    ctr, val = fc_wait(fc)
                    W(eng, key, ctr, val)
                else:
                    for ctr, val in pre:
                        W(eng, key, ctr, val)

            @blk.scalar
            def _(s):
                for fc in range(FCX):
                    fcw(s, "act", fc)
                    i = s.activation(lnsq[:, fc * T:(fc + 1) * T],
                                     x_fm[:, fc * T:(fc + 1) * T], AT.Square)
                    m["sq%d" % fc] = I(i, c_act)

            for fc in range(FCX):
                @blk.vector
                def _(v, fc=fc):
                    for ctr, val in pre:
                        W(v, "vec", ctr, val)
                    fcw(v, "vec", fc)
                    sl = slice(fc * T, (fc + 1) * T)
                    if fc == 0:
                        v.tensor_copy(lnacc[:, 0:256], x_fm[:, sl])
                    else:
                        v.tensor_add(lnacc[:, 0:256], lnacc[:, 0:256], x_fm[:, sl])
                    W(v, "vec", c_act, m["sq%d" % fc])
                    if fc == 0:
                        i = v.tensor_copy(lnacc[:, 256:512], lnsq[:, sl])
                    else:
                        i = v.tensor_add(lnacc[:, 256:512], lnacc[:, 256:512],
                                         lnsq[:, sl])
                    if fc == FCX - 1:
                        m["acc"] = I(i, c_vec)

            @blk.tensor
            def _(t):
                for ctr, val in pe_pre:
                    W(t, "pe", ctr, val)
                W(t, "pe", c_vec, m["acc"])
                i = t.matmul(pb[6][0:1, 0:512], onesc[:, 0:1], lnacc[:, :],
                             start=True, stop=True)
                m["stats"] = I(i, c_pe)

            @blk.scalar
            def _(s):
                W(s, "act", c_pe, m["stats"])
                s.mul(rowA[0:1, :], pb[6][0:1, 0:256], 1.0 / 2048)
                s.mul(rowB[0:1, :], pb[6][0:1, 256:512], 1.0 / 2048)
                i = s.activation(rowC[0:1, :], rowA[0:1, :], AT.Square)
                m["rows1"] = I(i, c_act)

            @blk.vector
            def _(v):
                W(v, "vec", c_act, m["rows1"])
                i = v.tensor_sub(rowB[0:1, :], rowB[0:1, :], rowC[0:1, :])
                m["var"] = I(i, c_vec)

            @blk.scalar
            def _(s):
                W(s, "act", c_vec, m["var"])
                i = s.activation(rowB[0:1, :], rowB[0:1, :], AT.Sqrt,
                                 bias=epscol[0:1, 0:1])
                m["sqrt"] = I(i, c_act)

            @blk.vector
            def _(v):
                W(v, "vec", c_act, m["sqrt"])
                v.reciprocal(rowC[0:1, :], rowB[0:1, :])
                i = v.scalar_tensor_tensor(rowA[0:1, :], rowA[0:1, :], -1.0,
                                           rowC[0:1, :], ALU.mult, ALU.mult)
                m["rows2"] = I(i, c_vec)

            @blk.tensor
            def _(t):
                W(t, "pe", c_vec, m["rows2"])
                t.matmul(pb[7][:, 0:256], onesr[0:1, :], rowC[0:1, :],
                         start=True, stop=True)
                i = t.matmul(pb[7][:, 256:512], onesr[0:1, :], rowA[0:1, :],
                             start=True, stop=True)
                m["bcast"] = I(i, c_pe)

            def _apply(fc):
                sl = slice(fc * T, (fc + 1) * T)

                @blk.vector
                def _(v, sl=sl, fc=fc):
                    W(v, "vec", c_pe, m["bcast"])
                    v.tensor_mul(xn32[:, sl], x_fm[:, sl], pb[7][:, 0:256])
                    i = v.tensor_add(xn32[:, sl], xn32[:, sl], pb[7][:, 256:512])
                    m["xn%d" % fc] = I(i, c_vec)

                @blk.scalar
                def _(s, sl=sl, fc=fc):
                    W(s, "act", c_vec, m["xn%d" % fc])
                    i = s.copy(xnp[:, fc * 512:fc * 512 + 256], xn32[:, sl])
                    m["hi%d" % fc] = I(i, c_act)

            def _losplit(fc):
                sl = slice(fc * T, (fc + 1) * T)

                @blk.vector
                def _(v, sl=sl, fc=fc):
                    W(v, "vec", c_act, m["hi%d" % fc])
                    v.tensor_sub(xnp[:, fc * 512 + 256:fc * 512 + 512],
                                 xn32[:, sl], xnp[:, fc * 512:fc * 512 + 256])

            for fc in range(FCX):
                _apply(fc)
                if fc >= 1:
                    _losplit(fc - 1)
            _losplit(FCX - 1)

    def emit_gateup_prefetch(blk, pref, waits=()):
        @blk.sync
        def _(sync):
            for ctr, val in waits:
                W(sync, "sy", ctr, val)
            W(sync, "sy", c_sw, castq["wg0"])
            W(sync, "sy", c_sw, castq["wu0"])
            for gr in (0, 1):
                srcg = wgbf.ap()[2 * gr:2 * gr + 2].rearrange("i p k m -> p i (k m)")
                dstg = wgb[gr][:, :].rearrange("p (i km) -> p i km", i=2)
                I(sync.dma_start(dstg, srcg), c_hw, 16)
                srcu = wubf.ap()[2 * gr:2 * gr + 2].rearrange("i p k m -> p i (k m)")
                dstu = wub[gr][:, :].rearrange("p (i km) -> p i km", i=2)
                pref[gr] = I(sync.dma_start(dstu, srcu), c_hw, 16)

    def emit_ffn_gateup(blk, pref_gu, pref_dn, gu_marks):
        if True:
            mk_dma = dict(pref_gu)
            mk_g = {}
            mk_u = {}
            mk_silu = {}
            mk_gsum = {}
            mk_usum = {}
            mk_prod = {}
            mk_hi = {}
            mk_flo = {}
            mk_grp = {}
            for gr in range(32):
                if gr >= 2:
                    @blk.sync
                    def _(g, gr=gr):
                        W(g, "sy", c_pe, mk_grp[gr - 2])
                        W(g, "sy", c_sw, castq["wg%d" % (gr // 8)])
                        W(g, "sy", c_sw, castq["wu%d" % (gr // 8)])
                        src = wgbf.ap()[2 * gr:2 * gr + 2].rearrange("i p k m -> p i (k m)")
                        dst = wgb[gr % 2][:, :].rearrange("p (i km) -> p i km", i=2)
                        I(g.dma_start(dst, src), c_hw, 16)
                        src = wubf.ap()[2 * gr:2 * gr + 2].rearrange("i p k m -> p i (k m)")
                        dst = wub[gr % 2][:, :].rearrange("p (i km) -> p i km", i=2)
                        mk_dma[gr] = I(g.dma_start(dst, src), c_hw, 16)

                for i2 in range(2):
                    ib = 2 * gr + i2
                    par = ib % 2

                    @blk.tensor
                    def _(t, gr=gr, i2=i2, ib=ib, par=par):
                        W(t, "pe", c_hw, mk_dma[gr])
                        if ib >= 2:
                            W(t, "pe", c_vec, mk_gsum[ib - 2])
                            W(t, "pe", c_vec, mk_usum[ib - 2])
                        pg = pb[par][:, 0:512]
                        pu = pb[2 + par][:, 0:512]
                        for kb in range(16):
                            wt = wgb[gr % 2][:, (i2 * 16 + kb) * 128:(i2 * 16 + kb + 1) * 128]
                            i = t.matmul(pg, wt, xnp[:, kb * 512:(kb + 1) * 512],
                                         start=(kb == 0), stop=(kb == 15))
                        mk_g[ib] = I(i, c_pe)
                        for kb in range(16):
                            wt = wub[gr % 2][:, (i2 * 16 + kb) * 128:(i2 * 16 + kb + 1) * 128]
                            i = t.matmul(pu, wt, xnp[:, kb * 512:(kb + 1) * 512],
                                         start=(kb == 0), stop=(kb == 15))
                        mk_u[ib] = I(i, c_pe)
                        if i2 == 1:
                            mk_grp[gr] = mk_u[ib]

                    @blk.vector
                    def _(v, ib=ib, par=par):
                        W(v, "vec", c_pe, mk_g[ib])
                        if ib >= 2:
                            W(v, "vec", c_act, mk_hi[ib - 2])
                        v.tensor_copy(tg[par][:, :], pb[par][:, 256:512])
                        i = v.tensor_add(tg[par][:, :], tg[par][:, :],
                                         pb[par][:, 0:256])
                        mk_gsum[ib] = I(i, c_vec)
                        W(v, "vec", c_pe, mk_u[ib])
                        v.tensor_copy(tu[par][:, :], pb[2 + par][:, 256:512])
                        i = v.tensor_add(tu[par][:, :], tu[par][:, :],
                                         pb[2 + par][:, 0:256])
                        mk_usum[ib] = I(i, c_vec)

                    @blk.scalar
                    def _(s, ib=ib, par=par):
                        W(s, "act", c_vec, mk_gsum[ib])
                        i = s.activation(tg[par][:, :], tg[par][:, :],
                                         AT.Silu, scale=sg_sb[:, ib:ib + 1])
                        mk_silu[ib] = I(i, c_act)

                    @blk.vector
                    def _(v, ib=ib, par=par):
                        W(v, "vec", c_act, mk_silu[ib])
                        i = v.scalar_tensor_tensor(tg[par][:, :], tu[par][:, :],
                                                   su_sb[:, ib:ib + 1],
                                                   tg[par][:, :], ALU.mult, ALU.mult)
                        mk_prod[ib] = I(i, c_vec)

                    @blk.scalar
                    def _(s, ib=ib, par=par):
                        W(s, "act", c_vec, mk_prod[ib])
                        i = s.copy(fusep[:, ib * 512:ib * 512 + 256], tg[par][:, :])
                        mk_hi[ib] = I(i, c_act)

                    @blk.vector
                    def _(v, ib=ib, par=par):
                        W(v, "vec", c_act, mk_hi[ib])
                        i = v.tensor_sub(fusep[:, ib * 512 + 256:ib * 512 + 512],
                                         tg[par][:, :],
                                         fusep[:, ib * 512:ib * 512 + 256])
                        mk_flo[ib] = I(i, c_vec)

            @blk.sync
            def _(sync):
                W(sync, "sy", c_sw, castq["wd0"])
                W(sync, "sy", c_pe, mk_grp[30])
                pref_dn[0] = I(sync.dma_start(
                    wdb[0][:, :], wdbf.ap()[0].rearrange("p k m -> p (k m)")),
                    c_hw, 16)
                W(sync, "sy", c_pe, mk_grp[31])
                pref_dn[1] = I(sync.dma_start(
                    wdb[1][:, :], wdbf.ap()[1].rearrange("p k m -> p (k m)")),
                    c_hw, 16)
            return mk_hi, mk_flo

    def emit_ffn_down(blk, pref_dn, fuse_hi_mk, fuse_lo_mk):
        if True:
            mk_dma = {}
            mk_mm = {}
            mk_ev = {}
            for oc in range(16):
                if oc >= 2:
                    @blk.sync
                    def _(g, oc=oc):
                        W(g, "sy", c_sw, castq["wd%d" % (oc // 8)])
                        W(g, "sy", c_pe, mk_mm[oc - 2])
                        src = wdbf.ap()[oc].rearrange("p k m -> p (k m)")
                        mk_dma[oc] = I(g.dma_start(wdb[oc % 2][:, :], src), c_hw, 16)

                @blk.tensor
                def _(t, oc=oc):
                    par = oc % 2
                    W(t, "pe", c_hw, mk_dma[oc] if oc >= 2 else (lambda oc=oc: pref_dn[oc]))
                    if oc == 0:
                        W(t, "pe", c_act, lambda: fuse_hi_mk[63])
                        W(t, "pe", c_vec, lambda: fuse_lo_mk[63])
                    if oc >= 2:
                        W(t, "pe", c_vec, mk_ev[oc - 2])
                    pd = pb[4 + par][:, 0:512]
                    for kc in range(64):
                        wt = wdb[par][:, kc * 128:(kc + 1) * 128]
                        i = t.matmul(pd, wt, fusep[:, kc * 512:(kc + 1) * 512],
                                     start=(kc == 0), stop=(kc == 63))
                    mk_mm[oc] = I(i, c_pe)

                @blk.vector
                def _(v, oc=oc):
                    par = oc % 2
                    W(v, "vec", c_pe, mk_mm[oc])
                    sl = slice(oc * T, (oc + 1) * T)
                    v.tensor_copy(tu[par][:, :], pb[4 + par][:, 256:512])
                    v.tensor_add(tu[par][:, :], tu[par][:, :], pb[4 + par][:, 0:256])
                    i = v.scalar_tensor_tensor(x_fm[:, sl], tu[par][:, :],
                                               sdh_sb[:, oc:oc + 1],
                                               x_fm[:, sl], ALU.mult, ALU.add)
                    mk_ev[oc] = I(i, c_vec)
            return mk_ev

    def _emit_proj_evac(blk, ob, mk_mm, mk_ev, pslot=None):
        @blk.scalar
        def _(s, ob=ob):
            W(s, "act", c_pe, mk_mm[ob])
            pp = pb[pslot if pslot is not None else ob % 4][:, 0:256]
            if ob < 32:
                i = s.copy(gate_fm[:, ob * T:(ob + 1) * T], pp)
            elif ob < 66:
                off = (ob - 32) * XBCW
                i = s.copy(xbc[:, off + 3:off + XBCW], pp)
            else:
                s.activation(escr[:, :], pp[0:64, :], AT.Exp,
                             bias=dtb_sb[:, 0:1])
                i = s.activation(dt32[:, :], escr[:, :], AT.Ln, bias=onesc[0:64, 0:1])
            mk_ev[ob] = I(i, c_act)

    def _emit_conv_tile(blk, bc, mk_ev, mkc):
        seq = mkc.setdefault("_seq", [])
        prev2 = seq[-2] if len(seq) >= 2 else None
        seq.append(bc)
        par = len(seq) % 2
        off = bc * XBCW

        @blk.vector
        def _(v, bc=bc, par=par, off=off, prev2=prev2):
            W(v, "vec", c_act, mk_ev[32 + bc])
            i = v.tensor_copy(halosave[:, bc * 3:bc * 3 + 3],
                              xbc[:, off + 256:off + 259])
            if prev2 is not None:
                W(v, "vec", c_act, mkc["si%d" % prev2])
            v.tensor_scalar(tg[par][:, :], xbc[:, off + 0:off + 256],
                            convw_sb[:, bc * 4:bc * 4 + 1], None, ALU.mult)
            for k in (1, 2, 3):
                i = v.scalar_tensor_tensor(
                    tg[par][:, :], xbc[:, off + k:off + k + 256],
                    convw_sb[:, bc * 4 + k:bc * 4 + k + 1],
                    tg[par][:, :], ALU.mult, ALU.add)
            mkc["acc%d" % bc] = I(i, c_vec)

        @blk.scalar
        def _(s, bc=bc, par=par, off=off):
            W(s, "act", c_vec, mkc["acc%d" % bc])
            i = s.activation(xbc[:, off + 3:off + XBCW], tg[par][:, :],
                             AT.Silu, bias=convb_sb[:, bc:bc + 1])
            mkc["si%d" % bc] = I(i, c_act)

    def emit_inproj(blk, ch):
        if True:
            mk_load = {}
            mk_hi = {}
            mk_lo = {}
            mk_mm = {}
            mk_ev = {}
            mkc = {}

            @blk.vector
            def _(v):
                heads = xbc[:, :].rearrange("p (bc c) -> p bc c", bc=34)[:, :, 0:3]
                if ch == 0:
                    v.memset(heads, 0.0)
                else:
                    v.tensor_copy(heads, halosave[:, :].rearrange(
                        "p (bc c) -> p bc c", bc=34))

            order = [66, 64, 65] + list(range(64))
            for si, ob in enumerate(order):
                wfull = 2048 if ob < 66 else 1024
                mtile = 128 if ob < 66 else 64

                @blk.sync
                def _(sync, si=si, ob=ob, wfull=wfull):
                    if si >= 2:
                        W(sync, "sy", c_vec, mk_lo[order[si - 2]])
                    if ob < 66:
                        src = wipm.ap()[ob].rearrange("p k m -> p (k m)")
                    else:
                        src = wipd.ap().rearrange("p k m -> p (k m)")
                    mk_load[ob] = I(sync.dma_start(proj32[si % 2][:, 0:wfull], src),
                                    c_hw, 16)

                @blk.vector
                def _(v, si=si, ob=ob, wfull=wfull):
                    W(v, "vec", c_hw, mk_load[ob])
                    if si >= 2:
                        W(v, "vec", c_pe, mk_mm[order[si - 2]])
                    v.tensor_copy(projh[si % 2][:, 0:wfull], proj32[si % 2][:, 0:wfull])
                    i = v.tensor_sub(projl[si % 2][:, 0:wfull],
                                     proj32[si % 2][:, 0:wfull],
                                     projh[si % 2][:, 0:wfull])
                    mk_lo[ob] = I(i, c_vec)

                if si >= 1:
                    prev = order[si - 1]
                    _emit_proj_evac(blk, prev, mk_mm, mk_ev, pslot=(si - 1) % 4)
                    if 32 <= prev < 66:
                        _emit_conv_tile(blk, prev - 32, mk_ev, mkc)
                if si == 2:
                    _emit_dt_tail(blk, mk_ev, mkc)
                if si == 6:
                    _emit_btcb_tail(blk, mk_ev, mkc)

                @blk.tensor
                def _(t, si=si, ob=ob, mtile=mtile):
                    W(t, "pe", c_vec, mk_lo[ob])
                    if si >= 4:
                        W(t, "pe", c_act, mk_ev[order[si - 4]])
                    pp = pb[si % 4][0:mtile, 0:256]
                    for kb in range(16):
                        wh = projh[si % 2][:, kb * mtile:(kb + 1) * mtile]
                        wl = projl[si % 2][:, kb * mtile:(kb + 1) * mtile]
                        xh = xnp[:, kb * 512:kb * 512 + 256]
                        xl = xnp[:, kb * 512 + 256:kb * 512 + 512]
                        t.matmul(pp, wh, xh, start=(kb == 0), stop=False)
                        t.matmul(pp, wh, xl, start=False, stop=False)
                        i = t.matmul(pp, wl, xh, start=False, stop=(kb == 15))
                    mk_mm[ob] = I(i, c_pe)

            _emit_proj_evac(blk, order[-1], mk_mm, mk_ev, pslot=(len(order) - 1) % 4)
            _emit_conv_tile(blk, order[-1] - 32, mk_ev, mkc)

    def _emit_dt_tail(blk, mk_ev, mkc):
        @blk.vector
        def _(v):
            W(v, "vec", c_act, mk_ev[66])
            i = v.tensor_scalar(escr[:, :], dt32[:, :], negA[:, 0:1], None,
                                ALU.mult)
            mkc["a"] = I(i, c_vec)

        @blk.scalar
        def _(s):
            W(s, "act", c_vec, mkc["a"])
            i = s.activation(ea_fm[:, :], escr[:, :], AT.Exp)
            mkc["ea"] = I(i, c_act)

        @blk.tensor
        def _(t):
            W(t, "pe", c_act, mk_ev[66])
            for sc in range(2):
                i = t.transpose(pb[4][:, sc * 64:sc * 64 + 64],
                                dt32[0:64, sc * 128:(sc + 1) * 128],
                                ident[0:64, 0:64])
            mkc["dtT"] = I(i, c_pe)

        @blk.vector
        def _(v):
            W(v, "vec", c_pe, mkc["dtT"])
            v.tensor_copy(dt_tm[:, 0:64], pb[4][:, 0:64])
            i = v.tensor_copy(dt_tm[:, 64:128], pb[4][:, 64:128])
            mkc["dtmev"] = I(i, c_vec)

    def _emit_btcb_tail(blk, mk_ev, mkc):
        @blk.tensor
        def _(t):
            offB = 32 * XBCW
            W(t, "pe", c_act, mkc["si33"])
            for sc in range(2):
                i = t.transpose(pb[5][:, sc * 128:sc * 128 + 128],
                                xbc[:, offB + 3 + sc * 128:offB + 3 + (sc + 1) * 128],
                                ident[:, :])
            mkc["btmT"] = I(i, c_pe)

        @blk.vector
        def _(v):
            W(v, "vec", c_pe, mkc["btmT"])
            v.tensor_copy(b_tm[:, 0:128], pb[5][:, 0:128])
            i = v.tensor_copy(b_tm[:, 128:256], pb[5][:, 128:256])
            mkc["btmev"] = I(i, c_vec)

        @blk.tensor
        def _(t):
            offB = 32 * XBCW
            offC = 33 * XBCW
            W(t, "pe", c_vec, mkc["dtmev"])
            W(t, "pe", c_vec, mkc["btmev"])
            t.matmul(pb[4][:, 256:512],
                     xbc[:, offB + 3:offB + 131],
                     xbc[:, offC + 3:offC + XBCW], start=True, stop=True)
            i = t.matmul(pb[5][:, 256:512],
                         xbc[:, offB + 131:offB + 259],
                         xbc[:, offC + 3:offC + XBCW], start=True, stop=True)
            mkc["cbtmm"] = I(i, c_pe)

        @blk.vector
        def _(v):
            W(v, "vec", c_pe, mkc["cbtmm"])
            v.tensor_copy(cbt[:, 0:256], pb[4][:, 256:512])
            i = v.tensor_copy(cbt[:, 256:512], pb[5][:, 256:512])
            mkc["cbtev"] = I(i, c_vec)

    def emit_ssd(blk, ch):
        if True:
            mk_grp = {}
            mk_eab = {}
            mk_sdo = {}
            mk_ldec = {}
            mk_bd = {}
            mk_gt = {}
            mk_xt = {}
            mk_xe = {}
            mk_ymm = {}
            mk_comb = {}
            mk_upd = {}

            def stage_front(h):
                par = h % 2
                g4 = h // 4
                bp = (h % 2) * 64
                fc = h // 2
                off = fc * XBCW
                if h % 4 == 0:
                    @blk.gpsimd
                    def _(g, h=h, g4=g4):
                        if g4 >= 2:
                            W(g, "gp", c_pe, mk_eab[4 * (g4 - 2) + 3])
                        base = (g4 % 2) * 1024
                        mk_grp[g4] = I(g.dma_start(
                            ea_p0[0:1, base:base + 1024],
                            ea_fm[4 * g4:4 * g4 + 4, :]), c_sw, 16)

                @blk.tensor
                def _(t, h=h, par=par, g4=g4, bp=bp, off=off):
                    W(t, "pe", c_sw, mk_grp[g4])
                    if h >= 2:
                        W(t, "pe", c_vec, mk_sdo[h - 2])
                        W(t, "pe", c_act, mk_xe[h - 2])
                    base = (g4 % 2) * 1024 + (h % 4) * 256
                    i = t.matmul(pb[par][:, 0:256],
                                 onesr[0:1, :], ea_p0[0:1, base:base + 256],
                                 start=True, stop=True)
                    mk_eab[h] = I(i, c_pe)
                    for sc in range(2):
                        i = t.transpose(
                            pb[4 + par][:, sc * 64:sc * 64 + 64],
                            xbc[bp:bp + 64, off + 3 + sc * 128:off + 3 + (sc + 1) * 128],
                            ident[bp:bp + 64, bp:bp + 64])
                    mk_xt[h] = I(i, c_pe)

                @blk.vector
                def _(v, h=h, par=par):
                    W(v, "vec", c_pe, mk_eab[h])
                    if h >= 2:
                        W(v, "vec", c_pe, mk_ymm[h - 2])
                        W(v, "vec", c_vec, mk_upd[h - 2])
                    eab = pb[par][:, 0:256]
                    v.tensor_tensor_scan(ldec[par][:, 0:256], eab, diag0[:, :],
                                         0.0, ALU.mult, ALU.add)
                    i = v.tensor_tensor_scan(ldec[par][:, 256:512], eab,
                                             diag1[:, :], 0.0, ALU.mult, ALU.add)
                    mk_ldec[h] = I(i, c_vec)
                    i = v.tensor_tensor_scan(sdo[par][:, :], eab, zeros[:, :],
                                             1.0, ALU.mult, ALU.add)
                    mk_sdo[h] = I(i, c_vec)

                @blk.scalar
                def _(s, h=h, par=par):
                    W(s, "act", c_vec, mk_ldec[h])
                    if h >= 2:
                        W(s, "act", c_pe, mk_ymm[h - 2])
                    s.mul(bd_tm[par][:, 0:128], b_tm[:, 0:128],
                          ldec[par][:, 255:256])
                    i = s.mul(bd_tm[par][:, 128:256], b_tm[:, 128:256],
                              ldec[par][:, 511:512])
                    mk_bd[h] = I(i, c_act)
                    W(s, "act", c_pe, mk_xt[h])
                    if h >= 2:
                        W(s, "act", c_pe, mk_ymm[h - 2])
                    for sc in range(2):
                        i = s.mul(xdt[par][:, sc * 64:sc * 64 + 64],
                                  pb[4 + par][:, sc * 64:sc * 64 + 64],
                                  dt_tm[:, sc * 64 + h:sc * 64 + h + 1])
                    mk_xe[h] = I(i, c_act)

                @blk.gpsimd
                def _(g, h=h, par=par):
                    W(g, "gp", c_act, mk_bd[h])
                    g.tensor_mul(ldec[par][:, 0:256], ldec[par][:, 0:256],
                                 cbt[:, 0:256])
                    i = g.tensor_mul(ldec[par][:, 256:512], ldec[par][:, 256:512],
                                     cbt[:, 256:512])
                    mk_gt[h] = I(i, c_gp)

            def stage_back(h):
                par = h % 2
                fc = h // 2
                bp = (h % 2) * 64
                off = fc * XBCW

                @blk.tensor
                def _(t, h=h, par=par, bp=bp):
                    W(t, "pe", c_gp, mk_gt[h])
                    W(t, "pe", c_act, mk_xe[h])
                    if h >= 2:
                        W(t, "pe", c_vec, mk_comb[h - 2])
                        W(t, "pe", c_vec, mk_upd[h - 2])
                    for sc in range(2):
                        t.matmul(pb[2 + par][bp:bp + 64, 0:256],
                                 xdt[par][:, sc * 64:sc * 64 + 64],
                                 ldec[par][:, sc * 256:sc * 256 + 256],
                                 start=(sc == 0), stop=(sc == 1))
                    for sc in range(2):
                        t.matmul(pb[6 + par][:, 0:64],
                                 bd_tm[par][:, sc * 128:sc * 128 + 128],
                                 xdt[par][:, sc * 64:sc * 64 + 64],
                                 start=(sc == 0), stop=(sc == 1))
                    i = t.matmul(pb[2 + par][bp:bp + 64, 256:512],
                                 s_all[:, h * 64:h * 64 + 64],
                                 xbc[:, 33 * XBCW + 3:33 * XBCW + XBCW],
                                 start=True, stop=True)
                    mk_ymm[h] = I(i, c_pe)

                @blk.vector
                def _(v, h=h, par=par, fc=fc, bp=bp, off=off):
                    W(v, "vec", c_pe, mk_ymm[h])
                    ysl = y_fm[bp:bp + 64, fc * T:(fc + 1) * T]
                    v.tensor_mul(ysl, pb[2 + par][bp:bp + 64, 256:512],
                                 sdo[par][bp:bp + 64, :])
                    v.tensor_add(ysl, ysl, pb[2 + par][bp:bp + 64, 0:256])
                    i = v.scalar_tensor_tensor(
                        ysl, xbc[bp:bp + 64, off + 3:off + XBCW],
                        dexpb[bp:bp + 64, h:h + 1], ysl, ALU.mult, ALU.add)
                    mk_comb[h] = I(i, c_vec)
                    i = v.scalar_tensor_tensor(
                        s_all[:, h * 64:h * 64 + 64], s_all[:, h * 64:h * 64 + 64],
                        sdo[par][:, 255:256],
                        pb[6 + par][:, 0:64],
                        ALU.mult, ALU.add)
                    mk_upd[h] = I(i, c_vec)

            mk_sg = {}
            mk_gu = {}
            mk_yh = {}
            mk_ylo = {}

            def stage_gate(yc):
                par = yc % 2

                @blk.scalar
                def _(s, yc=yc, par=par):
                    if yc >= 2:
                        W(s, "act", c_vec, mk_gu[yc - 2])
                    i = s.activation(tg[par][:, :],
                                     gate_fm[:, yc * T:(yc + 1) * T], AT.Silu)
                    mk_sg[yc] = I(i, c_act)

                @blk.vector
                def _(v, yc=yc, par=par):
                    W(v, "vec", c_act, mk_sg[yc])
                    W(v, "vec", c_vec, mk_comb[2 * yc + 1])
                    if yc >= 2:
                        W(v, "vec", c_act, mk_yh[yc - 2])
                    i = v.scalar_tensor_tensor(tu[par][:, :],
                                               y_fm[:, yc * T:(yc + 1) * T],
                                               nw_sb[:, yc:yc + 1], tg[par][:, :],
                                               ALU.mult, ALU.mult)
                    mk_gu[yc] = I(i, c_vec)

                @blk.scalar
                def _(s, yc=yc, par=par):
                    W(s, "act", c_vec, mk_gu[yc])
                    i = s.copy(ygp[:, yc * 512:yc * 512 + 256], tu[par][:, :])
                    mk_yh[yc] = I(i, c_act)

                @blk.vector
                def _(v, yc=yc, par=par):
                    W(v, "vec", c_act, mk_yh[yc])
                    i = v.tensor_sub(ygp[:, yc * 512 + 256:yc * 512 + 512],
                                     tu[par][:, :],
                                     ygp[:, yc * 512:yc * 512 + 256])
                    mk_ylo[yc] = I(i, c_vec)

            for h in range(NH + 1):
                if h < NH:
                    stage_front(h)
                if h >= 1:
                    stage_back(h - 1)
                    if (h - 1) % 2 == 1:
                        stage_gate((h - 1) // 2)
            return mk_comb, mk_upd

    def emit_outproj(blk):
        if True:
            mk_load = {}
            mk_hi = {}
            mk_lo = {}
            mk_mm = {}
            mk_ev = {}
            mr = {}

            for yc in range(32):
                @blk.scalar
                def _(s, yc=yc):
                    if yc >= 2:
                        W(s, "act", c_vec, mr["ac%d" % (yc - 2)])
                    i = s.activation(tg[yc % 2][:, :],
                                     y_fm[:, yc * T:(yc + 1) * T], AT.Square)
                    mr["sq%d" % yc] = I(i, c_act)

                @blk.vector
                def _(v, yc=yc):
                    W(v, "vec", c_act, mr["sq%d" % yc])
                    if yc == 0:
                        i = v.tensor_copy(lnacc[:, 0:256], tg[0][:, :])
                    else:
                        i = v.tensor_add(lnacc[:, 0:256], lnacc[:, 0:256],
                                         tg[yc % 2][:, :])
                    mr["ac%d" % yc] = I(i, c_vec)

            @blk.tensor
            def _(t):
                W(t, "pe", c_vec, mr["ac31"])
                i = t.matmul(pb[3][0:1, 0:256], onesc[:, 0:1], lnacc[:, 0:256],
                             start=True, stop=True)
                mr["mm31"] = I(i, c_pe)

            @blk.scalar
            def _(s):
                W(s, "act", c_pe, mr["mm31"])
                i = s.activation(rowB[0:1, :], pb[3][0:1, 0:256], AT.Sqrt,
                                 bias=epscol[0:1, 0:1], scale=1.0 / 4096)
                mr["sqrt"] = I(i, c_act)

            @blk.vector
            def _(v):
                W(v, "vec", c_act, mr["sqrt"])
                i = v.reciprocal(rowC[0:1, :], rowB[0:1, :])
                mr["recip"] = I(i, c_vec)

            @blk.tensor
            def _(t):
                W(t, "pe", c_vec, mr["recip"])
                i = t.matmul(pb[7][:, 0:256], onesr[0:1, :], rowC[0:1, :],
                             start=True, stop=True)
                mr["bcast"] = I(i, c_pe)

            @blk.vector
            def _(v):
                W(v, "vec", c_pe, mr["bcast"])
                i = v.tensor_copy(sdo[0][:, :], pb[7][:, 0:256])
                mr["rstdb"] = I(i, c_vec)
            def _emit_op_evac(blk, oc):
                @blk.vector
                def _(v, oc=oc):
                    W(v, "vec", c_pe, mk_mm[2 * oc + 1])
                    sl = slice(oc * T, (oc + 1) * T)
                    v.tensor_mul(tu[oc % 2][:, :], pb[4 + oc % 2][:, 0:256],
                                 sdo[0][:, :])
                    i = v.tensor_add(x_fm[:, sl], tu[oc % 2][:, :], x_fm[:, sl])
                    mk_ev[oc] = I(i, c_vec)

            for hs in range(32):
                oc, kh = hs // 2, hs % 2

                @blk.sync
                def _(sync, hs=hs, oc=oc, kh=kh):
                    if hs >= 2:
                        W(sync, "sy", c_vec, mk_lo[hs - 2])
                    src = wop.ap()[oc, kh].rearrange("p k m -> p (k m)")
                    mk_load[hs] = I(sync.dma_start(proj32[hs % 2][:, :], src),
                                    c_hw, 16)

                @blk.vector
                def _(v, hs=hs):
                    W(v, "vec", c_hw, mk_load[hs])
                    if hs >= 2:
                        W(v, "vec", c_pe, mk_mm[hs - 2])
                    v.tensor_copy(projh[hs % 2][:, :], proj32[hs % 2][:, :])
                    i = v.tensor_sub(projl[hs % 2][:, :], proj32[hs % 2][:, :],
                                     projh[hs % 2][:, :])
                    mk_lo[hs] = I(i, c_vec)

                if hs >= 1 and hs % 2 == 0:
                    _emit_op_evac(blk, hs // 2 - 1)

                @blk.tensor
                def _(t, hs=hs, oc=oc, kh=kh):
                    W(t, "pe", c_vec, mk_lo[hs])
                    if kh == 0 and oc >= 2:
                        W(t, "pe", c_vec, mk_ev[oc - 2])
                    pd = pb[4 + oc % 2][:, 0:256]
                    for kb in range(16):
                        kg = kh * 16 + kb
                        wh = projh[hs % 2][:, kb * 128:(kb + 1) * 128]
                        wl = projl[hs % 2][:, kb * 128:(kb + 1) * 128]
                        gh = ygp[:, kg * 512:kg * 512 + 256]
                        gl = ygp[:, kg * 512 + 256:kg * 512 + 512]
                        t.matmul(pd, wh, gh, start=(kh == 0 and kb == 0), stop=False)
                        t.matmul(pd, wh, gl, start=False, stop=False)
                        i = t.matmul(pd, wl, gh, start=False,
                                     stop=(kh == 1 and kb == 15))
                    mk_mm[hs] = I(i, c_pe)

            _emit_op_evac(blk, 15)
            return mk_ev, mk_lo, mk_mm, mr

    st_mark = 0
    for ch in range(n_chunks):
        pref_gu1, pref_dn1, pref_gu2, pref_dn2 = {}, {}, {}, {}
        mxbox = {}

        with phase("loadln1_c%d" % ch) as blk:
            @blk.sync
            def _(sync, ch=ch):
                W(sync, "sy", c_st, st_mark)
                srcx = xt.ap()[:, ch * T:(ch + 1) * T].rearrange(
                    "(fc p) t -> p fc t", p=128)
                dstx = x_fm[:, :].rearrange("p (fc t) -> p fc t", fc=FCX)
                mxbox["x"] = I(sync.dma_start(dstx, srcx), c_hw, 16)
            if do_ffn1:
                emit_gateup_prefetch(blk, pref_gu1)
                emit_ln(blk, pre=((c_hw, lambda: mxbox["x"]),))

        if do_ffn1:
            with phase("ffn1_c%d" % ch) as blk:
                gum1 = {}
                fhi, flo = emit_ffn_gateup(blk, pref_gu1, pref_dn1, gum1)
                dn_ev = emit_ffn_down(blk, pref_dn1, fhi, flo)
                if do_mamba:
                    emit_ln(blk, fc_wait=lambda fc: (c_vec, lambda: dn_ev[fc]))
        if do_mamba:
            with phase("inproj_c%d" % ch) as blk:
                emit_inproj(blk, ch)
            with phase("ssdrms_c%d" % ch) as blk:
                emit_ssd(blk, ch)
            with phase("outprojln3_c%d" % ch) as blk:
                op_ev, op_lo, op_mm, op_mr = emit_outproj(blk)
                if do_ffn2:
                    emit_ln(blk, fc_wait=lambda fc: (c_vec, lambda: op_ev[fc]),
                            pre=((c_pe, lambda: op_mr["mm31"]),))
                    emit_gateup_prefetch(blk, pref_gu2,
                                         waits=((c_vec, lambda: op_lo[31]),
                                                (c_pe, lambda: op_mm[31])))
        if do_ffn2:
            with phase("ffn2_c%d" % ch) as blk:
                gum2 = {}
                fhi2, flo2 = emit_ffn_gateup(blk, pref_gu2, pref_dn2, gum2)
                dn_ev2 = emit_ffn_down(blk, pref_dn2, fhi2, flo2)

                @blk.sync
                def _(sync, ch=ch):
                    W(sync, "sy", c_vec, lambda: dn_ev2[15])
                    dsty = yt.ap()[:, ch * T:(ch + 1) * T].rearrange(
                        "(fc p) t -> p fc t", p=128)
                    srcy = x_fm[:, :].rearrange("p (fc t) -> p fc t", fc=FCX)
                    I(sync.dma_start(dsty, srcy), c_st, 16)
        st_mark = c_st.v

    with nc.Block() as blk:
        @blk.sync
        def _(sync):
            sync.wait_ge(c_st.h, c_st.v)

    return nc



def prep_in_maps(L0_attn_hidden_states, L0_attn_in_proj_w, L0_attn_conv_w,
                 L0_attn_conv_b, L0_attn_dt_bias, L0_attn_A_log, L0_attn_D,
                 L0_attn_norm_weight, L0_attn_out_proj_w, L0_ffn_w_gate,
                 L0_ffn_w_up, L0_ffn_w_down, L0_ffn_s_gate, L0_ffn_s_up,
                 L0_ffn_s_down):
    x = np.ascontiguousarray(np.asarray(L0_attn_hidden_states, dtype=np.float32))
    wip = np.asarray(L0_attn_in_proj_w, dtype=np.float32)
    wout = np.asarray(L0_attn_out_proj_w, dtype=np.float32)
    cw = np.asarray(L0_attn_conv_w, dtype=np.float32)
    cb = np.asarray(L0_attn_conv_b, dtype=np.float32)
    dtb = np.asarray(L0_attn_dt_bias, dtype=np.float32)
    alog = np.asarray(L0_attn_A_log, dtype=np.float32)
    dvec = np.asarray(L0_attn_D, dtype=np.float32)
    nw = np.asarray(L0_attn_norm_weight, dtype=np.float32)
    wg = np.asarray(L0_ffn_w_gate)
    wu = np.asarray(L0_ffn_w_up)
    wd = np.asarray(L0_ffn_w_down)
    sgt = np.asarray(L0_ffn_s_gate, dtype=np.float32)
    sut = np.asarray(L0_ffn_s_up, dtype=np.float32)
    sdt = np.asarray(L0_ffn_s_down, dtype=np.float32)

    wipm = np.ascontiguousarray(
        wip[:, :8448].reshape(16, 128, 66, 128).transpose(2, 1, 0, 3))
    wipd = np.ascontiguousarray(
        wip[:, 8448:].reshape(16, 128, 64).transpose(1, 0, 2))
    wopr = np.ascontiguousarray(
        wout.reshape(2, 16, 128, 16, 128).transpose(3, 0, 2, 1, 4))
    wg_r = np.ascontiguousarray(wg.reshape(16, 128, 64, 128).transpose(2, 1, 0, 3))
    wu_r = np.ascontiguousarray(wu.reshape(16, 128, 64, 128).transpose(2, 1, 0, 3))
    wd_r = np.ascontiguousarray(wd.reshape(64, 128, 16, 128).transpose(2, 1, 0, 3))

    shared = {
        "wipm": wipm, "wipd": wipd, "wop": wopr,
        "wg8": wg_r, "wu8": wu_r, "wd8": wd_r,
        "sgate_d": np.ascontiguousarray(sgt.reshape(64, 128).T),
        "sup_d": np.ascontiguousarray(sut.reshape(64, 128).T),
        "sdownh_d": np.ascontiguousarray((0.5 * sdt).reshape(16, 128).T),
        "nw_d": np.ascontiguousarray(nw.reshape(32, 128).T),
        "convb_d": np.ascontiguousarray(cb.reshape(34, 128).T),
        "convw_d": np.ascontiguousarray(
            cw.reshape(34, 128, 4).transpose(1, 0, 2).reshape(128, 136)),
        "dtb_d": dtb.reshape(64, 1),
        "alog_d": alog.reshape(64, 1),
        "dvec_d": dvec.reshape(1, 64),
    }
    in_maps = []
    for i in range(8):
        d = dict(shared)
        d["xt"] = np.ascontiguousarray(x[i].T)
        in_maps.append(d)
    return in_maps


def kernel(**inputs):
    in_maps = prep_in_maps(**inputs)
    nc = build_nc()
    res = run_bass_kernel_spmd(nc, in_maps, core_ids=list(range(8)))
    out = np.empty((8, NTOK, HID), dtype=np.float32)
    for i in range(8):
        out[i] = np.asarray(res.results[i]["yt"]).T
    return out


# version 13
# speedup vs baseline: 1.0835x; 1.0835x over previous
import numpy as np
import concourse.bass as bass
import concourse.mybir as mybir
from concourse.bass_utils import run_bass_kernel_spmd
from concourse.bass_types import AP

F32 = mybir.dt.float32
BF16 = mybir.dt.bfloat16
I8 = mybir.dt.int8
U8 = mybir.dt.uint8
AT = mybir.ActivationFunctionType
ALU = mybir.AluOpType

HID = 2048
IFF = 8192
IM = 4096
NH = 64
CS = 256
NTOK = 2048
NCH = NTOK // CS
T = CS
FCX = 16
NIB = 64
EPS = 1e-5
XBCW = 259

RACE_CHECK = False


class Ctr:
    def __init__(self, nc, name):
        self.h = nc.alloc_semaphore(name)
        self.v = 0


def build_nc(n_chunks=NCH, do_ffn1=True, do_mamba=True, do_ffn2=True):
    nc = bass.Bass("TRN2", target_bir_lowering=False,
                   detect_race_conditions=RACE_CHECK)

    xt = nc.dram_tensor("xt", [HID, NTOK], F32, kind="ExternalInput")
    yt = nc.dram_tensor("yt", [HID, NTOK], F32, kind="ExternalOutput")
    wipm = nc.dram_tensor("wipm", [66, 128, 16, 128], F32, kind="ExternalInput")
    wipd = nc.dram_tensor("wipd", [128, 16, 64], F32, kind="ExternalInput")
    wop = nc.dram_tensor("wop", [16, 2, 128, 16, 128], F32, kind="ExternalInput")
    wg8 = nc.dram_tensor("wg8", [64, 128, 16, 128], I8, kind="ExternalInput")
    wu8 = nc.dram_tensor("wu8", [64, 128, 16, 128], I8, kind="ExternalInput")
    wd8 = nc.dram_tensor("wd8", [16, 128, 64, 128], I8, kind="ExternalInput")
    sgate_d = nc.dram_tensor("sgate_d", [128, 64], F32, kind="ExternalInput")
    sup_d = nc.dram_tensor("sup_d", [128, 64], F32, kind="ExternalInput")
    sdownh_d = nc.dram_tensor("sdownh_d", [128, 16], F32, kind="ExternalInput")
    nw_d = nc.dram_tensor("nw_d", [128, 32], F32, kind="ExternalInput")
    convb_d = nc.dram_tensor("convb_d", [128, 34], F32, kind="ExternalInput")
    convw_d = nc.dram_tensor("convw_d", [128, 136], F32, kind="ExternalInput")
    dtb_d = nc.dram_tensor("dtb_d", [64, 1], F32, kind="ExternalInput")
    alog_d = nc.dram_tensor("alog_d", [64, 1], F32, kind="ExternalInput")
    dvec_d = nc.dram_tensor("dvec_d", [1, 64], F32, kind="ExternalInput")
    wgbf = nc.dram_tensor("wgbf", [64, 128, 16, 128], BF16, kind="Internal")
    wubf = nc.dram_tensor("wubf", [64, 128, 16, 128], BF16, kind="Internal")
    wdbf = nc.dram_tensor("wdbf", [16, 128, 64, 128], BF16, kind="Internal")

    ident_d = nc.inline_tensor(np.eye(128, dtype=np.float32), name="ident_c")
    onesr_d = nc.inline_tensor(np.ones((1, 128), dtype=np.float32), name="onesr_c")
    onesc_d = nc.inline_tensor(np.ones((128, 1), dtype=np.float32), name="onesc_c")
    dg0 = np.zeros((128, 256), dtype=np.float32)
    dg0[np.arange(128), np.arange(128)] = 1.0
    dg1 = np.zeros((128, 256), dtype=np.float32)
    dg1[np.arange(128), 128 + np.arange(128)] = 1.0
    diag0_d = nc.inline_tensor(dg0, name="diag0_c")
    diag1_d = nc.inline_tensor(dg1, name="diag1_c")
    zeros_d = nc.inline_tensor(np.zeros((128, 256), dtype=np.float32), name="zeros_c")

    A = nc.alloc_sbuf_tensor
    AT_ = nc.alloc_sbuf_tensor_at
    x_fm = A("x_fm", [128, FCX * T], F32)
    s_all = A("s_all", [128, 4096], F32)

    r_xny = A("r_xny", [128, 32768], U8)
    bx = nc.lookup_mloc(r_xny).addr
    xn32 = AT_("xn32", [128, 4096], F32, offset=bx)
    xnp = AT_("xnp", [128, 8192], BF16, offset=bx + 16384)
    y_fm = AT_("y_fm", [128, 8192], F32, offset=bx)

    r_big = A("r_big", [128, 68160], U8)
    bb = nc.lookup_mloc(r_big).addr
    fusep = AT_("fusep", [128, 32768], BF16, offset=bb)
    gate_fm = AT_("gate_fm", [128, 8192], F32, offset=bb)
    lnsq = AT_("lnsq", [128, 4096], F32, offset=bx)
    xbc = AT_("xbc", [128, 34 * XBCW], F32, offset=bb + 32768)
    ygp = AT_("ygp", [128, 16384], BF16, offset=bb + 32768)

    r_wb = A("r_wb", [128, 32768], U8)
    bw = nc.lookup_mloc(r_wb).addr
    wgb = [AT_("wgb%d" % i, [128, 4096], BF16, offset=bw + i * 16384) for i in range(2)]
    wub = [AT_("wub%d" % i, [128, 4096], BF16, offset=bw + 8192 + i * 16384) for i in range(2)]
    wdb = [AT_("wdb%d" % i, [128, 8192], BF16, offset=bw + i * 16384) for i in range(2)]
    proj32 = [AT_("proj32%d" % i, [128, 2048], F32, offset=bw + i * 8192) for i in range(2)]
    projh = [AT_("projh%d" % i, [128, 2048], BF16, offset=bw + 16384 + i * 4096) for i in range(2)]
    projl = [AT_("projl%d" % i, [128, 2048], BF16, offset=bw + 24576 + i * 4096) for i in range(2)]

    ea_p0 = A("ea_p0", [1, 2048], F32)
    halosave = A("halosave", [128, 102], F32)
    tg = [A("tg%d" % i, [128, 256], F32) for i in range(2)]
    tu = [A("tu%d" % i, [128, 256], F32) for i in range(2)]
    cbt = A("cbt", [128, 512], F32)
    ldec = [A("ldec%d" % i, [128, 512], F32) for i in range(2)]
    sdo = [A("sdo%d" % i, [128, 256], F32) for i in range(2)]
    b_tm = A("b_tm", [128, 256], F32)
    bd_tm = [A("bd_tm%d" % i, [128, 256], F32) for i in range(2)]
    xdt = [A("xdt%d" % i, [128, 128], F32) for i in range(2)]
    dt32 = A("dt32", [64, 256], F32)
    escr = A("escr", [64, 256], F32)
    ea_fm = A("ea_fm", [64, 256], F32)
    dt_tm = A("dt_tm", [128, 128], F32)
    rowA = A("rowA", [1, 256], F32)
    rowB = A("rowB", [1, 256], F32)
    rowC = A("rowC", [1, 256], F32)

    ident = A("ident", [128, 128], F32)
    onesr = A("onesr", [1, 128], F32)
    onesc = A("onesc", [128, 1], F32)
    diag0 = A("diag0", [128, 256], F32)
    diag1 = A("diag1", [128, 256], F32)
    zeros = A("zeros", [128, 256], F32)
    sg_sb = A("sg_sb", [128, 64], F32)
    su_sb = A("su_sb", [128, 64], F32)
    sdh_sb = A("sdh_sb", [128, 16], F32)
    nw_sb = A("nw_sb", [128, 32], F32)
    convb_sb = A("convb_sb", [128, 34], F32)
    convw_sb = A("convw_sb", [128, 136], F32)
    dtb_sb = A("dtb_sb", [64, 1], F32)
    negA = A("negA", [64, 1], F32)
    dexpb = A("dexpb", [128, 64], F32)
    d_p0 = A("d_p0", [1, 64], F32)
    epscol = A("epscol", [128, 1], F32)
    lnacc = A("lnacc", [128, 512], F32)
    sqt = A("sqt", [128, 512], F32)

    pb = [nc.alloc_psum_tensor("pb%d" % i, [128, 512], F32) for i in range(8)]

    c_hw = Ctr(nc, "c_hw")
    c_sw = Ctr(nc, "c_sw")
    c_pe = Ctr(nc, "c_pe")
    c_act = Ctr(nc, "c_act")
    c_vec = Ctr(nc, "c_vec")
    c_gp = Ctr(nc, "c_gp")
    c_st = Ctr(nc, "c_st")

    hiwater = {}

    dry_box = [False]

    def W(eng, key, ctr, val):
        if callable(val):
            val = val()
        if dry_box[0] or val <= 0:
            return
        k = (key, ctr.h.num)
        if hiwater.get(k, 0) < val:
            eng.wait_ge(ctr.h, val)
            hiwater[k] = val

    def I(instr, ctr, k=1):
        if instr is not None:
            instr.then_inc(ctr.h, k)
        ctr.v += k
        return ctr.v

    class _Dry:
        def __getattr__(self, name):
            return lambda *a, **kw: None

    class _PhaseCollector:
        def __init__(self):
            self.items = []

        def _deco(self, key):
            def d(fn):
                self.items.append((key, fn))
            return d

        @property
        def sync(self):
            return self._deco("sync")

        @property
        def gpsimd(self):
            return self._deco("gpsimd")

        @property
        def scalar(self):
            return self._deco("scalar")

        @property
        def tensor(self):
            return self._deco("tensor")

        @property
        def vector(self):
            return self._deco("vector")

    _ctrs = None

    from contextlib import contextmanager

    @contextmanager
    def phase(pname="ph"):
        ph = _PhaseCollector()
        yield ph
        ctrs = (c_hw, c_sw, c_pe, c_act, c_vec, c_gp, c_st)
        snap = [c.v for c in ctrs]
        dry_box[0] = True
        dg = _Dry()
        for _k, fn in ph.items:
            fn(dg)
        dry_box[0] = False
        finals = [c.v for c in ctrs]
        for c, s in zip(ctrs, snap):
            c.v = s
        with nc.named_scope(pname), nc.Block() as blk:
            for key in ("sync", "gpsimd", "scalar", "tensor", "vector"):
                fns = [f for k2, f in ph.items if k2 == key]
                if fns:
                    getattr(blk, key)(lambda eng, fns=fns: [f(eng) for f in fns])
        for c, f in zip(ctrs, finals):
            assert c.v == f, "phase counter mismatch (dry=%d real=%d)" % (f, c.v)

    with nc.Block() as blk:
        marks = {}

        @blk.sync
        def _(sync):
            for dst, src in (
                (ident, ident_d), (onesr, onesr_d), (onesc, onesc_d),
                (diag0, diag0_d), (diag1, diag1_d), (zeros, zeros_d),
                (sg_sb, sgate_d), (su_sb, sup_d), (sdh_sb, sdownh_d),
                (nw_sb, nw_d), (convb_sb, convb_d), (convw_sb, convw_d),
                (dtb_sb, dtb_d), (negA, alog_d), (d_p0, dvec_d),
            ):
                I(sync.dma_start(dst[:, :], src.ap()), c_hw, 16)
            marks["pre"] = c_hw.v

        @blk.gpsimd
        def _(g):
            g.memset(s_all[:, :], 0.0)
            g.memset(ea_p0[:, :], 0.0)
            g.memset(halosave[:, :], 0.0)
            g.memset(epscol[:, :], EPS)

        @blk.scalar
        def _(s):
            W(s, "act", c_hw, marks["pre"])
            s.activation(negA[:, :], negA[:, :], AT.Exp)
            s.mul(negA[:, :], negA[:, :], -1.0)

        @blk.tensor
        def _(t):
            W(t, "pe", c_hw, marks["pre"])
            m = t.matmul(pb[7][:, 0:64], onesr[0:1, :], d_p0[0:1, :],
                         start=True, stop=True)
            marks["dexp"] = I(m, c_pe)

        @blk.vector
        def _(v):
            W(v, "vec", c_pe, marks["dexp"])
            v.tensor_copy(dexpb[:, :], pb[7][:, 0:64])

    castq = {}
    with phase("wcast") as blk:
        @blk.gpsimd
        def _(g):
            for q in range(4):
                i = g.dma_start(wgbf.ap()[16 * q:16 * q + 16],
                                wg8.ap()[16 * q:16 * q + 16])
                castq["wg%d" % q] = I(i, c_sw, 16)
            for q in range(4):
                i = g.dma_start(wubf.ap()[16 * q:16 * q + 16],
                                wu8.ap()[16 * q:16 * q + 16])
                castq["wu%d" % q] = I(i, c_sw, 16)
            for q in range(2):
                i = g.dma_start(wdbf.ap()[8 * q:8 * q + 8],
                                wd8.ap()[8 * q:8 * q + 8])
                castq["wd%d" % q] = I(i, c_sw, 16)


    def emit_ln(blk, pre=(), fc_wait=None, pe_pre=()):
        if True:
            m = {}

            def fcw(eng, key, fc):
                if fc_wait is not None:
                    ctr, val = fc_wait(fc)
                    W(eng, key, ctr, val)
                else:
                    for ctr, val in pre:
                        W(eng, key, ctr, val)

            @blk.scalar
            def _(s):
                for fc in range(FCX):
                    fcw(s, "act", fc)
                    i = s.activation(lnsq[:, fc * T:(fc + 1) * T],
                                     x_fm[:, fc * T:(fc + 1) * T], AT.Square)
                    m["sq%d" % fc] = I(i, c_act)

            for fc in range(FCX):
                @blk.vector
                def _(v, fc=fc):
                    for ctr, val in pre:
                        W(v, "vec", ctr, val)
                    fcw(v, "vec", fc)
                    sl = slice(fc * T, (fc + 1) * T)
                    if fc == 0:
                        v.tensor_copy(lnacc[:, 0:256], x_fm[:, sl])
                    else:
                        v.tensor_add(lnacc[:, 0:256], lnacc[:, 0:256], x_fm[:, sl])
                    W(v, "vec", c_act, m["sq%d" % fc])
                    if fc == 0:
                        i = v.tensor_copy(lnacc[:, 256:512], lnsq[:, sl])
                    else:
                        i = v.tensor_add(lnacc[:, 256:512], lnacc[:, 256:512],
                                         lnsq[:, sl])
                    if fc == FCX - 1:
                        m["acc"] = I(i, c_vec)

            @blk.tensor
            def _(t):
                for ctr, val in pe_pre:
                    W(t, "pe", ctr, val)
                W(t, "pe", c_vec, m["acc"])
                i = t.matmul(pb[6][0:1, 0:512], onesc[:, 0:1], lnacc[:, :],
                             start=True, stop=True)
                m["stats"] = I(i, c_pe)

            @blk.scalar
            def _(s):
                W(s, "act", c_pe, m["stats"])
                s.mul(rowA[0:1, :], pb[6][0:1, 0:256], 1.0 / 2048)
                s.mul(rowB[0:1, :], pb[6][0:1, 256:512], 1.0 / 2048)
                i = s.activation(rowC[0:1, :], rowA[0:1, :], AT.Square)
                m["rows1"] = I(i, c_act)

            @blk.vector
            def _(v):
                W(v, "vec", c_act, m["rows1"])
                i = v.tensor_sub(rowB[0:1, :], rowB[0:1, :], rowC[0:1, :])
                m["var"] = I(i, c_vec)

            @blk.scalar
            def _(s):
                W(s, "act", c_vec, m["var"])
                i = s.activation(rowB[0:1, :], rowB[0:1, :], AT.Sqrt,
                                 bias=epscol[0:1, 0:1])
                m["sqrt"] = I(i, c_act)

            @blk.vector
            def _(v):
                W(v, "vec", c_act, m["sqrt"])
                v.reciprocal(rowC[0:1, :], rowB[0:1, :])
                i = v.scalar_tensor_tensor(rowA[0:1, :], rowA[0:1, :], -1.0,
                                           rowC[0:1, :], ALU.mult, ALU.mult)
                m["rows2"] = I(i, c_vec)

            @blk.tensor
            def _(t):
                W(t, "pe", c_vec, m["rows2"])
                t.matmul(pb[7][:, 0:256], onesr[0:1, :], rowC[0:1, :],
                         start=True, stop=True)
                i = t.matmul(pb[7][:, 256:512], onesr[0:1, :], rowA[0:1, :],
                             start=True, stop=True)
                m["bcast"] = I(i, c_pe)

            @blk.vector
            def _(v):
                W(v, "vec", c_pe, m["bcast"])
                fcw(v, "vec", FCX - 1)
                x3 = AP(x_fm[:, :].tensor, 0, [[4096, 128], [256, 16], [1, 256]])
                xn3 = AP(xn32[:, :].tensor, 0, [[4096, 128], [256, 16], [1, 256]])
                alb = AP(pb[7][:, :].tensor, 0, [[512, 128], [0, 16], [1, 256]])
                beb = AP(pb[7][:, :].tensor, 256, [[512, 128], [0, 16], [1, 256]])
                v.tensor_tensor(xn3, x3, alb, ALU.mult)
                i = v.tensor_tensor(xn3, xn3, beb, ALU.add)
                m["xn"] = I(i, c_vec)

            @blk.scalar
            def _(s):
                W(s, "act", c_vec, m["xn"])
                xn3 = AP(xn32[:, :].tensor, 0, [[4096, 128], [256, 16], [1, 256]])
                hi3 = AP(xnp[:, :].tensor, 0, [[8192, 128], [512, 16], [1, 256]])
                i = s.copy(hi3, xn3)
                m["hi"] = I(i, c_act)

            @blk.vector
            def _(v):
                W(v, "vec", c_act, m["hi"])
                xn3 = AP(xn32[:, :].tensor, 0, [[4096, 128], [256, 16], [1, 256]])
                hi3 = AP(xnp[:, :].tensor, 0, [[8192, 128], [512, 16], [1, 256]])
                lo3 = AP(xnp[:, :].tensor, 256, [[8192, 128], [512, 16], [1, 256]])
                v.tensor_tensor(lo3, xn3, hi3, ALU.subtract)

    def emit_gateup_prefetch(blk, pref, waits=()):
        @blk.sync
        def _(sync):
            for ctr, val in waits:
                W(sync, "sy", ctr, val)
            W(sync, "sy", c_sw, castq["wg0"])
            W(sync, "sy", c_sw, castq["wu0"])
            for gr in (0, 1):
                srcg = wgbf.ap()[2 * gr:2 * gr + 2].rearrange("i p k m -> p i (k m)")
                dstg = wgb[gr][:, :].rearrange("p (i km) -> p i km", i=2)
                I(sync.dma_start(dstg, srcg), c_hw, 16)
                srcu = wubf.ap()[2 * gr:2 * gr + 2].rearrange("i p k m -> p i (k m)")
                dstu = wub[gr][:, :].rearrange("p (i km) -> p i km", i=2)
                pref[gr] = I(sync.dma_start(dstu, srcu), c_hw, 16)

    def emit_ffn_gateup(blk, pref_gu, pref_dn, gu_marks):
        if True:
            mk_dma = dict(pref_gu)
            mk_g = {}
            mk_u = {}
            mk_silu = {}
            mk_gsum = {}
            mk_usum = {}
            mk_prod = {}
            mk_hi = {}
            mk_flo = {}
            mk_grp = {}
            for gr in range(32):
                if gr >= 2:
                    @blk.sync
                    def _(g, gr=gr):
                        W(g, "sy", c_pe, mk_grp[gr - 2])
                        W(g, "sy", c_sw, castq["wg%d" % (gr // 8)])
                        W(g, "sy", c_sw, castq["wu%d" % (gr // 8)])
                        src = wgbf.ap()[2 * gr:2 * gr + 2].rearrange("i p k m -> p i (k m)")
                        dst = wgb[gr % 2][:, :].rearrange("p (i km) -> p i km", i=2)
                        I(g.dma_start(dst, src), c_hw, 16)
                        src = wubf.ap()[2 * gr:2 * gr + 2].rearrange("i p k m -> p i (k m)")
                        dst = wub[gr % 2][:, :].rearrange("p (i km) -> p i km", i=2)
                        mk_dma[gr] = I(g.dma_start(dst, src), c_hw, 16)

                for i2 in range(2):
                    ib = 2 * gr + i2
                    par = ib % 2

                    @blk.tensor
                    def _(t, gr=gr, i2=i2, ib=ib, par=par):
                        W(t, "pe", c_hw, mk_dma[gr])
                        if ib >= 2:
                            W(t, "pe", c_vec, mk_gsum[ib - 2])
                            W(t, "pe", c_vec, mk_usum[ib - 2])
                        pg = pb[par][:, 0:512]
                        pu = pb[2 + par][:, 0:512]
                        for kb in range(16):
                            wt = wgb[gr % 2][:, (i2 * 16 + kb) * 128:(i2 * 16 + kb + 1) * 128]
                            i = t.matmul(pg, wt, xnp[:, kb * 512:(kb + 1) * 512],
                                         start=(kb == 0), stop=(kb == 15))
                        mk_g[ib] = I(i, c_pe)
                        for kb in range(16):
                            wt = wub[gr % 2][:, (i2 * 16 + kb) * 128:(i2 * 16 + kb + 1) * 128]
                            i = t.matmul(pu, wt, xnp[:, kb * 512:(kb + 1) * 512],
                                         start=(kb == 0), stop=(kb == 15))
                        mk_u[ib] = I(i, c_pe)
                        if i2 == 1:
                            mk_grp[gr] = mk_u[ib]

                    @blk.vector
                    def _(v, ib=ib, par=par):
                        W(v, "vec", c_pe, mk_g[ib])
                        if ib >= 2:
                            W(v, "vec", c_act, mk_hi[ib - 2])
                        v.tensor_copy(tg[par][:, :], pb[par][:, 256:512])
                        i = v.tensor_add(tg[par][:, :], tg[par][:, :],
                                         pb[par][:, 0:256])
                        mk_gsum[ib] = I(i, c_vec)
                        W(v, "vec", c_pe, mk_u[ib])
                        v.tensor_copy(tu[par][:, :], pb[2 + par][:, 256:512])
                        i = v.tensor_add(tu[par][:, :], tu[par][:, :],
                                         pb[2 + par][:, 0:256])
                        mk_usum[ib] = I(i, c_vec)

                    @blk.scalar
                    def _(s, ib=ib, par=par):
                        W(s, "act", c_vec, mk_gsum[ib])
                        i = s.activation(tg[par][:, :], tg[par][:, :],
                                         AT.Silu, scale=sg_sb[:, ib:ib + 1])
                        mk_silu[ib] = I(i, c_act)

                    @blk.vector
                    def _(v, ib=ib, par=par):
                        W(v, "vec", c_act, mk_silu[ib])
                        i = v.scalar_tensor_tensor(tg[par][:, :], tu[par][:, :],
                                                   su_sb[:, ib:ib + 1],
                                                   tg[par][:, :], ALU.mult, ALU.mult)
                        mk_prod[ib] = I(i, c_vec)

                    @blk.scalar
                    def _(s, ib=ib, par=par):
                        W(s, "act", c_vec, mk_prod[ib])
                        i = s.copy(fusep[:, ib * 512:ib * 512 + 256], tg[par][:, :])
                        mk_hi[ib] = I(i, c_act)

                    @blk.vector
                    def _(v, ib=ib, par=par):
                        W(v, "vec", c_act, mk_hi[ib])
                        i = v.tensor_sub(fusep[:, ib * 512 + 256:ib * 512 + 512],
                                         tg[par][:, :],
                                         fusep[:, ib * 512:ib * 512 + 256])
                        mk_flo[ib] = I(i, c_vec)

            @blk.sync
            def _(sync):
                W(sync, "sy", c_sw, castq["wd0"])
                W(sync, "sy", c_pe, mk_grp[30])
                pref_dn[0] = I(sync.dma_start(
                    wdb[0][:, :], wdbf.ap()[0].rearrange("p k m -> p (k m)")),
                    c_hw, 16)
                W(sync, "sy", c_pe, mk_grp[31])
                pref_dn[1] = I(sync.dma_start(
                    wdb[1][:, :], wdbf.ap()[1].rearrange("p k m -> p (k m)")),
                    c_hw, 16)
            return mk_hi, mk_flo

    def emit_ffn_down(blk, pref_dn, fuse_hi_mk, fuse_lo_mk):
        if True:
            mk_dma = {}
            mk_mm = {}
            mk_ev = {}
            for oc in range(16):
                if oc >= 2:
                    @blk.sync
                    def _(g, oc=oc):
                        W(g, "sy", c_sw, castq["wd%d" % (oc // 8)])
                        W(g, "sy", c_pe, mk_mm[oc - 2])
                        src = wdbf.ap()[oc].rearrange("p k m -> p (k m)")
                        mk_dma[oc] = I(g.dma_start(wdb[oc % 2][:, :], src), c_hw, 16)

                @blk.tensor
                def _(t, oc=oc):
                    par = oc % 2
                    W(t, "pe", c_hw, mk_dma[oc] if oc >= 2 else (lambda oc=oc: pref_dn[oc]))
                    if oc == 0:
                        W(t, "pe", c_act, lambda: fuse_hi_mk[63])
                        W(t, "pe", c_vec, lambda: fuse_lo_mk[63])
                    if oc >= 2:
                        W(t, "pe", c_vec, mk_ev[oc - 2])
                    pd = pb[4 + par][:, 0:512]
                    for kc in range(64):
                        wt = wdb[par][:, kc * 128:(kc + 1) * 128]
                        i = t.matmul(pd, wt, fusep[:, kc * 512:(kc + 1) * 512],
                                     start=(kc == 0), stop=(kc == 63))
                    mk_mm[oc] = I(i, c_pe)

                @blk.vector
                def _(v, oc=oc):
                    par = oc % 2
                    W(v, "vec", c_pe, mk_mm[oc])
                    sl = slice(oc * T, (oc + 1) * T)
                    v.tensor_copy(tu[par][:, :], pb[4 + par][:, 256:512])
                    v.tensor_add(tu[par][:, :], tu[par][:, :], pb[4 + par][:, 0:256])
                    i = v.scalar_tensor_tensor(x_fm[:, sl], tu[par][:, :],
                                               sdh_sb[:, oc:oc + 1],
                                               x_fm[:, sl], ALU.mult, ALU.add)
                    mk_ev[oc] = I(i, c_vec)
            return mk_ev

    def _emit_proj_evac(blk, ob, mk_mm, mk_ev, pslot=None):
        @blk.scalar
        def _(s, ob=ob):
            W(s, "act", c_pe, mk_mm[ob])
            pp = pb[pslot if pslot is not None else ob % 4][:, 0:256]
            if ob < 32:
                i = s.copy(gate_fm[:, ob * T:(ob + 1) * T], pp)
            elif ob < 66:
                off = (ob - 32) * XBCW
                i = s.copy(xbc[:, off + 3:off + XBCW], pp)
            else:
                s.activation(escr[:, :], pp[0:64, :], AT.Exp,
                             bias=dtb_sb[:, 0:1])
                i = s.activation(dt32[:, :], escr[:, :], AT.Ln, bias=onesc[0:64, 0:1])
            mk_ev[ob] = I(i, c_act)

    def _emit_conv_tile(blk, bc, mk_ev, mkc):
        seq = mkc.setdefault("_seq", [])
        prev2 = seq[-2] if len(seq) >= 2 else None
        seq.append(bc)
        par = len(seq) % 2
        off = bc * XBCW

        @blk.vector
        def _(v, bc=bc, par=par, off=off, prev2=prev2):
            W(v, "vec", c_act, mk_ev[32 + bc])
            i = v.tensor_copy(halosave[:, bc * 3:bc * 3 + 3],
                              xbc[:, off + 256:off + 259])
            if prev2 is not None:
                W(v, "vec", c_act, mkc["si%d" % prev2])
            v.tensor_scalar(tg[par][:, :], xbc[:, off + 0:off + 256],
                            convw_sb[:, bc * 4:bc * 4 + 1], None, ALU.mult)
            for k in (1, 2, 3):
                i = v.scalar_tensor_tensor(
                    tg[par][:, :], xbc[:, off + k:off + k + 256],
                    convw_sb[:, bc * 4 + k:bc * 4 + k + 1],
                    tg[par][:, :], ALU.mult, ALU.add)
            mkc["acc%d" % bc] = I(i, c_vec)

        @blk.scalar
        def _(s, bc=bc, par=par, off=off):
            W(s, "act", c_vec, mkc["acc%d" % bc])
            i = s.activation(xbc[:, off + 3:off + XBCW], tg[par][:, :],
                             AT.Silu, bias=convb_sb[:, bc:bc + 1])
            mkc["si%d" % bc] = I(i, c_act)

    def emit_inproj(blk, ch):
        if True:
            mk_load = {}
            mk_hi = {}
            mk_lo = {}
            mk_mm = {}
            mk_ev = {}
            mkc = {}

            @blk.vector
            def _(v):
                heads = xbc[:, :].rearrange("p (bc c) -> p bc c", bc=34)[:, :, 0:3]
                if ch == 0:
                    v.memset(heads, 0.0)
                else:
                    v.tensor_copy(heads, halosave[:, :].rearrange(
                        "p (bc c) -> p bc c", bc=34))

            order = [66, 64, 65] + list(range(64))
            for si, ob in enumerate(order):
                wfull = 2048 if ob < 66 else 1024
                mtile = 128 if ob < 66 else 64

                @blk.sync
                def _(sync, si=si, ob=ob, wfull=wfull):
                    if si >= 2:
                        W(sync, "sy", c_vec, mk_lo[order[si - 2]])
                    if ob < 66:
                        src = wipm.ap()[ob].rearrange("p k m -> p (k m)")
                    else:
                        src = wipd.ap().rearrange("p k m -> p (k m)")
                    mk_load[ob] = I(sync.dma_start(proj32[si % 2][:, 0:wfull], src),
                                    c_hw, 16)

                @blk.vector
                def _(v, si=si, ob=ob, wfull=wfull):
                    W(v, "vec", c_hw, mk_load[ob])
                    if si >= 2:
                        W(v, "vec", c_pe, mk_mm[order[si - 2]])
                    v.tensor_copy(projh[si % 2][:, 0:wfull], proj32[si % 2][:, 0:wfull])
                    i = v.tensor_sub(projl[si % 2][:, 0:wfull],
                                     proj32[si % 2][:, 0:wfull],
                                     projh[si % 2][:, 0:wfull])
                    mk_lo[ob] = I(i, c_vec)

                if si >= 1:
                    prev = order[si - 1]
                    _emit_proj_evac(blk, prev, mk_mm, mk_ev, pslot=(si - 1) % 4)
                    if 32 <= prev < 66:
                        _emit_conv_tile(blk, prev - 32, mk_ev, mkc)
                if si == 2:
                    _emit_dt_tail(blk, mk_ev, mkc)
                if si == 6:
                    _emit_btcb_tail(blk, mk_ev, mkc)

                @blk.tensor
                def _(t, si=si, ob=ob, mtile=mtile):
                    W(t, "pe", c_vec, mk_lo[ob])
                    if si >= 4:
                        W(t, "pe", c_act, mk_ev[order[si - 4]])
                    pp = pb[si % 4][0:mtile, 0:256]
                    for kb in range(16):
                        wh = projh[si % 2][:, kb * mtile:(kb + 1) * mtile]
                        wl = projl[si % 2][:, kb * mtile:(kb + 1) * mtile]
                        xh = xnp[:, kb * 512:kb * 512 + 256]
                        xl = xnp[:, kb * 512 + 256:kb * 512 + 512]
                        t.matmul(pp, wh, xh, start=(kb == 0), stop=False)
                        t.matmul(pp, wh, xl, start=False, stop=False)
                        i = t.matmul(pp, wl, xh, start=False, stop=(kb == 15))
                    mk_mm[ob] = I(i, c_pe)

            _emit_proj_evac(blk, order[-1], mk_mm, mk_ev, pslot=(len(order) - 1) % 4)
            _emit_conv_tile(blk, order[-1] - 32, mk_ev, mkc)

    def _emit_dt_tail(blk, mk_ev, mkc):
        @blk.vector
        def _(v):
            W(v, "vec", c_act, mk_ev[66])
            i = v.tensor_scalar(escr[:, :], dt32[:, :], negA[:, 0:1], None,
                                ALU.mult)
            mkc["a"] = I(i, c_vec)

        @blk.scalar
        def _(s):
            W(s, "act", c_vec, mkc["a"])
            i = s.activation(ea_fm[:, :], escr[:, :], AT.Exp)
            mkc["ea"] = I(i, c_act)

        @blk.tensor
        def _(t):
            W(t, "pe", c_act, mk_ev[66])
            for sc in range(2):
                i = t.transpose(pb[4][:, sc * 64:sc * 64 + 64],
                                dt32[0:64, sc * 128:(sc + 1) * 128],
                                ident[0:64, 0:64])
            mkc["dtT"] = I(i, c_pe)

        @blk.vector
        def _(v):
            W(v, "vec", c_pe, mkc["dtT"])
            v.tensor_copy(dt_tm[:, 0:64], pb[4][:, 0:64])
            i = v.tensor_copy(dt_tm[:, 64:128], pb[4][:, 64:128])
            mkc["dtmev"] = I(i, c_vec)

    def _emit_btcb_tail(blk, mk_ev, mkc):
        @blk.tensor
        def _(t):
            offB = 32 * XBCW
            W(t, "pe", c_act, mkc["si33"])
            for sc in range(2):
                i = t.transpose(pb[5][:, sc * 128:sc * 128 + 128],
                                xbc[:, offB + 3 + sc * 128:offB + 3 + (sc + 1) * 128],
                                ident[:, :])
            mkc["btmT"] = I(i, c_pe)

        @blk.vector
        def _(v):
            W(v, "vec", c_pe, mkc["btmT"])
            v.tensor_copy(b_tm[:, 0:128], pb[5][:, 0:128])
            i = v.tensor_copy(b_tm[:, 128:256], pb[5][:, 128:256])
            mkc["btmev"] = I(i, c_vec)

        @blk.tensor
        def _(t):
            offB = 32 * XBCW
            offC = 33 * XBCW
            W(t, "pe", c_vec, mkc["dtmev"])
            W(t, "pe", c_vec, mkc["btmev"])
            t.matmul(pb[4][:, 256:512],
                     xbc[:, offB + 3:offB + 131],
                     xbc[:, offC + 3:offC + XBCW], start=True, stop=True)
            i = t.matmul(pb[5][:, 256:512],
                         xbc[:, offB + 131:offB + 259],
                         xbc[:, offC + 3:offC + XBCW], start=True, stop=True)
            mkc["cbtmm"] = I(i, c_pe)

        @blk.vector
        def _(v):
            W(v, "vec", c_pe, mkc["cbtmm"])
            v.tensor_copy(cbt[:, 0:256], pb[4][:, 256:512])
            i = v.tensor_copy(cbt[:, 256:512], pb[5][:, 256:512])
            mkc["cbtev"] = I(i, c_vec)

    def emit_ssd(blk, ch):
        if True:
            mk_grp = {}
            mk_eab = {}
            mk_sdo = {}
            mk_ldec = {}
            mk_bd = {}
            mk_gt = {}
            mk_xt = {}
            mk_xe = {}
            mk_ymm = {}
            mk_comb = {}
            mk_upd = {}

            def stage_front(h):
                par = h % 2
                g4 = h // 4
                bp = (h % 2) * 64
                fc = h // 2
                off = fc * XBCW
                if h % 4 == 0:
                    @blk.gpsimd
                    def _(g, h=h, g4=g4):
                        if g4 >= 2:
                            W(g, "gp", c_pe, mk_eab[4 * (g4 - 2) + 3])
                        base = (g4 % 2) * 1024
                        mk_grp[g4] = I(g.dma_start(
                            ea_p0[0:1, base:base + 1024],
                            ea_fm[4 * g4:4 * g4 + 4, :]), c_sw, 16)

                @blk.tensor
                def _(t, h=h, par=par, g4=g4, bp=bp, off=off):
                    W(t, "pe", c_sw, mk_grp[g4])
                    if h >= 2:
                        W(t, "pe", c_vec, mk_sdo[h - 2])
                        W(t, "pe", c_act, mk_xe[h - 2])
                    base = (g4 % 2) * 1024 + (h % 4) * 256
                    i = t.matmul(pb[par][:, 0:256],
                                 onesr[0:1, :], ea_p0[0:1, base:base + 256],
                                 start=True, stop=True)
                    mk_eab[h] = I(i, c_pe)
                    for sc in range(2):
                        i = t.transpose(
                            pb[4 + par][:, sc * 64:sc * 64 + 64],
                            xbc[bp:bp + 64, off + 3 + sc * 128:off + 3 + (sc + 1) * 128],
                            ident[bp:bp + 64, bp:bp + 64])
                    mk_xt[h] = I(i, c_pe)

                @blk.vector
                def _(v, h=h, par=par):
                    W(v, "vec", c_pe, mk_eab[h])
                    if h >= 2:
                        W(v, "vec", c_pe, mk_ymm[h - 2])
                        W(v, "vec", c_vec, mk_upd[h - 2])
                    eab = pb[par][:, 0:256]
                    v.tensor_tensor_scan(ldec[par][:, 0:256], eab, diag0[:, :],
                                         0.0, ALU.mult, ALU.add)
                    i = v.tensor_tensor_scan(ldec[par][:, 256:512], eab,
                                             diag1[:, :], 0.0, ALU.mult, ALU.add)
                    mk_ldec[h] = I(i, c_vec)
                    i = v.tensor_tensor_scan(sdo[par][:, :], eab, zeros[:, :],
                                             1.0, ALU.mult, ALU.add)
                    mk_sdo[h] = I(i, c_vec)

                @blk.scalar
                def _(s, h=h, par=par):
                    W(s, "act", c_vec, mk_ldec[h])
                    if h >= 2:
                        W(s, "act", c_pe, mk_ymm[h - 2])
                    s.mul(bd_tm[par][:, 0:128], b_tm[:, 0:128],
                          ldec[par][:, 255:256])
                    i = s.mul(bd_tm[par][:, 128:256], b_tm[:, 128:256],
                              ldec[par][:, 511:512])
                    mk_bd[h] = I(i, c_act)
                    W(s, "act", c_pe, mk_xt[h])
                    if h >= 2:
                        W(s, "act", c_pe, mk_ymm[h - 2])
                    for sc in range(2):
                        i = s.mul(xdt[par][:, sc * 64:sc * 64 + 64],
                                  pb[4 + par][:, sc * 64:sc * 64 + 64],
                                  dt_tm[:, sc * 64 + h:sc * 64 + h + 1])
                    mk_xe[h] = I(i, c_act)

                @blk.gpsimd
                def _(g, h=h, par=par):
                    W(g, "gp", c_act, mk_bd[h])
                    g.tensor_mul(ldec[par][:, 0:256], ldec[par][:, 0:256],
                                 cbt[:, 0:256])
                    i = g.tensor_mul(ldec[par][:, 256:512], ldec[par][:, 256:512],
                                     cbt[:, 256:512])
                    mk_gt[h] = I(i, c_gp)

            def stage_back(h):
                par = h % 2
                fc = h // 2
                bp = (h % 2) * 64
                off = fc * XBCW

                @blk.tensor
                def _(t, h=h, par=par, bp=bp):
                    W(t, "pe", c_gp, mk_gt[h])
                    W(t, "pe", c_act, mk_xe[h])
                    if h >= 2:
                        W(t, "pe", c_vec, mk_comb[h - 2])
                        W(t, "pe", c_vec, mk_upd[h - 2])
                    for sc in range(2):
                        t.matmul(pb[2 + par][bp:bp + 64, 0:256],
                                 xdt[par][:, sc * 64:sc * 64 + 64],
                                 ldec[par][:, sc * 256:sc * 256 + 256],
                                 start=(sc == 0), stop=(sc == 1))
                    for sc in range(2):
                        t.matmul(pb[6 + par][:, 0:64],
                                 bd_tm[par][:, sc * 128:sc * 128 + 128],
                                 xdt[par][:, sc * 64:sc * 64 + 64],
                                 start=(sc == 0), stop=(sc == 1))
                    i = t.matmul(pb[2 + par][bp:bp + 64, 256:512],
                                 s_all[:, h * 64:h * 64 + 64],
                                 xbc[:, 33 * XBCW + 3:33 * XBCW + XBCW],
                                 start=True, stop=True)
                    mk_ymm[h] = I(i, c_pe)

                @blk.vector
                def _(v, h=h, par=par, fc=fc, bp=bp, off=off):
                    W(v, "vec", c_pe, mk_ymm[h])
                    ysl = y_fm[bp:bp + 64, fc * T:(fc + 1) * T]
                    v.tensor_mul(ysl, pb[2 + par][bp:bp + 64, 256:512],
                                 sdo[par][bp:bp + 64, :])
                    v.tensor_add(ysl, ysl, pb[2 + par][bp:bp + 64, 0:256])
                    i = v.scalar_tensor_tensor(
                        ysl, xbc[bp:bp + 64, off + 3:off + XBCW],
                        dexpb[bp:bp + 64, h:h + 1], ysl, ALU.mult, ALU.add)
                    mk_comb[h] = I(i, c_vec)
                    i = v.scalar_tensor_tensor(
                        s_all[:, h * 64:h * 64 + 64], s_all[:, h * 64:h * 64 + 64],
                        sdo[par][:, 255:256],
                        pb[6 + par][:, 0:64],
                        ALU.mult, ALU.add)
                    mk_upd[h] = I(i, c_vec)

            mk_sg = {}
            mk_gu = {}
            mk_yh = {}
            mk_ylo = {}
            mk_rsq = {}
            mk_rac = {}

            def stage_gate(yc):
                par = yc % 2

                @blk.scalar
                def _(s, yc=yc, par=par):
                    if yc >= 2:
                        W(s, "act", c_vec, mk_gu[yc - 2])
                    i = s.activation(tg[par][:, :],
                                     gate_fm[:, yc * T:(yc + 1) * T], AT.Silu)
                    mk_sg[yc] = I(i, c_act)

                @blk.vector
                def _(v, yc=yc, par=par):
                    W(v, "vec", c_act, mk_sg[yc])
                    W(v, "vec", c_vec, mk_comb[2 * yc + 1])
                    if yc >= 2:
                        W(v, "vec", c_act, mk_yh[yc - 2])
                    i = v.scalar_tensor_tensor(tu[par][:, :],
                                               y_fm[:, yc * T:(yc + 1) * T],
                                               nw_sb[:, yc:yc + 1], tg[par][:, :],
                                               ALU.mult, ALU.mult)
                    mk_gu[yc] = I(i, c_vec)

                @blk.scalar
                def _(s, yc=yc, par=par):
                    W(s, "act", c_vec, mk_gu[yc])
                    i = s.copy(ygp[:, yc * 512:yc * 512 + 256], tu[par][:, :])
                    mk_yh[yc] = I(i, c_act)

                @blk.vector
                def _(v, yc=yc, par=par):
                    W(v, "vec", c_act, mk_yh[yc])
                    i = v.tensor_sub(ygp[:, yc * 512 + 256:yc * 512 + 512],
                                     tu[par][:, :],
                                     ygp[:, yc * 512:yc * 512 + 256])
                    mk_ylo[yc] = I(i, c_vec)

                @blk.scalar
                def _(s, yc=yc, par=par):
                    if yc >= 2:
                        W(s, "act", c_vec, mk_rac[yc - 2])
                    i = s.activation(sqt[:, par * 256:par * 256 + 256],
                                     y_fm[:, yc * T:(yc + 1) * T], AT.Square)
                    mk_rsq[yc] = I(i, c_act)

                @blk.vector
                def _(v, yc=yc, par=par):
                    W(v, "vec", c_act, mk_rsq[yc])
                    if yc == 0:
                        i = v.tensor_copy(lnacc[:, 0:256],
                                          sqt[:, 0:256])
                    else:
                        i = v.tensor_add(lnacc[:, 0:256], lnacc[:, 0:256],
                                         sqt[:, par * 256:par * 256 + 256])
                    mk_rac[yc] = I(i, c_vec)

            for h in range(NH + 1):
                if h < NH:
                    stage_front(h)
                if h >= 1:
                    stage_back(h - 1)
                    if (h - 1) % 2 == 1:
                        stage_gate((h - 1) // 2)
            return {"rac": mk_rac}

    def emit_outproj(blk, ssdm):
        if True:
            mk_load = {}
            mk_hi = {}
            mk_lo = {}
            mk_mm = {}
            mk_ev = {}
            mr = {}

            @blk.tensor
            def _(t):
                W(t, "pe", c_vec, lambda: ssdm["rac"][31])
                i = t.matmul(pb[3][0:1, 0:256], onesc[:, 0:1], lnacc[:, 0:256],
                             start=True, stop=True)
                mr["mm31"] = I(i, c_pe)

            @blk.scalar
            def _(s):
                W(s, "act", c_pe, mr["mm31"])
                i = s.activation(rowB[0:1, :], pb[3][0:1, 0:256], AT.Sqrt,
                                 bias=epscol[0:1, 0:1], scale=1.0 / 4096)
                mr["sqrt"] = I(i, c_act)

            @blk.vector
            def _(v):
                W(v, "vec", c_act, mr["sqrt"])
                i = v.reciprocal(rowC[0:1, :], rowB[0:1, :])
                mr["recip"] = I(i, c_vec)

            @blk.tensor
            def _(t):
                W(t, "pe", c_vec, mr["recip"])
                i = t.matmul(pb[7][:, 0:256], onesr[0:1, :], rowC[0:1, :],
                             start=True, stop=True)
                mr["bcast"] = I(i, c_pe)

            @blk.vector
            def _(v):
                W(v, "vec", c_pe, mr["bcast"])
                i = v.tensor_copy(sdo[0][:, :], pb[7][:, 0:256])
                mr["rstdb"] = I(i, c_vec)
            def _emit_op_evac(blk, oc):
                @blk.vector
                def _(v, oc=oc):
                    W(v, "vec", c_pe, mk_mm[2 * oc + 1])
                    sl = slice(oc * T, (oc + 1) * T)
                    v.tensor_mul(tu[oc % 2][:, :], pb[4 + oc % 2][:, 0:256],
                                 sdo[0][:, :])
                    i = v.tensor_add(x_fm[:, sl], tu[oc % 2][:, :], x_fm[:, sl])
                    mk_ev[oc] = I(i, c_vec)

            for hs in range(32):
                oc, kh = hs // 2, hs % 2

                @blk.sync
                def _(sync, hs=hs, oc=oc, kh=kh):
                    if hs >= 2:
                        W(sync, "sy", c_vec, mk_lo[hs - 2])
                    src = wop.ap()[oc, kh].rearrange("p k m -> p (k m)")
                    mk_load[hs] = I(sync.dma_start(proj32[hs % 2][:, :], src),
                                    c_hw, 16)

                @blk.vector
                def _(v, hs=hs):
                    W(v, "vec", c_hw, mk_load[hs])
                    if hs >= 2:
                        W(v, "vec", c_pe, mk_mm[hs - 2])
                    v.tensor_copy(projh[hs % 2][:, :], proj32[hs % 2][:, :])
                    i = v.tensor_sub(projl[hs % 2][:, :], proj32[hs % 2][:, :],
                                     projh[hs % 2][:, :])
                    mk_lo[hs] = I(i, c_vec)

                if hs >= 1 and hs % 2 == 0:
                    _emit_op_evac(blk, hs // 2 - 1)

                @blk.tensor
                def _(t, hs=hs, oc=oc, kh=kh):
                    W(t, "pe", c_vec, mk_lo[hs])
                    if kh == 0 and oc >= 2:
                        W(t, "pe", c_vec, mk_ev[oc - 2])
                    pd = pb[4 + oc % 2][:, 0:256]
                    for kb in range(16):
                        kg = kh * 16 + kb
                        wh = projh[hs % 2][:, kb * 128:(kb + 1) * 128]
                        wl = projl[hs % 2][:, kb * 128:(kb + 1) * 128]
                        gh = ygp[:, kg * 512:kg * 512 + 256]
                        gl = ygp[:, kg * 512 + 256:kg * 512 + 512]
                        t.matmul(pd, wh, gh, start=(kh == 0 and kb == 0), stop=False)
                        t.matmul(pd, wh, gl, start=False, stop=False)
                        i = t.matmul(pd, wl, gh, start=False,
                                     stop=(kh == 1 and kb == 15))
                    mk_mm[hs] = I(i, c_pe)

            _emit_op_evac(blk, 15)
            return mk_ev, mk_lo, mk_mm, mr

    st_mark = 0
    for ch in range(n_chunks):
        pref_gu1, pref_dn1, pref_gu2, pref_dn2 = {}, {}, {}, {}
        mxbox = {}

        with phase("loadln1_c%d" % ch) as blk:
            @blk.sync
            def _(sync, ch=ch):
                W(sync, "sy", c_st, st_mark)
                srcx = xt.ap()[:, ch * T:(ch + 1) * T].rearrange(
                    "(fc p) t -> p fc t", p=128)
                dstx = x_fm[:, :].rearrange("p (fc t) -> p fc t", fc=FCX)
                mxbox["x"] = I(sync.dma_start(dstx, srcx), c_hw, 16)
            if do_ffn1:
                emit_gateup_prefetch(blk, pref_gu1)
                emit_ln(blk, pre=((c_hw, lambda: mxbox["x"]),))

        if do_ffn1:
            with phase("ffn1_c%d" % ch) as blk:
                gum1 = {}
                fhi, flo = emit_ffn_gateup(blk, pref_gu1, pref_dn1, gum1)
                dn_ev = emit_ffn_down(blk, pref_dn1, fhi, flo)
                if do_mamba:
                    emit_ln(blk, fc_wait=lambda fc: (c_vec, lambda: dn_ev[fc]))
        if do_mamba:
            with phase("inproj_c%d" % ch) as blk:
                emit_inproj(blk, ch)
            ssdm = {}
            with phase("ssdrms_c%d" % ch) as blk:
                ssdm.update(emit_ssd(blk, ch))
            with phase("outprojln3_c%d" % ch) as blk:
                op_ev, op_lo, op_mm, op_mr = emit_outproj(blk, ssdm)
                if do_ffn2:
                    emit_ln(blk, fc_wait=lambda fc: (c_vec, lambda: op_ev[fc]),
                            pre=((c_pe, lambda: op_mr["mm31"]),))
                    emit_gateup_prefetch(blk, pref_gu2,
                                         waits=((c_vec, lambda: op_lo[31]),
                                                (c_pe, lambda: op_mm[31])))
        if do_ffn2:
            with phase("ffn2_c%d" % ch) as blk:
                gum2 = {}
                fhi2, flo2 = emit_ffn_gateup(blk, pref_gu2, pref_dn2, gum2)
                dn_ev2 = emit_ffn_down(blk, pref_dn2, fhi2, flo2)

                @blk.sync
                def _(sync, ch=ch):
                    W(sync, "sy", c_vec, lambda: dn_ev2[15])
                    dsty = yt.ap()[:, ch * T:(ch + 1) * T].rearrange(
                        "(fc p) t -> p fc t", p=128)
                    srcy = x_fm[:, :].rearrange("p (fc t) -> p fc t", fc=FCX)
                    I(sync.dma_start(dsty, srcy), c_st, 16)
        st_mark = c_st.v

    with nc.Block() as blk:
        @blk.sync
        def _(sync):
            sync.wait_ge(c_st.h, c_st.v)

    return nc



def prep_in_maps(L0_attn_hidden_states, L0_attn_in_proj_w, L0_attn_conv_w,
                 L0_attn_conv_b, L0_attn_dt_bias, L0_attn_A_log, L0_attn_D,
                 L0_attn_norm_weight, L0_attn_out_proj_w, L0_ffn_w_gate,
                 L0_ffn_w_up, L0_ffn_w_down, L0_ffn_s_gate, L0_ffn_s_up,
                 L0_ffn_s_down):
    x = np.ascontiguousarray(np.asarray(L0_attn_hidden_states, dtype=np.float32))
    wip = np.asarray(L0_attn_in_proj_w, dtype=np.float32)
    wout = np.asarray(L0_attn_out_proj_w, dtype=np.float32)
    cw = np.asarray(L0_attn_conv_w, dtype=np.float32)
    cb = np.asarray(L0_attn_conv_b, dtype=np.float32)
    dtb = np.asarray(L0_attn_dt_bias, dtype=np.float32)
    alog = np.asarray(L0_attn_A_log, dtype=np.float32)
    dvec = np.asarray(L0_attn_D, dtype=np.float32)
    nw = np.asarray(L0_attn_norm_weight, dtype=np.float32)
    wg = np.asarray(L0_ffn_w_gate)
    wu = np.asarray(L0_ffn_w_up)
    wd = np.asarray(L0_ffn_w_down)
    sgt = np.asarray(L0_ffn_s_gate, dtype=np.float32)
    sut = np.asarray(L0_ffn_s_up, dtype=np.float32)
    sdt = np.asarray(L0_ffn_s_down, dtype=np.float32)

    wipm = np.ascontiguousarray(
        wip[:, :8448].reshape(16, 128, 66, 128).transpose(2, 1, 0, 3))
    wipd = np.ascontiguousarray(
        wip[:, 8448:].reshape(16, 128, 64).transpose(1, 0, 2))
    wopr = np.ascontiguousarray(
        wout.reshape(2, 16, 128, 16, 128).transpose(3, 0, 2, 1, 4))
    wg_r = np.ascontiguousarray(wg.reshape(16, 128, 64, 128).transpose(2, 1, 0, 3))
    wu_r = np.ascontiguousarray(wu.reshape(16, 128, 64, 128).transpose(2, 1, 0, 3))
    wd_r = np.ascontiguousarray(wd.reshape(64, 128, 16, 128).transpose(2, 1, 0, 3))

    shared = {
        "wipm": wipm, "wipd": wipd, "wop": wopr,
        "wg8": wg_r, "wu8": wu_r, "wd8": wd_r,
        "sgate_d": np.ascontiguousarray(sgt.reshape(64, 128).T),
        "sup_d": np.ascontiguousarray(sut.reshape(64, 128).T),
        "sdownh_d": np.ascontiguousarray((0.5 * sdt).reshape(16, 128).T),
        "nw_d": np.ascontiguousarray(nw.reshape(32, 128).T),
        "convb_d": np.ascontiguousarray(cb.reshape(34, 128).T),
        "convw_d": np.ascontiguousarray(
            cw.reshape(34, 128, 4).transpose(1, 0, 2).reshape(128, 136)),
        "dtb_d": dtb.reshape(64, 1),
        "alog_d": alog.reshape(64, 1),
        "dvec_d": dvec.reshape(1, 64),
    }
    in_maps = []
    for i in range(8):
        d = dict(shared)
        d["xt"] = np.ascontiguousarray(x[i].T)
        in_maps.append(d)
    return in_maps


def kernel(**inputs):
    in_maps = prep_in_maps(**inputs)
    nc = build_nc()
    res = run_bass_kernel_spmd(nc, in_maps, core_ids=list(range(8)))
    out = np.empty((8, NTOK, HID), dtype=np.float32)
    for i in range(8):
        out[i] = np.asarray(res.results[i]["yt"]).T
    return out


# version 16
# speedup vs baseline: 1.0969x; 1.0123x over previous
import numpy as np
import concourse.bass as bass
import concourse.mybir as mybir
from concourse.bass_utils import run_bass_kernel_spmd
from concourse.bass_types import AP

F32 = mybir.dt.float32
BF16 = mybir.dt.bfloat16
I8 = mybir.dt.int8
U8 = mybir.dt.uint8
AT = mybir.ActivationFunctionType
ALU = mybir.AluOpType

HID = 2048
IFF = 8192
IM = 4096
NH = 64
CS = 256
NTOK = 2048
NCH = NTOK // CS
T = CS
FCX = 16
NIB = 64
EPS = 1e-5
XBCW = 259

RACE_CHECK = False


class Ctr:
    def __init__(self, nc, name):
        self.h = nc.alloc_semaphore(name)
        self.v = 0


def build_nc(n_chunks=NCH, do_ffn1=True, do_mamba=True, do_ffn2=True):
    nc = bass.Bass("TRN2", target_bir_lowering=False,
                   detect_race_conditions=RACE_CHECK)

    xt = nc.dram_tensor("xt", [HID, NTOK], F32, kind="ExternalInput")
    yt = nc.dram_tensor("yt", [HID, NTOK], F32, kind="ExternalOutput")
    wipm = nc.dram_tensor("wipm", [66, 128, 16, 128], F32, kind="ExternalInput")
    wipd = nc.dram_tensor("wipd", [128, 16, 64], F32, kind="ExternalInput")
    wop = nc.dram_tensor("wop", [16, 2, 128, 16, 128], F32, kind="ExternalInput")
    wg8 = nc.dram_tensor("wg8", [64, 128, 16, 128], I8, kind="ExternalInput")
    wu8 = nc.dram_tensor("wu8", [64, 128, 16, 128], I8, kind="ExternalInput")
    wd8 = nc.dram_tensor("wd8", [16, 128, 64, 128], I8, kind="ExternalInput")
    sgate_d = nc.dram_tensor("sgate_d", [128, 64], F32, kind="ExternalInput")
    sup_d = nc.dram_tensor("sup_d", [128, 64], F32, kind="ExternalInput")
    sdownh_d = nc.dram_tensor("sdownh_d", [128, 16], F32, kind="ExternalInput")
    nw_d = nc.dram_tensor("nw_d", [128, 32], F32, kind="ExternalInput")
    convb_d = nc.dram_tensor("convb_d", [128, 34], F32, kind="ExternalInput")
    convw_d = nc.dram_tensor("convw_d", [128, 136], F32, kind="ExternalInput")
    dtb_d = nc.dram_tensor("dtb_d", [64, 1], F32, kind="ExternalInput")
    alog_d = nc.dram_tensor("alog_d", [64, 1], F32, kind="ExternalInput")
    dvec_d = nc.dram_tensor("dvec_d", [1, 64], F32, kind="ExternalInput")
    wgbf = nc.dram_tensor("wgbf", [64, 128, 16, 128], BF16, kind="Internal")
    wubf = nc.dram_tensor("wubf", [64, 128, 16, 128], BF16, kind="Internal")
    wdbf = nc.dram_tensor("wdbf", [16, 128, 64, 128], BF16, kind="Internal")

    ident_d = nc.inline_tensor(np.eye(128, dtype=np.float32), name="ident_c")
    onesr_d = nc.inline_tensor(np.ones((1, 128), dtype=np.float32), name="onesr_c")
    onesc_d = nc.inline_tensor(np.ones((128, 1), dtype=np.float32), name="onesc_c")
    dg0 = np.zeros((128, 256), dtype=np.float32)
    dg0[np.arange(128), np.arange(128)] = 1.0
    dg1 = np.zeros((128, 256), dtype=np.float32)
    dg1[np.arange(128), 128 + np.arange(128)] = 1.0
    diag0_d = nc.inline_tensor(dg0, name="diag0_c")
    diag1_d = nc.inline_tensor(dg1, name="diag1_c")
    zeros_d = nc.inline_tensor(np.zeros((128, 256), dtype=np.float32), name="zeros_c")

    A = nc.alloc_sbuf_tensor
    AT_ = nc.alloc_sbuf_tensor_at
    x_fm = A("x_fm", [128, FCX * T], F32)
    s_all = A("s_all", [128, 4096], F32)

    r_xny = A("r_xny", [128, 32768], U8)
    bx = nc.lookup_mloc(r_xny).addr
    xn32 = AT_("xn32", [128, 4096], F32, offset=bx)
    xnp = AT_("xnp", [128, 8192], BF16, offset=bx + 16384)
    y_fm = AT_("y_fm", [128, 8192], F32, offset=bx)

    r_big = A("r_big", [128, 68160], U8)
    bb = nc.lookup_mloc(r_big).addr
    fusep = AT_("fusep", [128, 32768], BF16, offset=bb)
    gate_fm = AT_("gate_fm", [128, 8192], F32, offset=bb)
    lnsq = AT_("lnsq", [128, 4096], F32, offset=bx)
    xbc = AT_("xbc", [128, 34 * XBCW], F32, offset=bb + 32768)
    ygp = AT_("ygp", [128, 16384], BF16, offset=bb + 32768)

    r_wb = A("r_wb", [128, 32768], U8)
    bw = nc.lookup_mloc(r_wb).addr
    wgb = [AT_("wgb%d" % i, [128, 4096], BF16, offset=bw + i * 16384) for i in range(2)]
    wub = [AT_("wub%d" % i, [128, 4096], BF16, offset=bw + 8192 + i * 16384) for i in range(2)]
    wdb = [AT_("wdb%d" % i, [128, 8192], BF16, offset=bw + i * 16384) for i in range(2)]
    proj32 = [AT_("proj32%d" % i, [128, 2048], F32, offset=bw + i * 8192) for i in range(2)]
    projh = [AT_("projh%d" % i, [128, 2048], BF16, offset=bw + 16384 + i * 4096) for i in range(2)]
    projl = [AT_("projl%d" % i, [128, 2048], BF16, offset=bw + 24576 + i * 4096) for i in range(2)]

    ea_p0 = A("ea_p0", [1, 2048], F32)
    halosave = A("halosave", [128, 102], F32)
    tg = [A("tg%d" % i, [128, 256], F32) for i in range(2)]
    tu = [A("tu%d" % i, [128, 256], F32) for i in range(2)]
    cbt = A("cbt", [128, 512], F32)
    ldec = [A("ldec%d" % i, [128, 512], F32) for i in range(2)]
    sdo = [A("sdo%d" % i, [128, 256], F32) for i in range(2)]
    b_tm = A("b_tm", [128, 256], F32)
    bd_tm = [A("bd_tm%d" % i, [128, 256], F32) for i in range(2)]
    xdt = [A("xdt%d" % i, [128, 128], F32) for i in range(2)]
    dt32 = A("dt32", [64, 256], F32)
    escr = A("escr", [64, 256], F32)
    ea_fm = A("ea_fm", [64, 256], F32)
    dt_tm = A("dt_tm", [128, 128], F32)
    rowA = A("rowA", [1, 256], F32)
    rowB = A("rowB", [1, 256], F32)
    rowC = A("rowC", [1, 256], F32)

    ident = A("ident", [128, 128], F32)
    onesr = A("onesr", [1, 128], F32)
    onesc = A("onesc", [128, 1], F32)
    diag0 = A("diag0", [128, 256], F32)
    diag1 = A("diag1", [128, 256], F32)
    zeros = A("zeros", [128, 256], F32)
    sg_sb = A("sg_sb", [128, 64], F32)
    su_sb = A("su_sb", [128, 64], F32)
    sdh_sb = A("sdh_sb", [128, 16], F32)
    nw_sb = A("nw_sb", [128, 32], F32)
    convb_sb = A("convb_sb", [128, 34], F32)
    convw_sb = A("convw_sb", [128, 136], F32)
    dtb_sb = A("dtb_sb", [64, 1], F32)
    negA = A("negA", [64, 1], F32)
    dexpb = A("dexpb", [128, 64], F32)
    d_p0 = A("d_p0", [1, 64], F32)
    epscol = A("epscol", [128, 1], F32)
    lnacc = A("lnacc", [128, 512], F32)
    sqt = A("sqt", [128, 512], F32)

    pb = [nc.alloc_psum_tensor("pb%d" % i, [128, 512], F32) for i in range(8)]

    c_hw = Ctr(nc, "c_hw")
    c_sw = Ctr(nc, "c_sw")
    c_pe = Ctr(nc, "c_pe")
    c_act = Ctr(nc, "c_act")
    c_vec = Ctr(nc, "c_vec")
    c_gp = Ctr(nc, "c_gp")
    c_st = Ctr(nc, "c_st")

    hiwater = {}

    dry_box = [False]

    def W(eng, key, ctr, val):
        if callable(val):
            val = val()
        if dry_box[0] or val <= 0:
            return
        k = (key, ctr.h.num)
        if hiwater.get(k, 0) < val:
            eng.wait_ge(ctr.h, val)
            hiwater[k] = val

    def I(instr, ctr, k=1):
        if instr is not None:
            instr.then_inc(ctr.h, k)
        ctr.v += k
        return ctr.v

    class _Dry:
        def __getattr__(self, name):
            return lambda *a, **kw: None

    class _PhaseCollector:
        def __init__(self):
            self.items = []

        def _deco(self, key):
            def d(fn):
                self.items.append((key, fn))
            return d

        @property
        def sync(self):
            return self._deco("sync")

        @property
        def gpsimd(self):
            return self._deco("gpsimd")

        @property
        def scalar(self):
            return self._deco("scalar")

        @property
        def tensor(self):
            return self._deco("tensor")

        @property
        def vector(self):
            return self._deco("vector")

    _ctrs = None

    from contextlib import contextmanager

    @contextmanager
    def phase(pname="ph"):
        ph = _PhaseCollector()
        yield ph
        ctrs = (c_hw, c_sw, c_pe, c_act, c_vec, c_gp, c_st)
        snap = [c.v for c in ctrs]
        dry_box[0] = True
        dg = _Dry()
        for _k, fn in ph.items:
            fn(dg)
        dry_box[0] = False
        finals = [c.v for c in ctrs]
        for c, s in zip(ctrs, snap):
            c.v = s
        with nc.named_scope(pname), nc.Block() as blk:
            for key in ("sync", "gpsimd", "scalar", "tensor", "vector"):
                fns = [f for k2, f in ph.items if k2 == key]
                if fns:
                    getattr(blk, key)(lambda eng, fns=fns: [f(eng) for f in fns])
        for c, f in zip(ctrs, finals):
            assert c.v == f, "phase counter mismatch (dry=%d real=%d)" % (f, c.v)

    with nc.Block() as blk:
        marks = {}

        @blk.sync
        def _(sync):
            for dst, src in (
                (ident, ident_d), (onesr, onesr_d), (onesc, onesc_d),
                (diag0, diag0_d), (diag1, diag1_d), (zeros, zeros_d),
                (sg_sb, sgate_d), (su_sb, sup_d), (sdh_sb, sdownh_d),
                (nw_sb, nw_d), (convb_sb, convb_d), (convw_sb, convw_d),
                (dtb_sb, dtb_d), (negA, alog_d), (d_p0, dvec_d),
            ):
                I(sync.dma_start(dst[:, :], src.ap()), c_hw, 16)
            marks["pre"] = c_hw.v

        @blk.gpsimd
        def _(g):
            g.memset(s_all[:, :], 0.0)
            g.memset(ea_p0[:, :], 0.0)
            g.memset(halosave[:, :], 0.0)
            g.memset(epscol[:, :], EPS)

        @blk.scalar
        def _(s):
            W(s, "act", c_hw, marks["pre"])
            s.activation(negA[:, :], negA[:, :], AT.Exp)
            s.mul(negA[:, :], negA[:, :], -1.0)

        @blk.tensor
        def _(t):
            W(t, "pe", c_hw, marks["pre"])
            m = t.matmul(pb[7][:, 0:64], onesr[0:1, :], d_p0[0:1, :],
                         start=True, stop=True)
            marks["dexp"] = I(m, c_pe)

        @blk.vector
        def _(v):
            W(v, "vec", c_pe, marks["dexp"])
            v.tensor_copy(dexpb[:, :], pb[7][:, 0:64])

    castq = {}
    with phase("wcast") as blk:
        @blk.gpsimd
        def _(g):
            for q in range(4):
                i = g.dma_start(wgbf.ap()[16 * q:16 * q + 16],
                                wg8.ap()[16 * q:16 * q + 16])
                castq["wg%d" % q] = I(i, c_sw, 16)
            for q in range(4):
                i = g.dma_start(wubf.ap()[16 * q:16 * q + 16],
                                wu8.ap()[16 * q:16 * q + 16])
                castq["wu%d" % q] = I(i, c_sw, 16)
            for q in range(2):
                i = g.dma_start(wdbf.ap()[8 * q:8 * q + 8],
                                wd8.ap()[8 * q:8 * q + 8])
                castq["wd%d" % q] = I(i, c_sw, 16)


    def emit_ln(blk, pre=(), fc_wait=None, pe_pre=()):
        if True:
            m = {}

            def fcw(eng, key, fc):
                if fc_wait is not None:
                    ctr, val = fc_wait(fc)
                    W(eng, key, ctr, val)
                else:
                    for ctr, val in pre:
                        W(eng, key, ctr, val)

            @blk.scalar
            def _(s):
                for fc in range(FCX):
                    fcw(s, "act", fc)
                    i = s.activation(lnsq[:, fc * T:(fc + 1) * T],
                                     x_fm[:, fc * T:(fc + 1) * T], AT.Square)
                    m["sq%d" % fc] = I(i, c_act)

            for fc in range(FCX):
                @blk.vector
                def _(v, fc=fc):
                    for ctr, val in pre:
                        W(v, "vec", ctr, val)
                    fcw(v, "vec", fc)
                    sl = slice(fc * T, (fc + 1) * T)
                    if fc == 0:
                        v.tensor_copy(lnacc[:, 0:256], x_fm[:, sl])
                    else:
                        v.tensor_add(lnacc[:, 0:256], lnacc[:, 0:256], x_fm[:, sl])
                    W(v, "vec", c_act, m["sq%d" % fc])
                    if fc == 0:
                        i = v.tensor_copy(lnacc[:, 256:512], lnsq[:, sl])
                    else:
                        i = v.tensor_add(lnacc[:, 256:512], lnacc[:, 256:512],
                                         lnsq[:, sl])
                    if fc == FCX - 1:
                        m["acc"] = I(i, c_vec)

            @blk.tensor
            def _(t):
                for ctr, val in pe_pre:
                    W(t, "pe", ctr, val)
                W(t, "pe", c_vec, m["acc"])
                i = t.matmul(pb[6][0:1, 0:512], onesc[:, 0:1], lnacc[:, :],
                             start=True, stop=True)
                m["stats"] = I(i, c_pe)

            @blk.scalar
            def _(s):
                W(s, "act", c_pe, m["stats"])
                s.mul(rowA[0:1, :], pb[6][0:1, 0:256], 1.0 / 2048)
                s.mul(rowB[0:1, :], pb[6][0:1, 256:512], 1.0 / 2048)
                i = s.activation(rowC[0:1, :], rowA[0:1, :], AT.Square)
                m["rows1"] = I(i, c_act)

            @blk.vector
            def _(v):
                W(v, "vec", c_act, m["rows1"])
                i = v.tensor_sub(rowB[0:1, :], rowB[0:1, :], rowC[0:1, :])
                m["var"] = I(i, c_vec)

            @blk.scalar
            def _(s):
                W(s, "act", c_vec, m["var"])
                i = s.activation(rowB[0:1, :], rowB[0:1, :], AT.Sqrt,
                                 bias=epscol[0:1, 0:1])
                m["sqrt"] = I(i, c_act)

            @blk.vector
            def _(v):
                W(v, "vec", c_act, m["sqrt"])
                v.reciprocal(rowC[0:1, :], rowB[0:1, :])
                i = v.scalar_tensor_tensor(rowA[0:1, :], rowA[0:1, :], -1.0,
                                           rowC[0:1, :], ALU.mult, ALU.mult)
                m["rows2"] = I(i, c_vec)

            @blk.tensor
            def _(t):
                W(t, "pe", c_vec, m["rows2"])
                t.matmul(pb[7][:, 0:256], onesr[0:1, :], rowC[0:1, :],
                         start=True, stop=True)
                i = t.matmul(pb[7][:, 256:512], onesr[0:1, :], rowA[0:1, :],
                             start=True, stop=True)
                m["bcast"] = I(i, c_pe)

            @blk.vector
            def _(v):
                W(v, "vec", c_pe, m["bcast"])
                fcw(v, "vec", FCX - 1)
                x3 = AP(x_fm[:, :].tensor, 0, [[4096, 128], [256, 16], [1, 256]])
                xn3 = AP(xn32[:, :].tensor, 0, [[4096, 128], [256, 16], [1, 256]])
                alb = AP(pb[7][:, :].tensor, 0, [[512, 128], [0, 16], [1, 256]])
                beb = AP(pb[7][:, :].tensor, 256, [[512, 128], [0, 16], [1, 256]])
                v.tensor_tensor(xn3, x3, alb, ALU.mult)
                i = v.tensor_tensor(xn3, xn3, beb, ALU.add)
                m["xn"] = I(i, c_vec)

            @blk.scalar
            def _(s):
                W(s, "act", c_vec, m["xn"])
                xn3 = AP(xn32[:, :].tensor, 0, [[4096, 128], [256, 16], [1, 256]])
                hi3 = AP(xnp[:, :].tensor, 0, [[8192, 128], [512, 16], [1, 256]])
                i = s.copy(hi3, xn3)
                m["hi"] = I(i, c_act)

            @blk.vector
            def _(v):
                W(v, "vec", c_act, m["hi"])
                xn3 = AP(xn32[:, :].tensor, 0, [[4096, 128], [256, 16], [1, 256]])
                hi3 = AP(xnp[:, :].tensor, 0, [[8192, 128], [512, 16], [1, 256]])
                lo3 = AP(xnp[:, :].tensor, 256, [[8192, 128], [512, 16], [1, 256]])
                v.tensor_tensor(lo3, xn3, hi3, ALU.subtract)

    def emit_gateup_prefetch(blk, pref, waits=()):
        @blk.sync
        def _(sync):
            for ctr, val in waits:
                W(sync, "sy", ctr, val)
            W(sync, "sy", c_sw, castq["wg0"])
            W(sync, "sy", c_sw, castq["wu0"])
            for gr in (0, 1):
                srcg = wgbf.ap()[2 * gr:2 * gr + 2].rearrange("i p k m -> p i (k m)")
                dstg = wgb[gr][:, :].rearrange("p (i km) -> p i km", i=2)
                I(sync.dma_start(dstg, srcg), c_hw, 16)
                srcu = wubf.ap()[2 * gr:2 * gr + 2].rearrange("i p k m -> p i (k m)")
                dstu = wub[gr][:, :].rearrange("p (i km) -> p i km", i=2)
                pref[gr] = I(sync.dma_start(dstu, srcu), c_hw, 16)

    def emit_ffn_gateup(blk, pref_gu, pref_dn, gu_marks):
        if True:
            mk_dma = dict(pref_gu)
            mk_g = {}
            mk_u = {}
            mk_silu = {}
            mk_gsum = {}
            mk_usum = {}
            mk_prod = {}
            mk_hi = {}
            mk_flo = {}
            mk_grp = {}
            for gr in range(32):
                if gr >= 2:
                    @blk.sync
                    def _(g, gr=gr):
                        W(g, "sy", c_pe, mk_grp[gr - 2])
                        W(g, "sy", c_sw, castq["wg%d" % (gr // 8)])
                        W(g, "sy", c_sw, castq["wu%d" % (gr // 8)])
                        src = wgbf.ap()[2 * gr:2 * gr + 2].rearrange("i p k m -> p i (k m)")
                        dst = wgb[gr % 2][:, :].rearrange("p (i km) -> p i km", i=2)
                        I(g.dma_start(dst, src), c_hw, 16)
                        src = wubf.ap()[2 * gr:2 * gr + 2].rearrange("i p k m -> p i (k m)")
                        dst = wub[gr % 2][:, :].rearrange("p (i km) -> p i km", i=2)
                        mk_dma[gr] = I(g.dma_start(dst, src), c_hw, 16)

                for i2 in range(2):
                    ib = 2 * gr + i2
                    par = ib % 2

                    @blk.tensor
                    def _(t, gr=gr, i2=i2, ib=ib, par=par):
                        W(t, "pe", c_hw, mk_dma[gr])
                        if ib >= 2:
                            W(t, "pe", c_vec, mk_gsum[ib - 2])
                            W(t, "pe", c_vec, mk_usum[ib - 2])
                        pg = pb[par][:, 0:512]
                        pu = pb[2 + par][:, 0:512]
                        for kb in range(16):
                            wt = wgb[gr % 2][:, (i2 * 16 + kb) * 128:(i2 * 16 + kb + 1) * 128]
                            i = t.matmul(pg, wt, xnp[:, kb * 512:(kb + 1) * 512],
                                         start=(kb == 0), stop=(kb == 15))
                        mk_g[ib] = I(i, c_pe)
                        for kb in range(16):
                            wt = wub[gr % 2][:, (i2 * 16 + kb) * 128:(i2 * 16 + kb + 1) * 128]
                            i = t.matmul(pu, wt, xnp[:, kb * 512:(kb + 1) * 512],
                                         start=(kb == 0), stop=(kb == 15))
                        mk_u[ib] = I(i, c_pe)
                        if i2 == 1:
                            mk_grp[gr] = mk_u[ib]

                    @blk.vector
                    def _(v, ib=ib, par=par):
                        W(v, "vec", c_pe, mk_g[ib])
                        if ib >= 2:
                            W(v, "vec", c_act, mk_hi[ib - 2])
                        v.tensor_copy(tg[par][:, :], pb[par][:, 256:512])
                        i = v.tensor_add(tg[par][:, :], tg[par][:, :],
                                         pb[par][:, 0:256])
                        mk_gsum[ib] = I(i, c_vec)
                        W(v, "vec", c_pe, mk_u[ib])
                        v.tensor_copy(tu[par][:, :], pb[2 + par][:, 256:512])
                        i = v.tensor_add(tu[par][:, :], tu[par][:, :],
                                         pb[2 + par][:, 0:256])
                        mk_usum[ib] = I(i, c_vec)

                    @blk.scalar
                    def _(s, ib=ib, par=par):
                        W(s, "act", c_vec, mk_gsum[ib])
                        i = s.activation(tg[par][:, :], tg[par][:, :],
                                         AT.Silu, scale=sg_sb[:, ib:ib + 1])
                        mk_silu[ib] = I(i, c_act)

                    @blk.vector
                    def _(v, ib=ib, par=par):
                        W(v, "vec", c_act, mk_silu[ib])
                        i = v.scalar_tensor_tensor(tg[par][:, :], tu[par][:, :],
                                                   su_sb[:, ib:ib + 1],
                                                   tg[par][:, :], ALU.mult, ALU.mult)
                        mk_prod[ib] = I(i, c_vec)

                    @blk.scalar
                    def _(s, ib=ib, par=par):
                        W(s, "act", c_vec, mk_prod[ib])
                        i = s.copy(fusep[:, ib * 512:ib * 512 + 256], tg[par][:, :])
                        mk_hi[ib] = I(i, c_act)

                    @blk.vector
                    def _(v, ib=ib, par=par):
                        W(v, "vec", c_act, mk_hi[ib])
                        i = v.tensor_sub(fusep[:, ib * 512 + 256:ib * 512 + 512],
                                         tg[par][:, :],
                                         fusep[:, ib * 512:ib * 512 + 256])
                        mk_flo[ib] = I(i, c_vec)

            @blk.sync
            def _(sync):
                W(sync, "sy", c_sw, castq["wd0"])
                W(sync, "sy", c_pe, mk_grp[30])
                pref_dn[0] = I(sync.dma_start(
                    wdb[0][:, :], wdbf.ap()[0].rearrange("p k m -> p (k m)")),
                    c_hw, 16)
                W(sync, "sy", c_pe, mk_grp[31])
                pref_dn[1] = I(sync.dma_start(
                    wdb[1][:, :], wdbf.ap()[1].rearrange("p k m -> p (k m)")),
                    c_hw, 16)
            return mk_hi, mk_flo

    def emit_ffn_down(blk, pref_dn, fuse_hi_mk, fuse_lo_mk):
        if True:
            mk_dma = {}
            mk_mm = {}
            mk_ev = {}
            for oc in range(16):
                if oc >= 2:
                    @blk.sync
                    def _(g, oc=oc):
                        W(g, "sy", c_sw, castq["wd%d" % (oc // 8)])
                        W(g, "sy", c_pe, mk_mm[oc - 2])
                        src = wdbf.ap()[oc].rearrange("p k m -> p (k m)")
                        mk_dma[oc] = I(g.dma_start(wdb[oc % 2][:, :], src), c_hw, 16)

                @blk.tensor
                def _(t, oc=oc):
                    par = oc % 2
                    W(t, "pe", c_hw, mk_dma[oc] if oc >= 2 else (lambda oc=oc: pref_dn[oc]))
                    if oc == 0:
                        W(t, "pe", c_act, lambda: fuse_hi_mk[63])
                        W(t, "pe", c_vec, lambda: fuse_lo_mk[63])
                    if oc >= 2:
                        W(t, "pe", c_vec, mk_ev[oc - 2])
                    pd = pb[4 + par][:, 0:512]
                    for kc in range(64):
                        wt = wdb[par][:, kc * 128:(kc + 1) * 128]
                        i = t.matmul(pd, wt, fusep[:, kc * 512:(kc + 1) * 512],
                                     start=(kc == 0), stop=(kc == 63))
                    mk_mm[oc] = I(i, c_pe)

                @blk.vector
                def _(v, oc=oc):
                    par = oc % 2
                    W(v, "vec", c_pe, mk_mm[oc])
                    sl = slice(oc * T, (oc + 1) * T)
                    v.tensor_copy(tu[par][:, :], pb[4 + par][:, 256:512])
                    v.tensor_add(tu[par][:, :], tu[par][:, :], pb[4 + par][:, 0:256])
                    i = v.scalar_tensor_tensor(x_fm[:, sl], tu[par][:, :],
                                               sdh_sb[:, oc:oc + 1],
                                               x_fm[:, sl], ALU.mult, ALU.add)
                    mk_ev[oc] = I(i, c_vec)
            return mk_ev

    def _emit_proj_evac(blk, ob, mk_mm, mk_ev, pslot=None):
        @blk.scalar
        def _(s, ob=ob):
            W(s, "act", c_pe, mk_mm[ob])
            pp = pb[pslot if pslot is not None else ob % 4][:, 0:256]
            if ob < 32:
                i = s.copy(gate_fm[:, ob * T:(ob + 1) * T], pp)
            elif ob < 66:
                off = (ob - 32) * XBCW
                i = s.copy(xbc[:, off + 3:off + XBCW], pp)
            else:
                s.activation(escr[:, :], pp[0:64, :], AT.Exp,
                             bias=dtb_sb[:, 0:1])
                i = s.activation(dt32[:, :], escr[:, :], AT.Ln, bias=onesc[0:64, 0:1])
            mk_ev[ob] = I(i, c_act)

    def _emit_conv_tile(blk, bc, mk_ev, mkc):
        seq = mkc.setdefault("_seq", [])
        prev2 = seq[-2] if len(seq) >= 2 else None
        seq.append(bc)
        par = len(seq) % 2
        off = bc * XBCW

        @blk.vector
        def _(v, bc=bc, par=par, off=off, prev2=prev2):
            W(v, "vec", c_act, mk_ev[32 + bc])
            i = v.tensor_copy(halosave[:, bc * 3:bc * 3 + 3],
                              xbc[:, off + 256:off + 259])
            if prev2 is not None:
                W(v, "vec", c_act, mkc["si%d" % prev2])
            v.tensor_scalar(tg[par][:, :], xbc[:, off + 0:off + 256],
                            convw_sb[:, bc * 4:bc * 4 + 1], None, ALU.mult)
            for k in (1, 2, 3):
                i = v.scalar_tensor_tensor(
                    tg[par][:, :], xbc[:, off + k:off + k + 256],
                    convw_sb[:, bc * 4 + k:bc * 4 + k + 1],
                    tg[par][:, :], ALU.mult, ALU.add)
            mkc["acc%d" % bc] = I(i, c_vec)

        @blk.scalar
        def _(s, bc=bc, par=par, off=off):
            W(s, "act", c_vec, mkc["acc%d" % bc])
            i = s.activation(xbc[:, off + 3:off + XBCW], tg[par][:, :],
                             AT.Silu, bias=convb_sb[:, bc:bc + 1])
            mkc["si%d" % bc] = I(i, c_act)

    def emit_inproj(blk, ch):
        if True:
            mk_load = {}
            mk_hi = {}
            mk_lo = {}
            mk_mm = {}
            mk_ev = {}
            mkc = {}

            @blk.vector
            def _(v):
                heads = xbc[:, :].rearrange("p (bc c) -> p bc c", bc=34)[:, :, 0:3]
                if ch == 0:
                    v.memset(heads, 0.0)
                else:
                    v.tensor_copy(heads, halosave[:, :].rearrange(
                        "p (bc c) -> p bc c", bc=34))

            order = [66, 64, 65] + list(range(64))
            for si, ob in enumerate(order):
                wfull = 2048 if ob < 66 else 1024
                mtile = 128 if ob < 66 else 64

                @blk.sync
                def _(sync, si=si, ob=ob, wfull=wfull):
                    if si >= 2:
                        W(sync, "sy", c_vec, mk_lo[order[si - 2]])
                    if ob < 66:
                        src = wipm.ap()[ob].rearrange("p k m -> p (k m)")
                    else:
                        src = wipd.ap().rearrange("p k m -> p (k m)")
                    mk_load[ob] = I(sync.dma_start(proj32[si % 2][:, 0:wfull], src),
                                    c_hw, 16)

                @blk.vector
                def _(v, si=si, ob=ob, wfull=wfull):
                    W(v, "vec", c_hw, mk_load[ob])
                    if si >= 2:
                        W(v, "vec", c_pe, mk_mm[order[si - 2]])
                    v.tensor_copy(projh[si % 2][:, 0:wfull], proj32[si % 2][:, 0:wfull])
                    i = v.tensor_sub(projl[si % 2][:, 0:wfull],
                                     proj32[si % 2][:, 0:wfull],
                                     projh[si % 2][:, 0:wfull])
                    mk_lo[ob] = I(i, c_vec)

                if si >= 1:
                    prev = order[si - 1]
                    _emit_proj_evac(blk, prev, mk_mm, mk_ev, pslot=(si - 1) % 4)
                    if 32 <= prev < 66:
                        _emit_conv_tile(blk, prev - 32, mk_ev, mkc)
                if si == 2:
                    _emit_dt_tail(blk, mk_ev, mkc)
                if si == 6:
                    _emit_btcb_tail(blk, mk_ev, mkc)

                @blk.tensor
                def _(t, si=si, ob=ob, mtile=mtile):
                    W(t, "pe", c_vec, mk_lo[ob])
                    if si >= 4:
                        W(t, "pe", c_act, mk_ev[order[si - 4]])
                    pp = pb[si % 4][0:mtile, 0:256]
                    for kb in range(16):
                        wh = projh[si % 2][:, kb * mtile:(kb + 1) * mtile]
                        wl = projl[si % 2][:, kb * mtile:(kb + 1) * mtile]
                        xh = xnp[:, kb * 512:kb * 512 + 256]
                        xl = xnp[:, kb * 512 + 256:kb * 512 + 512]
                        t.matmul(pp, wh, xh, start=(kb == 0), stop=False)
                        t.matmul(pp, wh, xl, start=False, stop=False)
                        i = t.matmul(pp, wl, xh, start=False, stop=(kb == 15))
                    mk_mm[ob] = I(i, c_pe)

            _emit_proj_evac(blk, order[-1], mk_mm, mk_ev, pslot=(len(order) - 1) % 4)
            _emit_conv_tile(blk, order[-1] - 32, mk_ev, mkc)

    def _emit_dt_tail(blk, mk_ev, mkc):
        @blk.vector
        def _(v):
            W(v, "vec", c_act, mk_ev[66])
            i = v.tensor_scalar(escr[:, :], dt32[:, :], negA[:, 0:1], None,
                                ALU.mult)
            mkc["a"] = I(i, c_vec)

        @blk.scalar
        def _(s):
            W(s, "act", c_vec, mkc["a"])
            i = s.activation(ea_fm[:, :], escr[:, :], AT.Exp)
            mkc["ea"] = I(i, c_act)

        @blk.tensor
        def _(t):
            W(t, "pe", c_act, mk_ev[66])
            for sc in range(2):
                i = t.transpose(pb[4][:, sc * 64:sc * 64 + 64],
                                dt32[0:64, sc * 128:(sc + 1) * 128],
                                ident[0:64, 0:64])
            mkc["dtT"] = I(i, c_pe)

        @blk.vector
        def _(v):
            W(v, "vec", c_pe, mkc["dtT"])
            v.tensor_copy(dt_tm[:, 0:64], pb[4][:, 0:64])
            i = v.tensor_copy(dt_tm[:, 64:128], pb[4][:, 64:128])
            mkc["dtmev"] = I(i, c_vec)

    def _emit_btcb_tail(blk, mk_ev, mkc):
        @blk.tensor
        def _(t):
            offB = 32 * XBCW
            W(t, "pe", c_act, mkc["si33"])
            for sc in range(2):
                i = t.transpose(pb[5][:, sc * 128:sc * 128 + 128],
                                xbc[:, offB + 3 + sc * 128:offB + 3 + (sc + 1) * 128],
                                ident[:, :])
            mkc["btmT"] = I(i, c_pe)

        @blk.vector
        def _(v):
            W(v, "vec", c_pe, mkc["btmT"])
            v.tensor_copy(b_tm[:, 0:128], pb[5][:, 0:128])
            i = v.tensor_copy(b_tm[:, 128:256], pb[5][:, 128:256])
            mkc["btmev"] = I(i, c_vec)

        @blk.tensor
        def _(t):
            offB = 32 * XBCW
            offC = 33 * XBCW
            W(t, "pe", c_vec, mkc["dtmev"])
            W(t, "pe", c_vec, mkc["btmev"])
            t.matmul(pb[4][:, 256:512],
                     xbc[:, offB + 3:offB + 131],
                     xbc[:, offC + 3:offC + XBCW], start=True, stop=True)
            i = t.matmul(pb[5][:, 256:512],
                         xbc[:, offB + 131:offB + 259],
                         xbc[:, offC + 3:offC + XBCW], start=True, stop=True)
            mkc["cbtmm"] = I(i, c_pe)

        @blk.vector
        def _(v):
            W(v, "vec", c_pe, mkc["cbtmm"])
            v.tensor_copy(cbt[:, 0:256], pb[4][:, 256:512])
            i = v.tensor_copy(cbt[:, 256:512], pb[5][:, 256:512])
            mkc["cbtev"] = I(i, c_vec)

    def emit_ssd(blk, ch):
        if True:
            mk_grp = {}
            mk_eab = {}
            mk_sdo = {}
            mk_ldec = {}
            mk_bd = {}
            mk_gt = {}
            mk_xt = {}
            mk_xe = {}
            mk_ymm = {}
            mk_ysum = {}
            mk_gadd = {}
            mk_comb = {}
            mk_upd = {}

            def stage_front(h):
                par = h % 2
                g4 = h // 4
                bp = (h % 2) * 64
                fc = h // 2
                off = fc * XBCW
                if h % 4 == 0:
                    @blk.gpsimd
                    def _(g, h=h, g4=g4):
                        if g4 >= 2:
                            W(g, "gp", c_pe, mk_eab[4 * (g4 - 2) + 3])
                        base = (g4 % 2) * 1024
                        mk_grp[g4] = I(g.dma_start(
                            ea_p0[0:1, base:base + 1024],
                            ea_fm[4 * g4:4 * g4 + 4, :]), c_sw, 16)

                @blk.tensor
                def _(t, h=h, par=par, g4=g4, bp=bp, off=off):
                    W(t, "pe", c_sw, mk_grp[g4])
                    if h >= 2:
                        W(t, "pe", c_vec, mk_sdo[h - 2])
                        W(t, "pe", c_act, mk_xe[h - 2])
                    base = (g4 % 2) * 1024 + (h % 4) * 256
                    i = t.matmul(pb[par][:, 0:256],
                                 onesr[0:1, :], ea_p0[0:1, base:base + 256],
                                 start=True, stop=True)
                    mk_eab[h] = I(i, c_pe)
                    for sc in range(2):
                        i = t.transpose(
                            pb[4 + par][:, sc * 64:sc * 64 + 64],
                            xbc[bp:bp + 64, off + 3 + sc * 128:off + 3 + (sc + 1) * 128],
                            ident[bp:bp + 64, bp:bp + 64])
                    mk_xt[h] = I(i, c_pe)

                @blk.vector
                def _(v, h=h, par=par):
                    W(v, "vec", c_pe, mk_eab[h])
                    if h >= 2:
                        W(v, "vec", c_pe, mk_ymm[h - 2])
                        W(v, "vec", c_vec, mk_upd[h - 2])
                    eab = pb[par][:, 0:256]
                    v.tensor_tensor_scan(ldec[par][:, 0:256], eab, diag0[:, :],
                                         0.0, ALU.mult, ALU.add)
                    i = v.tensor_tensor_scan(ldec[par][:, 256:512], eab,
                                             diag1[:, :], 0.0, ALU.mult, ALU.add)
                    mk_ldec[h] = I(i, c_vec)
                    i = v.tensor_tensor_scan(sdo[par][:, :], eab, zeros[:, :],
                                             1.0, ALU.mult, ALU.add)
                    mk_sdo[h] = I(i, c_vec)

                @blk.scalar
                def _(s, h=h, par=par):
                    W(s, "act", c_vec, mk_ldec[h])
                    if h >= 2:
                        W(s, "act", c_pe, mk_ymm[h - 2])
                    s.mul(bd_tm[par][:, 0:128], b_tm[:, 0:128],
                          ldec[par][:, 255:256])
                    i = s.mul(bd_tm[par][:, 128:256], b_tm[:, 128:256],
                              ldec[par][:, 511:512])
                    mk_bd[h] = I(i, c_act)
                    W(s, "act", c_pe, mk_xt[h])
                    if h >= 2:
                        W(s, "act", c_pe, mk_ymm[h - 2])
                    for sc in range(2):
                        i = s.mul(xdt[par][:, sc * 64:sc * 64 + 64],
                                  pb[4 + par][:, sc * 64:sc * 64 + 64],
                                  dt_tm[:, sc * 64 + h:sc * 64 + h + 1])
                    mk_xe[h] = I(i, c_act)

                @blk.gpsimd
                def _(g, h=h, par=par):
                    W(g, "gp", c_act, mk_bd[h])
                    g.tensor_mul(ldec[par][:, 0:256], ldec[par][:, 0:256],
                                 cbt[:, 0:256])
                    i = g.tensor_mul(ldec[par][:, 256:512], ldec[par][:, 256:512],
                                     cbt[:, 256:512])
                    mk_gt[h] = I(i, c_gp)

            def stage_back(h):
                par = h % 2
                fc = h // 2
                bp = (h % 2) * 64
                off = fc * XBCW

                @blk.tensor
                def _(t, h=h, par=par, bp=bp):
                    W(t, "pe", c_gp, mk_gt[h])
                    W(t, "pe", c_act, mk_xe[h])
                    if h >= 2:
                        W(t, "pe", c_vec, mk_comb[h - 2])
                        W(t, "pe", c_vec, mk_upd[h - 2])
                    for sc in range(2):
                        t.matmul(pb[2 + par][bp:bp + 64, 0:256],
                                 xdt[par][:, sc * 64:sc * 64 + 64],
                                 ldec[par][:, sc * 256:sc * 256 + 256],
                                 start=(sc == 0), stop=(sc == 1))
                    for sc in range(2):
                        t.matmul(pb[6 + par][:, 0:64],
                                 bd_tm[par][:, sc * 128:sc * 128 + 128],
                                 xdt[par][:, sc * 64:sc * 64 + 64],
                                 start=(sc == 0), stop=(sc == 1))
                    i = t.matmul(pb[2 + par][bp:bp + 64, 256:512],
                                 s_all[:, h * 64:h * 64 + 64],
                                 xbc[:, 33 * XBCW + 3:33 * XBCW + XBCW],
                                 start=True, stop=True)
                    mk_ymm[h] = I(i, c_pe)

                @blk.vector
                def _(v, h=h, par=par, fc=fc, bp=bp, off=off):
                    W(v, "vec", c_pe, mk_ymm[h])
                    ysl = y_fm[bp:bp + 64, fc * T:(fc + 1) * T]
                    v.tensor_mul(ysl, pb[2 + par][bp:bp + 64, 256:512],
                                 sdo[par][bp:bp + 64, :])
                    v.tensor_add(ysl, ysl, pb[2 + par][bp:bp + 64, 0:256])
                    i = v.scalar_tensor_tensor(
                        ysl, xbc[bp:bp + 64, off + 3:off + XBCW],
                        dexpb[bp:bp + 64, h:h + 1], ysl, ALU.mult, ALU.add)
                    mk_comb[h] = I(i, c_vec)

                @blk.vector
                def _(v, h=h, par=par):
                    W(v, "vec", c_pe, mk_ymm[h])
                    i = v.scalar_tensor_tensor(
                        s_all[:, h * 64:h * 64 + 64], s_all[:, h * 64:h * 64 + 64],
                        sdo[par][:, 255:256],
                        pb[6 + par][:, 0:64],
                        ALU.mult, ALU.add)
                    mk_upd[h] = I(i, c_vec)

            mk_sg = {}
            mk_gu = {}
            mk_yh = {}
            mk_ylo = {}
            mk_rsq = {}
            mk_rac = {}

            def stage_gate(yc):
                par = yc % 2

                @blk.scalar
                def _(s, yc=yc, par=par):
                    if yc >= 2:
                        W(s, "act", c_vec, mk_gu[yc - 2])
                    i = s.activation(tg[par][:, :],
                                     gate_fm[:, yc * T:(yc + 1) * T], AT.Silu)
                    mk_sg[yc] = I(i, c_act)

                @blk.vector
                def _(v, yc=yc, par=par):
                    W(v, "vec", c_act, mk_sg[yc])
                    W(v, "vec", c_vec, mk_comb[2 * yc + 1])
                    if yc >= 2:
                        W(v, "vec", c_act, mk_yh[yc - 2])
                    i = v.scalar_tensor_tensor(tu[par][:, :],
                                               y_fm[:, yc * T:(yc + 1) * T],
                                               nw_sb[:, yc:yc + 1], tg[par][:, :],
                                               ALU.mult, ALU.mult)
                    mk_gu[yc] = I(i, c_vec)

                @blk.scalar
                def _(s, yc=yc, par=par):
                    W(s, "act", c_vec, mk_gu[yc])
                    i = s.copy(ygp[:, yc * 512:yc * 512 + 256], tu[par][:, :])
                    mk_yh[yc] = I(i, c_act)

                @blk.vector
                def _(v, yc=yc, par=par):
                    W(v, "vec", c_act, mk_yh[yc])
                    i = v.tensor_sub(ygp[:, yc * 512 + 256:yc * 512 + 512],
                                     tu[par][:, :],
                                     ygp[:, yc * 512:yc * 512 + 256])
                    mk_ylo[yc] = I(i, c_vec)

                @blk.scalar
                def _(s, yc=yc, par=par):
                    if yc >= 2:
                        W(s, "act", c_gp, mk_rac[yc - 2])
                    i = s.activation(sqt[:, par * 256:par * 256 + 256],
                                     y_fm[:, yc * T:(yc + 1) * T], AT.Square)
                    mk_rsq[yc] = I(i, c_act)

                @blk.gpsimd
                def _(g, yc=yc, par=par):
                    W(g, "gp", c_act, mk_rsq[yc])
                    if yc == 0:
                        i = g.tensor_copy(lnacc[:, 0:256], sqt[:, 0:256])
                    else:
                        i = g.tensor_add(lnacc[:, 0:256], lnacc[:, 0:256],
                                         sqt[:, par * 256:par * 256 + 256])
                    mk_rac[yc] = I(i, c_gp)

            for h in range(NH + 1):
                if h < NH:
                    stage_front(h)
                if h >= 1:
                    stage_back(h - 1)
                    if (h - 1) % 2 == 1:
                        stage_gate((h - 1) // 2)
            return {"rac": mk_rac}

    def emit_outproj(blk, ssdm):
        if True:
            mk_load = {}
            mk_hi = {}
            mk_lo = {}
            mk_mm = {}
            mk_ev = {}
            mr = {}

            @blk.tensor
            def _(t):
                W(t, "pe", c_gp, lambda: ssdm["rac"][31])
                i = t.matmul(pb[3][0:1, 0:256], onesc[:, 0:1], lnacc[:, 0:256],
                             start=True, stop=True)
                mr["mm31"] = I(i, c_pe)

            @blk.scalar
            def _(s):
                W(s, "act", c_pe, mr["mm31"])
                i = s.activation(rowB[0:1, :], pb[3][0:1, 0:256], AT.Sqrt,
                                 bias=epscol[0:1, 0:1], scale=1.0 / 4096)
                mr["sqrt"] = I(i, c_act)

            @blk.vector
            def _(v):
                W(v, "vec", c_act, mr["sqrt"])
                i = v.reciprocal(rowC[0:1, :], rowB[0:1, :])
                mr["recip"] = I(i, c_vec)

            @blk.tensor
            def _(t):
                W(t, "pe", c_vec, mr["recip"])
                i = t.matmul(pb[7][:, 0:256], onesr[0:1, :], rowC[0:1, :],
                             start=True, stop=True)
                mr["bcast"] = I(i, c_pe)

            @blk.vector
            def _(v):
                W(v, "vec", c_pe, mr["bcast"])
                i = v.tensor_copy(sdo[0][:, :], pb[7][:, 0:256])
                mr["rstdb"] = I(i, c_vec)
            def _emit_op_evac(blk, oc):
                @blk.vector
                def _(v, oc=oc):
                    W(v, "vec", c_pe, mk_mm[2 * oc + 1])
                    sl = slice(oc * T, (oc + 1) * T)
                    v.tensor_mul(tu[oc % 2][:, :], pb[4 + oc % 2][:, 0:256],
                                 sdo[0][:, :])
                    i = v.tensor_add(x_fm[:, sl], tu[oc % 2][:, :], x_fm[:, sl])
                    mk_ev[oc] = I(i, c_vec)

            for hs in range(32):
                oc, kh = hs // 2, hs % 2

                @blk.sync
                def _(sync, hs=hs, oc=oc, kh=kh):
                    if hs >= 2:
                        W(sync, "sy", c_vec, mk_lo[hs - 2])
                    src = wop.ap()[oc, kh].rearrange("p k m -> p (k m)")
                    mk_load[hs] = I(sync.dma_start(proj32[hs % 2][:, :], src),
                                    c_hw, 16)

                @blk.vector
                def _(v, hs=hs):
                    W(v, "vec", c_hw, mk_load[hs])
                    if hs >= 2:
                        W(v, "vec", c_pe, mk_mm[hs - 2])
                    v.tensor_copy(projh[hs % 2][:, :], proj32[hs % 2][:, :])
                    i = v.tensor_sub(projl[hs % 2][:, :], proj32[hs % 2][:, :],
                                     projh[hs % 2][:, :])
                    mk_lo[hs] = I(i, c_vec)

                if hs >= 1 and hs % 2 == 0:
                    _emit_op_evac(blk, hs // 2 - 1)

                @blk.tensor
                def _(t, hs=hs, oc=oc, kh=kh):
                    W(t, "pe", c_vec, mk_lo[hs])
                    if kh == 0 and oc >= 2:
                        W(t, "pe", c_vec, mk_ev[oc - 2])
                    pd = pb[4 + oc % 2][:, 0:256]
                    for kb in range(16):
                        kg = kh * 16 + kb
                        wh = projh[hs % 2][:, kb * 128:(kb + 1) * 128]
                        wl = projl[hs % 2][:, kb * 128:(kb + 1) * 128]
                        gh = ygp[:, kg * 512:kg * 512 + 256]
                        gl = ygp[:, kg * 512 + 256:kg * 512 + 512]
                        t.matmul(pd, wh, gh, start=(kh == 0 and kb == 0), stop=False)
                        t.matmul(pd, wh, gl, start=False, stop=False)
                        i = t.matmul(pd, wl, gh, start=False,
                                     stop=(kh == 1 and kb == 15))
                    mk_mm[hs] = I(i, c_pe)

            _emit_op_evac(blk, 15)
            return mk_ev, mk_lo, mk_mm, mr

    st_mark = 0
    for ch in range(n_chunks):
        pref_gu1, pref_dn1, pref_gu2, pref_dn2 = {}, {}, {}, {}
        mxbox = {}

        with phase("loadln1_c%d" % ch) as blk:
            @blk.sync
            def _(sync, ch=ch):
                W(sync, "sy", c_st, st_mark)
                srcx = xt.ap()[:, ch * T:(ch + 1) * T].rearrange(
                    "(fc p) t -> p fc t", p=128)
                dstx = x_fm[:, :].rearrange("p (fc t) -> p fc t", fc=FCX)
                mxbox["x"] = I(sync.dma_start(dstx, srcx), c_hw, 16)
            if do_ffn1:
                emit_gateup_prefetch(blk, pref_gu1)
                emit_ln(blk, pre=((c_hw, lambda: mxbox["x"]),))

        if do_ffn1:
            with phase("ffn1_c%d" % ch) as blk:
                gum1 = {}
                fhi, flo = emit_ffn_gateup(blk, pref_gu1, pref_dn1, gum1)
                dn_ev = emit_ffn_down(blk, pref_dn1, fhi, flo)
                if do_mamba:
                    emit_ln(blk, fc_wait=lambda fc: (c_vec, lambda: dn_ev[fc]))
        if do_mamba:
            with phase("inproj_c%d" % ch) as blk:
                emit_inproj(blk, ch)
            ssdm = {}
            with phase("ssdrms_c%d" % ch) as blk:
                ssdm.update(emit_ssd(blk, ch))
            with phase("outprojln3_c%d" % ch) as blk:
                op_ev, op_lo, op_mm, op_mr = emit_outproj(blk, ssdm)
                if do_ffn2:
                    emit_ln(blk, fc_wait=lambda fc: (c_vec, lambda: op_ev[fc]),
                            pre=((c_pe, lambda: op_mr["mm31"]),))
                    emit_gateup_prefetch(blk, pref_gu2,
                                         waits=((c_vec, lambda: op_lo[31]),
                                                (c_pe, lambda: op_mm[31])))
        if do_ffn2:
            with phase("ffn2_c%d" % ch) as blk:
                gum2 = {}
                fhi2, flo2 = emit_ffn_gateup(blk, pref_gu2, pref_dn2, gum2)
                dn_ev2 = emit_ffn_down(blk, pref_dn2, fhi2, flo2)

                @blk.sync
                def _(sync, ch=ch):
                    W(sync, "sy", c_vec, lambda: dn_ev2[15])
                    dsty = yt.ap()[:, ch * T:(ch + 1) * T].rearrange(
                        "(fc p) t -> p fc t", p=128)
                    srcy = x_fm[:, :].rearrange("p (fc t) -> p fc t", fc=FCX)
                    I(sync.dma_start(dsty, srcy), c_st, 16)
        st_mark = c_st.v

    with nc.Block() as blk:
        @blk.sync
        def _(sync):
            sync.wait_ge(c_st.h, c_st.v)

    return nc



def prep_in_maps(L0_attn_hidden_states, L0_attn_in_proj_w, L0_attn_conv_w,
                 L0_attn_conv_b, L0_attn_dt_bias, L0_attn_A_log, L0_attn_D,
                 L0_attn_norm_weight, L0_attn_out_proj_w, L0_ffn_w_gate,
                 L0_ffn_w_up, L0_ffn_w_down, L0_ffn_s_gate, L0_ffn_s_up,
                 L0_ffn_s_down):
    x = np.ascontiguousarray(np.asarray(L0_attn_hidden_states, dtype=np.float32))
    wip = np.asarray(L0_attn_in_proj_w, dtype=np.float32)
    wout = np.asarray(L0_attn_out_proj_w, dtype=np.float32)
    cw = np.asarray(L0_attn_conv_w, dtype=np.float32)
    cb = np.asarray(L0_attn_conv_b, dtype=np.float32)
    dtb = np.asarray(L0_attn_dt_bias, dtype=np.float32)
    alog = np.asarray(L0_attn_A_log, dtype=np.float32)
    dvec = np.asarray(L0_attn_D, dtype=np.float32)
    nw = np.asarray(L0_attn_norm_weight, dtype=np.float32)
    wg = np.asarray(L0_ffn_w_gate)
    wu = np.asarray(L0_ffn_w_up)
    wd = np.asarray(L0_ffn_w_down)
    sgt = np.asarray(L0_ffn_s_gate, dtype=np.float32)
    sut = np.asarray(L0_ffn_s_up, dtype=np.float32)
    sdt = np.asarray(L0_ffn_s_down, dtype=np.float32)

    wipm = np.ascontiguousarray(
        wip[:, :8448].reshape(16, 128, 66, 128).transpose(2, 1, 0, 3))
    wipd = np.ascontiguousarray(
        wip[:, 8448:].reshape(16, 128, 64).transpose(1, 0, 2))
    wopr = np.ascontiguousarray(
        wout.reshape(2, 16, 128, 16, 128).transpose(3, 0, 2, 1, 4))
    wg_r = np.ascontiguousarray(wg.reshape(16, 128, 64, 128).transpose(2, 1, 0, 3))
    wu_r = np.ascontiguousarray(wu.reshape(16, 128, 64, 128).transpose(2, 1, 0, 3))
    wd_r = np.ascontiguousarray(wd.reshape(64, 128, 16, 128).transpose(2, 1, 0, 3))

    shared = {
        "wipm": wipm, "wipd": wipd, "wop": wopr,
        "wg8": wg_r, "wu8": wu_r, "wd8": wd_r,
        "sgate_d": np.ascontiguousarray(sgt.reshape(64, 128).T),
        "sup_d": np.ascontiguousarray(sut.reshape(64, 128).T),
        "sdownh_d": np.ascontiguousarray((0.5 * sdt).reshape(16, 128).T),
        "nw_d": np.ascontiguousarray(nw.reshape(32, 128).T),
        "convb_d": np.ascontiguousarray(cb.reshape(34, 128).T),
        "convw_d": np.ascontiguousarray(
            cw.reshape(34, 128, 4).transpose(1, 0, 2).reshape(128, 136)),
        "dtb_d": dtb.reshape(64, 1),
        "alog_d": alog.reshape(64, 1),
        "dvec_d": dvec.reshape(1, 64),
    }
    in_maps = []
    for i in range(8):
        d = dict(shared)
        d["xt"] = np.ascontiguousarray(x[i].T)
        in_maps.append(d)
    return in_maps


def kernel(**inputs):
    in_maps = prep_in_maps(**inputs)
    nc = build_nc()
    res = run_bass_kernel_spmd(nc, in_maps, core_ids=list(range(8)))
    out = np.empty((8, NTOK, HID), dtype=np.float32)
    for i in range(8):
        out[i] = np.asarray(res.results[i]["yt"]).T
    return out


# version 17
# speedup vs baseline: 1.1071x; 1.0093x over previous
import numpy as np
import concourse.bass as bass
import concourse.mybir as mybir
from concourse.bass_utils import run_bass_kernel_spmd
from concourse.bass_types import AP

F32 = mybir.dt.float32
BF16 = mybir.dt.bfloat16
I8 = mybir.dt.int8
U8 = mybir.dt.uint8
AT = mybir.ActivationFunctionType
ALU = mybir.AluOpType

HID = 2048
IFF = 8192
IM = 4096
NH = 64
CS = 256
NTOK = 2048
NCH = NTOK // CS
T = CS
FCX = 16
NIB = 64
EPS = 1e-5
XBCW = 259

RACE_CHECK = False


class Ctr:
    def __init__(self, nc, name):
        self.h = nc.alloc_semaphore(name)
        self.v = 0


def build_nc(n_chunks=NCH, do_ffn1=True, do_mamba=True, do_ffn2=True):
    nc = bass.Bass("TRN2", target_bir_lowering=False,
                   detect_race_conditions=RACE_CHECK)

    xt = nc.dram_tensor("xt", [HID, NTOK], F32, kind="ExternalInput")
    yt = nc.dram_tensor("yt", [HID, NTOK], F32, kind="ExternalOutput")
    wipm = nc.dram_tensor("wipm", [66, 128, 16, 128], F32, kind="ExternalInput")
    wipd = nc.dram_tensor("wipd", [128, 16, 64], F32, kind="ExternalInput")
    wop = nc.dram_tensor("wop", [16, 2, 128, 16, 128], F32, kind="ExternalInput")
    wg8 = nc.dram_tensor("wg8", [64, 128, 16, 128], I8, kind="ExternalInput")
    wu8 = nc.dram_tensor("wu8", [64, 128, 16, 128], I8, kind="ExternalInput")
    wd8 = nc.dram_tensor("wd8", [16, 128, 64, 128], I8, kind="ExternalInput")
    sgate_d = nc.dram_tensor("sgate_d", [128, 64], F32, kind="ExternalInput")
    sup_d = nc.dram_tensor("sup_d", [128, 64], F32, kind="ExternalInput")
    sdownh_d = nc.dram_tensor("sdownh_d", [128, 16], F32, kind="ExternalInput")
    nw_d = nc.dram_tensor("nw_d", [128, 32], F32, kind="ExternalInput")
    convb_d = nc.dram_tensor("convb_d", [128, 34], F32, kind="ExternalInput")
    convw_d = nc.dram_tensor("convw_d", [128, 136], F32, kind="ExternalInput")
    dtb_d = nc.dram_tensor("dtb_d", [64, 1], F32, kind="ExternalInput")
    alog_d = nc.dram_tensor("alog_d", [64, 1], F32, kind="ExternalInput")
    dvec_d = nc.dram_tensor("dvec_d", [1, 64], F32, kind="ExternalInput")
    wgbf = nc.dram_tensor("wgbf", [64, 128, 16, 128], BF16, kind="Internal")
    wubf = nc.dram_tensor("wubf", [64, 128, 16, 128], BF16, kind="Internal")
    wdbf = nc.dram_tensor("wdbf", [16, 128, 64, 128], BF16, kind="Internal")

    ident_d = nc.inline_tensor(np.eye(128, dtype=np.float32), name="ident_c")
    onesr_d = nc.inline_tensor(np.ones((1, 128), dtype=np.float32), name="onesr_c")
    onesc_d = nc.inline_tensor(np.ones((128, 1), dtype=np.float32), name="onesc_c")
    dg0 = np.zeros((128, 256), dtype=np.float32)
    dg0[np.arange(128), np.arange(128)] = 1.0
    dg1 = np.zeros((128, 256), dtype=np.float32)
    dg1[np.arange(128), 128 + np.arange(128)] = 1.0
    diag0_d = nc.inline_tensor(dg0, name="diag0_c")
    diag1_d = nc.inline_tensor(dg1, name="diag1_c")
    zeros_d = nc.inline_tensor(np.zeros((128, 256), dtype=np.float32), name="zeros_c")

    A = nc.alloc_sbuf_tensor
    AT_ = nc.alloc_sbuf_tensor_at
    x_fm = A("x_fm", [128, FCX * T], F32)
    s_all = A("s_all", [128, 4096], F32)

    r_xny = A("r_xny", [128, 32768], U8)
    bx = nc.lookup_mloc(r_xny).addr
    xn32 = AT_("xn32", [128, 4096], F32, offset=bx)
    xnp = AT_("xnp", [128, 8192], BF16, offset=bx + 16384)
    y_fm = AT_("y_fm", [128, 8192], F32, offset=bx)

    r_big = A("r_big", [128, 68160], U8)
    bb = nc.lookup_mloc(r_big).addr
    fusep = AT_("fusep", [128, 32768], BF16, offset=bb)
    gate_fm = AT_("gate_fm", [128, 8192], F32, offset=bb)
    lnsq = AT_("lnsq", [128, 4096], F32, offset=bx)
    xbc = AT_("xbc", [128, 34 * XBCW], F32, offset=bb + 32768)
    ygp = AT_("ygp", [128, 16384], BF16, offset=bb + 32768)

    r_wb = A("r_wb", [128, 32768], U8)
    bw = nc.lookup_mloc(r_wb).addr
    wgb = [AT_("wgb%d" % i, [128, 4096], BF16, offset=bw + i * 16384) for i in range(2)]
    wub = [AT_("wub%d" % i, [128, 4096], BF16, offset=bw + 8192 + i * 16384) for i in range(2)]
    wdb = [AT_("wdb%d" % i, [128, 8192], BF16, offset=bw + i * 16384) for i in range(2)]
    proj32 = [AT_("proj32%d" % i, [128, 2048], F32, offset=bw + i * 8192) for i in range(2)]
    projh = [AT_("projh%d" % i, [128, 2048], BF16, offset=bw + 16384 + i * 4096) for i in range(2)]
    projl = [AT_("projl%d" % i, [128, 2048], BF16, offset=bw + 24576 + i * 4096) for i in range(2)]

    ea_p0 = A("ea_p0", [1, 2048], F32)
    halosave = A("halosave", [128, 102], F32)
    tg = [A("tg%d" % i, [128, 256], F32) for i in range(2)]
    tu = [A("tu%d" % i, [128, 256], F32) for i in range(2)]
    cbt = A("cbt", [128, 512], F32)
    ldec = [A("ldec%d" % i, [128, 512], F32) for i in range(2)]
    sdo = [A("sdo%d" % i, [128, 256], F32) for i in range(2)]
    b_tm = A("b_tm", [128, 256], F32)
    bd_tm = [A("bd_tm%d" % i, [128, 256], F32) for i in range(2)]
    xdt = [A("xdt%d" % i, [128, 128], F32) for i in range(2)]
    dt32 = A("dt32", [64, 256], F32)
    escr = A("escr", [64, 256], F32)
    ea_fm = A("ea_fm", [64, 256], F32)
    dt_tm = A("dt_tm", [128, 128], F32)
    rowA = A("rowA", [1, 256], F32)
    rowB = A("rowB", [1, 256], F32)
    rowC = A("rowC", [1, 256], F32)

    ident = A("ident", [128, 128], F32)
    onesr = A("onesr", [1, 128], F32)
    onesc = A("onesc", [128, 1], F32)
    diag0 = A("diag0", [128, 256], F32)
    diag1 = A("diag1", [128, 256], F32)
    zeros = A("zeros", [128, 256], F32)
    sg_sb = A("sg_sb", [128, 64], F32)
    su_sb = A("su_sb", [128, 64], F32)
    sdh_sb = A("sdh_sb", [128, 16], F32)
    nw_sb = A("nw_sb", [128, 32], F32)
    convb_sb = A("convb_sb", [128, 34], F32)
    convw_sb = A("convw_sb", [128, 136], F32)
    dtb_sb = A("dtb_sb", [64, 1], F32)
    negA = A("negA", [64, 1], F32)
    dexpb = A("dexpb", [128, 64], F32)
    d_p0 = A("d_p0", [1, 64], F32)
    epscol = A("epscol", [128, 1], F32)
    lnacc = A("lnacc", [128, 512], F32)
    sqt = A("sqt", [128, 512], F32)

    pb = [nc.alloc_psum_tensor("pb%d" % i, [128, 512], F32) for i in range(8)]

    c_hw = Ctr(nc, "c_hw")
    c_sw = Ctr(nc, "c_sw")
    c_pe = Ctr(nc, "c_pe")
    c_act = Ctr(nc, "c_act")
    c_vec = Ctr(nc, "c_vec")
    c_gp = Ctr(nc, "c_gp")
    c_st = Ctr(nc, "c_st")

    hiwater = {}

    dry_box = [False]

    def W(eng, key, ctr, val):
        if callable(val):
            val = val()
        if dry_box[0] or val <= 0:
            return
        k = (key, ctr.h.num)
        if hiwater.get(k, 0) < val:
            eng.wait_ge(ctr.h, val)
            hiwater[k] = val

    def I(instr, ctr, k=1):
        if instr is not None:
            instr.then_inc(ctr.h, k)
        ctr.v += k
        return ctr.v

    class _Dry:
        def __getattr__(self, name):
            return lambda *a, **kw: None

    class _PhaseCollector:
        def __init__(self):
            self.items = []

        def _deco(self, key):
            def d(fn):
                self.items.append((key, fn))
            return d

        @property
        def sync(self):
            return self._deco("sync")

        @property
        def gpsimd(self):
            return self._deco("gpsimd")

        @property
        def scalar(self):
            return self._deco("scalar")

        @property
        def tensor(self):
            return self._deco("tensor")

        @property
        def vector(self):
            return self._deco("vector")

    _ctrs = None

    from contextlib import contextmanager

    @contextmanager
    def phase(pname="ph"):
        ph = _PhaseCollector()
        yield ph
        ctrs = (c_hw, c_sw, c_pe, c_act, c_vec, c_gp, c_st)
        snap = [c.v for c in ctrs]
        dry_box[0] = True
        dg = _Dry()
        for _k, fn in ph.items:
            fn(dg)
        dry_box[0] = False
        finals = [c.v for c in ctrs]
        for c, s in zip(ctrs, snap):
            c.v = s
        with nc.named_scope(pname), nc.Block() as blk:
            for key in ("sync", "gpsimd", "scalar", "tensor", "vector"):
                fns = [f for k2, f in ph.items if k2 == key]
                if fns:
                    getattr(blk, key)(lambda eng, fns=fns: [f(eng) for f in fns])
        for c, f in zip(ctrs, finals):
            assert c.v == f, "phase counter mismatch (dry=%d real=%d)" % (f, c.v)

    with nc.Block() as blk:
        marks = {}

        @blk.sync
        def _(sync):
            for dst, src in (
                (ident, ident_d), (onesr, onesr_d), (onesc, onesc_d),
                (diag0, diag0_d), (diag1, diag1_d), (zeros, zeros_d),
                (sg_sb, sgate_d), (su_sb, sup_d), (sdh_sb, sdownh_d),
                (nw_sb, nw_d), (convb_sb, convb_d), (convw_sb, convw_d),
                (dtb_sb, dtb_d), (negA, alog_d), (d_p0, dvec_d),
            ):
                I(sync.dma_start(dst[:, :], src.ap()), c_hw, 16)
            marks["pre"] = c_hw.v

        @blk.gpsimd
        def _(g):
            g.memset(s_all[:, :], 0.0)
            g.memset(ea_p0[:, :], 0.0)
            g.memset(halosave[:, :], 0.0)
            g.memset(epscol[:, :], EPS)

        @blk.scalar
        def _(s):
            W(s, "act", c_hw, marks["pre"])
            s.activation(negA[:, :], negA[:, :], AT.Exp)
            s.mul(negA[:, :], negA[:, :], -1.0)

        @blk.tensor
        def _(t):
            W(t, "pe", c_hw, marks["pre"])
            m = t.matmul(pb[7][:, 0:64], onesr[0:1, :], d_p0[0:1, :],
                         start=True, stop=True)
            marks["dexp"] = I(m, c_pe)

        @blk.vector
        def _(v):
            W(v, "vec", c_pe, marks["dexp"])
            v.tensor_copy(dexpb[:, :], pb[7][:, 0:64])

    castq = {}
    with phase("wcast") as blk:
        @blk.gpsimd
        def _(g):
            for q in range(4):
                i = g.dma_start(wgbf.ap()[16 * q:16 * q + 16],
                                wg8.ap()[16 * q:16 * q + 16])
                castq["wg%d" % q] = I(i, c_sw, 16)
            for q in range(4):
                i = g.dma_start(wubf.ap()[16 * q:16 * q + 16],
                                wu8.ap()[16 * q:16 * q + 16])
                castq["wu%d" % q] = I(i, c_sw, 16)
            for q in range(2):
                i = g.dma_start(wdbf.ap()[8 * q:8 * q + 8],
                                wd8.ap()[8 * q:8 * q + 8])
                castq["wd%d" % q] = I(i, c_sw, 16)


    def emit_ln_sq_fc(blk, fc, m, fcw):
        @blk.scalar
        def _(s, fc=fc):
            fcw(s, "act", fc)
            i = s.activation(lnsq[:, fc * T:(fc + 1) * T],
                             x_fm[:, fc * T:(fc + 1) * T], AT.Square)
            m["sq%d" % fc] = I(i, c_act)

    def emit_ln_acc_fc(blk, fc, m, fcw):
        @blk.vector
        def _(v, fc=fc):
            fcw(v, "vec", fc)
            sl = slice(fc * T, (fc + 1) * T)
            if fc == 0:
                v.tensor_copy(lnacc[:, 0:256], x_fm[:, sl])
            else:
                v.tensor_add(lnacc[:, 0:256], lnacc[:, 0:256], x_fm[:, sl])
            W(v, "vec", c_act, m["sq%d" % fc])
            if fc == 0:
                i = v.tensor_copy(lnacc[:, 256:512], lnsq[:, sl])
            else:
                i = v.tensor_add(lnacc[:, 256:512], lnacc[:, 256:512],
                                 lnsq[:, sl])
            if fc == FCX - 1:
                m["acc"] = I(i, c_vec)

    def emit_ln_stats(blk, m, pe_pre=()):
        @blk.tensor
        def _(t):
            for ctr, val in pe_pre:
                W(t, "pe", ctr, val)
            W(t, "pe", c_vec, lambda: m["acc"])
            i = t.matmul(pb[6][0:1, 0:512], onesc[:, 0:1], lnacc[:, :],
                         start=True, stop=True)
            m["stats"] = I(i, c_pe)

        @blk.scalar
        def _(s):
            W(s, "act", c_pe, m["stats"])
            s.mul(rowA[0:1, :], pb[6][0:1, 0:256], 1.0 / 2048)
            s.mul(rowB[0:1, :], pb[6][0:1, 256:512], 1.0 / 2048)
            i = s.activation(rowC[0:1, :], rowA[0:1, :], AT.Square)
            m["rows1"] = I(i, c_act)

        @blk.vector
        def _(v):
            W(v, "vec", c_act, m["rows1"])
            i = v.tensor_sub(rowB[0:1, :], rowB[0:1, :], rowC[0:1, :])
            m["var"] = I(i, c_vec)

        @blk.scalar
        def _(s):
            W(s, "act", c_vec, m["var"])
            i = s.activation(rowB[0:1, :], rowB[0:1, :], AT.Sqrt,
                             bias=epscol[0:1, 0:1])
            m["sqrt"] = I(i, c_act)

        @blk.vector
        def _(v):
            W(v, "vec", c_act, m["sqrt"])
            v.reciprocal(rowC[0:1, :], rowB[0:1, :])
            i = v.scalar_tensor_tensor(rowA[0:1, :], rowA[0:1, :], -1.0,
                                       rowC[0:1, :], ALU.mult, ALU.mult)
            m["rows2"] = I(i, c_vec)

        @blk.tensor
        def _(t):
            W(t, "pe", c_vec, m["rows2"])
            t.matmul(pb[7][:, 0:256], onesr[0:1, :], rowC[0:1, :],
                     start=True, stop=True)
            i = t.matmul(pb[7][:, 256:512], onesr[0:1, :], rowA[0:1, :],
                         start=True, stop=True)
            m["bcast"] = I(i, c_pe)

    def emit_ln_norm(blk, m, mk_lo):
        for j in range(FCX + 2):
            @blk.vector
            def _(v, j=j):
                if j >= 2:
                    fc = j - 2
                    sl = slice(fc * T, (fc + 1) * T)
                    W(v, "vec", c_act, m["hi%d" % fc])
                    i = v.tensor_sub(xnp[:, fc * 512 + 256:fc * 512 + 512],
                                     xn32[:, sl], xnp[:, fc * 512:fc * 512 + 256])
                    mk_lo[fc] = I(i, c_vec)
                if j < FCX:
                    if j == 0:
                        W(v, "vec", c_pe, m["bcast"])
                    sl = slice(j * T, (j + 1) * T)
                    v.tensor_tensor(xn32[:, sl], x_fm[:, sl], pb[7][:, 0:256],
                                    ALU.mult)
                    i = v.tensor_tensor(xn32[:, sl], xn32[:, sl],
                                        pb[7][:, 256:512], ALU.add)
                    m["xn%d" % j] = I(i, c_vec)

            if j < FCX:
                @blk.scalar
                def _(s, j=j):
                    W(s, "act", c_vec, m["xn%d" % j])
                    sl = slice(j * T, (j + 1) * T)
                    i = s.copy(xnp[:, j * 512:j * 512 + 256], xn32[:, sl])
                    m["hi%d" % j] = I(i, c_act)

    def emit_gateup_prefetch(blk, pref, waits=()):
        @blk.sync
        def _(sync):
            for ctr, val in waits:
                W(sync, "sy", ctr, val)
            W(sync, "sy", c_sw, castq["wg0"])
            W(sync, "sy", c_sw, castq["wu0"])
            for gr in (0, 1):
                srcg = wgbf.ap()[2 * gr:2 * gr + 2].rearrange("i p k m -> p i (k m)")
                dstg = wgb[gr][:, :].rearrange("p (i km) -> p i km", i=2)
                I(sync.dma_start(dstg, srcg), c_hw, 16)
                srcu = wubf.ap()[2 * gr:2 * gr + 2].rearrange("i p k m -> p i (k m)")
                dstu = wub[gr][:, :].rearrange("p (i km) -> p i km", i=2)
                pref[gr] = I(sync.dma_start(dstu, srcu), c_hw, 16)

    def emit_ffn_gateup(blk, pref_gu, pref_dn, gu_marks, xnp_marks=None):
        if True:
            mk_dma = dict(pref_gu)
            mk_g = {}
            mk_u = {}
            mk_silu = {}
            mk_gsum = {}
            mk_usum = {}
            mk_prod = {}
            mk_hi = {}
            mk_flo = {}
            mk_grp = {}
            for gr in range(32):
                if gr >= 2:
                    @blk.sync
                    def _(g, gr=gr):
                        W(g, "sy", c_pe, mk_grp[gr - 2])
                        W(g, "sy", c_sw, castq["wg%d" % (gr // 8)])
                        W(g, "sy", c_sw, castq["wu%d" % (gr // 8)])
                        src = wgbf.ap()[2 * gr:2 * gr + 2].rearrange("i p k m -> p i (k m)")
                        dst = wgb[gr % 2][:, :].rearrange("p (i km) -> p i km", i=2)
                        I(g.dma_start(dst, src), c_hw, 16)
                        src = wubf.ap()[2 * gr:2 * gr + 2].rearrange("i p k m -> p i (k m)")
                        dst = wub[gr % 2][:, :].rearrange("p (i km) -> p i km", i=2)
                        mk_dma[gr] = I(g.dma_start(dst, src), c_hw, 16)

                for i2 in range(2):
                    ib = 2 * gr + i2
                    par = ib % 2

                    @blk.tensor
                    def _(t, gr=gr, i2=i2, ib=ib, par=par):
                        W(t, "pe", c_hw, mk_dma[gr])
                        if ib >= 2:
                            W(t, "pe", c_vec, mk_gsum[ib - 2])
                            W(t, "pe", c_vec, mk_usum[ib - 2])
                        pg = pb[par][:, 0:512]
                        pu = pb[2 + par][:, 0:512]
                        for kb in range(16):
                            if ib == 0 and xnp_marks is not None:
                                W(t, "pe", c_vec, xnp_marks[kb])
                            wt = wgb[gr % 2][:, (i2 * 16 + kb) * 128:(i2 * 16 + kb + 1) * 128]
                            i = t.matmul(pg, wt, xnp[:, kb * 512:(kb + 1) * 512],
                                         start=(kb == 0), stop=(kb == 15))
                        mk_g[ib] = I(i, c_pe)
                        for kb in range(16):
                            wt = wub[gr % 2][:, (i2 * 16 + kb) * 128:(i2 * 16 + kb + 1) * 128]
                            i = t.matmul(pu, wt, xnp[:, kb * 512:(kb + 1) * 512],
                                         start=(kb == 0), stop=(kb == 15))
                        mk_u[ib] = I(i, c_pe)
                        if i2 == 1:
                            mk_grp[gr] = mk_u[ib]

                    @blk.vector
                    def _(v, ib=ib, par=par):
                        W(v, "vec", c_pe, mk_g[ib])
                        if ib >= 2:
                            W(v, "vec", c_act, mk_hi[ib - 2])
                        v.tensor_copy(tg[par][:, :], pb[par][:, 256:512])
                        i = v.tensor_add(tg[par][:, :], tg[par][:, :],
                                         pb[par][:, 0:256])
                        mk_gsum[ib] = I(i, c_vec)
                        W(v, "vec", c_pe, mk_u[ib])
                        v.tensor_copy(tu[par][:, :], pb[2 + par][:, 256:512])
                        i = v.tensor_add(tu[par][:, :], tu[par][:, :],
                                         pb[2 + par][:, 0:256])
                        mk_usum[ib] = I(i, c_vec)

                    @blk.scalar
                    def _(s, ib=ib, par=par):
                        W(s, "act", c_vec, mk_gsum[ib])
                        i = s.activation(tg[par][:, :], tg[par][:, :],
                                         AT.Silu, scale=sg_sb[:, ib:ib + 1])
                        mk_silu[ib] = I(i, c_act)

                    @blk.vector
                    def _(v, ib=ib, par=par):
                        W(v, "vec", c_act, mk_silu[ib])
                        i = v.scalar_tensor_tensor(tg[par][:, :], tu[par][:, :],
                                                   su_sb[:, ib:ib + 1],
                                                   tg[par][:, :], ALU.mult, ALU.mult)
                        mk_prod[ib] = I(i, c_vec)

                    @blk.scalar
                    def _(s, ib=ib, par=par):
                        W(s, "act", c_vec, mk_prod[ib])
                        i = s.copy(fusep[:, ib * 512:ib * 512 + 256], tg[par][:, :])
                        mk_hi[ib] = I(i, c_act)

                    @blk.vector
                    def _(v, ib=ib, par=par):
                        W(v, "vec", c_act, mk_hi[ib])
                        i = v.tensor_sub(fusep[:, ib * 512 + 256:ib * 512 + 512],
                                         tg[par][:, :],
                                         fusep[:, ib * 512:ib * 512 + 256])
                        mk_flo[ib] = I(i, c_vec)

            @blk.sync
            def _(sync):
                W(sync, "sy", c_sw, castq["wd0"])
                W(sync, "sy", c_pe, mk_grp[30])
                pref_dn[0] = I(sync.dma_start(
                    wdb[0][:, :], wdbf.ap()[0].rearrange("p k m -> p (k m)")),
                    c_hw, 16)
                W(sync, "sy", c_pe, mk_grp[31])
                pref_dn[1] = I(sync.dma_start(
                    wdb[1][:, :], wdbf.ap()[1].rearrange("p k m -> p (k m)")),
                    c_hw, 16)
            return mk_hi, mk_flo

    def emit_ffn_down(blk, pref_dn, fuse_hi_mk, fuse_lo_mk, ln_m=None):
        if True:
            mk_dma = {}
            mk_mm = {}
            mk_ev = {}
            for oc in range(16):
                if oc >= 2:
                    @blk.sync
                    def _(g, oc=oc):
                        W(g, "sy", c_sw, castq["wd%d" % (oc // 8)])
                        W(g, "sy", c_pe, mk_mm[oc - 2])
                        src = wdbf.ap()[oc].rearrange("p k m -> p (k m)")
                        mk_dma[oc] = I(g.dma_start(wdb[oc % 2][:, :], src), c_hw, 16)

                @blk.tensor
                def _(t, oc=oc):
                    par = oc % 2
                    W(t, "pe", c_hw, mk_dma[oc] if oc >= 2 else (lambda oc=oc: pref_dn[oc]))
                    if oc == 0:
                        W(t, "pe", c_act, lambda: fuse_hi_mk[63])
                        W(t, "pe", c_vec, lambda: fuse_lo_mk[63])
                    if oc >= 2:
                        W(t, "pe", c_vec, mk_ev[oc - 2])
                    pd = pb[4 + par][:, 0:512]
                    for kc in range(64):
                        wt = wdb[par][:, kc * 128:(kc + 1) * 128]
                        i = t.matmul(pd, wt, fusep[:, kc * 512:(kc + 1) * 512],
                                     start=(kc == 0), stop=(kc == 63))
                    mk_mm[oc] = I(i, c_pe)

                @blk.vector
                def _(v, oc=oc):
                    par = oc % 2
                    W(v, "vec", c_pe, mk_mm[oc])
                    sl = slice(oc * T, (oc + 1) * T)
                    v.tensor_copy(tu[par][:, :], pb[4 + par][:, 256:512])
                    v.tensor_add(tu[par][:, :], tu[par][:, :], pb[4 + par][:, 0:256])
                    i = v.scalar_tensor_tensor(x_fm[:, sl], tu[par][:, :],
                                               sdh_sb[:, oc:oc + 1],
                                               x_fm[:, sl], ALU.mult, ALU.add)
                    mk_ev[oc] = I(i, c_vec)

                if ln_m is not None:
                    fcw = lambda eng, key, fc: W(eng, key, c_vec, mk_ev[fc])
                    emit_ln_sq_fc(blk, oc, ln_m, fcw)
                    emit_ln_acc_fc(blk, oc, ln_m, fcw)
            return mk_ev

    def _emit_proj_evac(blk, ob, mk_mm, mk_ev, pslot=None):
        @blk.scalar
        def _(s, ob=ob):
            W(s, "act", c_pe, mk_mm[ob])
            pp = pb[pslot if pslot is not None else ob % 4][:, 0:256]
            if ob < 32:
                i = s.copy(gate_fm[:, ob * T:(ob + 1) * T], pp)
            elif ob < 66:
                off = (ob - 32) * XBCW
                i = s.copy(xbc[:, off + 3:off + XBCW], pp)
            else:
                s.activation(escr[:, :], pp[0:64, :], AT.Exp,
                             bias=dtb_sb[:, 0:1])
                i = s.activation(dt32[:, :], escr[:, :], AT.Ln, bias=onesc[0:64, 0:1])
            mk_ev[ob] = I(i, c_act)

    def _emit_conv_tile(blk, bc, mk_ev, mkc):
        seq = mkc.setdefault("_seq", [])
        prev2 = seq[-2] if len(seq) >= 2 else None
        seq.append(bc)
        par = len(seq) % 2
        off = bc * XBCW

        @blk.vector
        def _(v, bc=bc, par=par, off=off, prev2=prev2):
            W(v, "vec", c_act, mk_ev[32 + bc])
            i = v.tensor_copy(halosave[:, bc * 3:bc * 3 + 3],
                              xbc[:, off + 256:off + 259])
            if prev2 is not None:
                W(v, "vec", c_act, mkc["si%d" % prev2])
            v.tensor_scalar(tg[par][:, :], xbc[:, off + 0:off + 256],
                            convw_sb[:, bc * 4:bc * 4 + 1], None, ALU.mult)
            for k in (1, 2, 3):
                i = v.scalar_tensor_tensor(
                    tg[par][:, :], xbc[:, off + k:off + k + 256],
                    convw_sb[:, bc * 4 + k:bc * 4 + k + 1],
                    tg[par][:, :], ALU.mult, ALU.add)
            mkc["acc%d" % bc] = I(i, c_vec)

        @blk.scalar
        def _(s, bc=bc, par=par, off=off):
            W(s, "act", c_vec, mkc["acc%d" % bc])
            i = s.activation(xbc[:, off + 3:off + XBCW], tg[par][:, :],
                             AT.Silu, bias=convb_sb[:, bc:bc + 1])
            mkc["si%d" % bc] = I(i, c_act)

    def emit_inproj(blk, ch, ln_m=None):
        if True:
            mk_load = {}
            mk_hi = {}
            mk_lo = {}
            mk_mm = {}
            mk_ev = {}
            mkc = {}
            mk_xlo = {}

            @blk.vector
            def _(v):
                heads = xbc[:, :].rearrange("p (bc c) -> p bc c", bc=34)[:, :, 0:3]
                if ch == 0:
                    v.memset(heads, 0.0)
                else:
                    v.tensor_copy(heads, halosave[:, :].rearrange(
                        "p (bc c) -> p bc c", bc=34))

            if ln_m is not None:
                emit_ln_stats(blk, ln_m)

            order = [66, 64, 65] + list(range(64))

            def emit_load_split(si, ob, wfull):
                @blk.sync
                def _(sync, si=si, ob=ob, wfull=wfull):
                    if si >= 2:
                        W(sync, "sy", c_vec, mk_lo[order[si - 2]])
                    if ob < 66:
                        src = wipm.ap()[ob].rearrange("p k m -> p (k m)")
                    else:
                        src = wipd.ap().rearrange("p k m -> p (k m)")
                    mk_load[ob] = I(sync.dma_start(proj32[si % 2][:, 0:wfull], src),
                                    c_hw, 16)

                @blk.vector
                def _(v, si=si, ob=ob, wfull=wfull):
                    W(v, "vec", c_hw, mk_load[ob])
                    if si >= 2:
                        W(v, "vec", c_pe, mk_mm[order[si - 2]])
                    v.tensor_copy(projh[si % 2][:, 0:wfull], proj32[si % 2][:, 0:wfull])
                    i = v.tensor_sub(projl[si % 2][:, 0:wfull],
                                     proj32[si % 2][:, 0:wfull],
                                     projh[si % 2][:, 0:wfull])
                    mk_lo[ob] = I(i, c_vec)

            emit_load_split(0, order[0], 1024)
            if ln_m is not None:
                emit_ln_norm(blk, ln_m, mk_xlo)

            for si, ob in enumerate(order):
                wfull = 2048 if ob < 66 else 1024
                mtile = 128 if ob < 66 else 64

                if si >= 1:
                    emit_load_split(si, ob, wfull)
                    prev = order[si - 1]
                    _emit_proj_evac(blk, prev, mk_mm, mk_ev, pslot=(si - 1) % 4)
                    if 32 <= prev < 66:
                        _emit_conv_tile(blk, prev - 32, mk_ev, mkc)
                if si == 2:
                    _emit_dt_tail(blk, mk_ev, mkc)
                if si == 6:
                    _emit_btcb_tail(blk, mk_ev, mkc)

                @blk.tensor
                def _(t, si=si, ob=ob, mtile=mtile):
                    W(t, "pe", c_vec, mk_lo[ob])
                    if si >= 4:
                        W(t, "pe", c_act, mk_ev[order[si - 4]])
                    pp = pb[si % 4][0:mtile, 0:256]
                    for kb in range(16):
                        if si == 0 and ln_m is not None:
                            W(t, "pe", c_vec, mk_xlo[kb])
                        wh = projh[si % 2][:, kb * mtile:(kb + 1) * mtile]
                        wl = projl[si % 2][:, kb * mtile:(kb + 1) * mtile]
                        xh = xnp[:, kb * 512:kb * 512 + 256]
                        xl = xnp[:, kb * 512 + 256:kb * 512 + 512]
                        t.matmul(pp, wh, xh, start=(kb == 0), stop=False)
                        t.matmul(pp, wh, xl, start=False, stop=False)
                        i = t.matmul(pp, wl, xh, start=False, stop=(kb == 15))
                    mk_mm[ob] = I(i, c_pe)

            _emit_proj_evac(blk, order[-1], mk_mm, mk_ev, pslot=(len(order) - 1) % 4)
            _emit_conv_tile(blk, order[-1] - 32, mk_ev, mkc)

    def _emit_dt_tail(blk, mk_ev, mkc):
        @blk.vector
        def _(v):
            W(v, "vec", c_act, mk_ev[66])
            i = v.tensor_scalar(escr[:, :], dt32[:, :], negA[:, 0:1], None,
                                ALU.mult)
            mkc["a"] = I(i, c_vec)

        @blk.scalar
        def _(s):
            W(s, "act", c_vec, mkc["a"])
            i = s.activation(ea_fm[:, :], escr[:, :], AT.Exp)
            mkc["ea"] = I(i, c_act)

        @blk.tensor
        def _(t):
            W(t, "pe", c_act, mk_ev[66])
            for sc in range(2):
                i = t.transpose(pb[4][:, sc * 64:sc * 64 + 64],
                                dt32[0:64, sc * 128:(sc + 1) * 128],
                                ident[0:64, 0:64])
            mkc["dtT"] = I(i, c_pe)

        @blk.vector
        def _(v):
            W(v, "vec", c_pe, mkc["dtT"])
            v.tensor_copy(dt_tm[:, 0:64], pb[4][:, 0:64])
            i = v.tensor_copy(dt_tm[:, 64:128], pb[4][:, 64:128])
            mkc["dtmev"] = I(i, c_vec)

    def _emit_btcb_tail(blk, mk_ev, mkc):
        @blk.tensor
        def _(t):
            offB = 32 * XBCW
            W(t, "pe", c_act, mkc["si33"])
            for sc in range(2):
                i = t.transpose(pb[5][:, sc * 128:sc * 128 + 128],
                                xbc[:, offB + 3 + sc * 128:offB + 3 + (sc + 1) * 128],
                                ident[:, :])
            mkc["btmT"] = I(i, c_pe)

        @blk.vector
        def _(v):
            W(v, "vec", c_pe, mkc["btmT"])
            v.tensor_copy(b_tm[:, 0:128], pb[5][:, 0:128])
            i = v.tensor_copy(b_tm[:, 128:256], pb[5][:, 128:256])
            mkc["btmev"] = I(i, c_vec)

        @blk.tensor
        def _(t):
            offB = 32 * XBCW
            offC = 33 * XBCW
            W(t, "pe", c_vec, mkc["dtmev"])
            W(t, "pe", c_vec, mkc["btmev"])
            t.matmul(pb[4][:, 256:512],
                     xbc[:, offB + 3:offB + 131],
                     xbc[:, offC + 3:offC + XBCW], start=True, stop=True)
            i = t.matmul(pb[5][:, 256:512],
                         xbc[:, offB + 131:offB + 259],
                         xbc[:, offC + 3:offC + XBCW], start=True, stop=True)
            mkc["cbtmm"] = I(i, c_pe)

        @blk.vector
        def _(v):
            W(v, "vec", c_pe, mkc["cbtmm"])
            v.tensor_copy(cbt[:, 0:256], pb[4][:, 256:512])
            i = v.tensor_copy(cbt[:, 256:512], pb[5][:, 256:512])
            mkc["cbtev"] = I(i, c_vec)

    def emit_ssd(blk, ch):
        if True:
            mk_grp = {}
            mk_eab = {}
            mk_sdo = {}
            mk_ldec = {}
            mk_bd = {}
            mk_gt = {}
            mk_xt = {}
            mk_xe = {}
            mk_ymm = {}
            mk_ysum = {}
            mk_gadd = {}
            mk_comb = {}
            mk_upd = {}

            def stage_front(h):
                par = h % 2
                g4 = h // 4
                bp = (h % 2) * 64
                fc = h // 2
                off = fc * XBCW
                if h % 4 == 0:
                    @blk.gpsimd
                    def _(g, h=h, g4=g4):
                        if g4 >= 2:
                            W(g, "gp", c_pe, mk_eab[4 * (g4 - 2) + 3])
                        base = (g4 % 2) * 1024
                        mk_grp[g4] = I(g.dma_start(
                            ea_p0[0:1, base:base + 1024],
                            ea_fm[4 * g4:4 * g4 + 4, :]), c_sw, 16)

                @blk.tensor
                def _(t, h=h, par=par, g4=g4, bp=bp, off=off):
                    W(t, "pe", c_sw, mk_grp[g4])
                    if h >= 2:
                        W(t, "pe", c_vec, mk_sdo[h - 2])
                        W(t, "pe", c_act, mk_xe[h - 2])
                    base = (g4 % 2) * 1024 + (h % 4) * 256
                    i = t.matmul(pb[par][:, 0:256],
                                 onesr[0:1, :], ea_p0[0:1, base:base + 256],
                                 start=True, stop=True)
                    mk_eab[h] = I(i, c_pe)
                    for sc in range(2):
                        i = t.transpose(
                            pb[4 + par][:, sc * 64:sc * 64 + 64],
                            xbc[bp:bp + 64, off + 3 + sc * 128:off + 3 + (sc + 1) * 128],
                            ident[bp:bp + 64, bp:bp + 64])
                    mk_xt[h] = I(i, c_pe)

                @blk.vector
                def _(v, h=h, par=par):
                    W(v, "vec", c_pe, mk_eab[h])
                    if h >= 2:
                        W(v, "vec", c_pe, mk_ymm[h - 2])
                        W(v, "vec", c_vec, mk_upd[h - 2])
                    eab = pb[par][:, 0:256]
                    v.tensor_tensor_scan(ldec[par][:, 0:256], eab, diag0[:, :],
                                         0.0, ALU.mult, ALU.add)
                    i = v.tensor_tensor_scan(ldec[par][:, 256:512], eab,
                                             diag1[:, :], 0.0, ALU.mult, ALU.add)
                    mk_ldec[h] = I(i, c_vec)
                    i = v.tensor_tensor_scan(sdo[par][:, :], eab, zeros[:, :],
                                             1.0, ALU.mult, ALU.add)
                    mk_sdo[h] = I(i, c_vec)

                @blk.scalar
                def _(s, h=h, par=par):
                    W(s, "act", c_vec, mk_ldec[h])
                    if h >= 2:
                        W(s, "act", c_pe, mk_ymm[h - 2])
                    s.mul(bd_tm[par][:, 0:128], b_tm[:, 0:128],
                          ldec[par][:, 255:256])
                    i = s.mul(bd_tm[par][:, 128:256], b_tm[:, 128:256],
                              ldec[par][:, 511:512])
                    mk_bd[h] = I(i, c_act)
                    W(s, "act", c_pe, mk_xt[h])
                    if h >= 2:
                        W(s, "act", c_pe, mk_ymm[h - 2])
                    for sc in range(2):
                        i = s.mul(xdt[par][:, sc * 64:sc * 64 + 64],
                                  pb[4 + par][:, sc * 64:sc * 64 + 64],
                                  dt_tm[:, sc * 64 + h:sc * 64 + h + 1])
                    mk_xe[h] = I(i, c_act)

                @blk.gpsimd
                def _(g, h=h, par=par):
                    W(g, "gp", c_act, mk_bd[h])
                    g.tensor_mul(ldec[par][:, 0:256], ldec[par][:, 0:256],
                                 cbt[:, 0:256])
                    i = g.tensor_mul(ldec[par][:, 256:512], ldec[par][:, 256:512],
                                     cbt[:, 256:512])
                    mk_gt[h] = I(i, c_gp)

            def stage_back(h):
                par = h % 2
                fc = h // 2
                bp = (h % 2) * 64
                off = fc * XBCW

                @blk.tensor
                def _(t, h=h, par=par, bp=bp):
                    W(t, "pe", c_gp, mk_gt[h])
                    W(t, "pe", c_act, mk_xe[h])
                    if h >= 2:
                        W(t, "pe", c_vec, mk_comb[h - 2])
                        W(t, "pe", c_vec, mk_upd[h - 2])
                    for sc in range(2):
                        t.matmul(pb[2 + par][bp:bp + 64, 0:256],
                                 xdt[par][:, sc * 64:sc * 64 + 64],
                                 ldec[par][:, sc * 256:sc * 256 + 256],
                                 start=(sc == 0), stop=(sc == 1))
                    for sc in range(2):
                        t.matmul(pb[6 + par][:, 0:64],
                                 bd_tm[par][:, sc * 128:sc * 128 + 128],
                                 xdt[par][:, sc * 64:sc * 64 + 64],
                                 start=(sc == 0), stop=(sc == 1))
                    i = t.matmul(pb[2 + par][bp:bp + 64, 256:512],
                                 s_all[:, h * 64:h * 64 + 64],
                                 xbc[:, 33 * XBCW + 3:33 * XBCW + XBCW],
                                 start=True, stop=True)
                    mk_ymm[h] = I(i, c_pe)

                @blk.vector
                def _(v, h=h, par=par, fc=fc, bp=bp, off=off):
                    W(v, "vec", c_pe, mk_ymm[h])
                    ysl = y_fm[bp:bp + 64, fc * T:(fc + 1) * T]
                    v.tensor_mul(ysl, pb[2 + par][bp:bp + 64, 256:512],
                                 sdo[par][bp:bp + 64, :])
                    v.tensor_add(ysl, ysl, pb[2 + par][bp:bp + 64, 0:256])
                    i = v.scalar_tensor_tensor(
                        ysl, xbc[bp:bp + 64, off + 3:off + XBCW],
                        dexpb[bp:bp + 64, h:h + 1], ysl, ALU.mult, ALU.add)
                    mk_comb[h] = I(i, c_vec)

                @blk.vector
                def _(v, h=h, par=par):
                    W(v, "vec", c_pe, mk_ymm[h])
                    i = v.scalar_tensor_tensor(
                        s_all[:, h * 64:h * 64 + 64], s_all[:, h * 64:h * 64 + 64],
                        sdo[par][:, 255:256],
                        pb[6 + par][:, 0:64],
                        ALU.mult, ALU.add)
                    mk_upd[h] = I(i, c_vec)

            mk_sg = {}
            mk_gu = {}
            mk_yh = {}
            mk_ylo = {}
            mk_rsq = {}
            mk_rac = {}

            def stage_gate(yc):
                par = yc % 2

                @blk.scalar
                def _(s, yc=yc, par=par):
                    if yc >= 2:
                        W(s, "act", c_vec, mk_gu[yc - 2])
                    i = s.activation(tg[par][:, :],
                                     gate_fm[:, yc * T:(yc + 1) * T], AT.Silu)
                    mk_sg[yc] = I(i, c_act)

                @blk.vector
                def _(v, yc=yc, par=par):
                    W(v, "vec", c_act, mk_sg[yc])
                    W(v, "vec", c_vec, mk_comb[2 * yc + 1])
                    if yc >= 2:
                        W(v, "vec", c_act, mk_yh[yc - 2])
                    i = v.scalar_tensor_tensor(tu[par][:, :],
                                               y_fm[:, yc * T:(yc + 1) * T],
                                               nw_sb[:, yc:yc + 1], tg[par][:, :],
                                               ALU.mult, ALU.mult)
                    mk_gu[yc] = I(i, c_vec)

                @blk.scalar
                def _(s, yc=yc, par=par):
                    W(s, "act", c_vec, mk_gu[yc])
                    i = s.copy(ygp[:, yc * 512:yc * 512 + 256], tu[par][:, :])
                    mk_yh[yc] = I(i, c_act)

                @blk.vector
                def _(v, yc=yc, par=par):
                    W(v, "vec", c_act, mk_yh[yc])
                    i = v.tensor_sub(ygp[:, yc * 512 + 256:yc * 512 + 512],
                                     tu[par][:, :],
                                     ygp[:, yc * 512:yc * 512 + 256])
                    mk_ylo[yc] = I(i, c_vec)

                @blk.scalar
                def _(s, yc=yc, par=par):
                    if yc >= 2:
                        W(s, "act", c_gp, mk_rac[yc - 2])
                    i = s.activation(sqt[:, par * 256:par * 256 + 256],
                                     y_fm[:, yc * T:(yc + 1) * T], AT.Square)
                    mk_rsq[yc] = I(i, c_act)

                @blk.gpsimd
                def _(g, yc=yc, par=par):
                    W(g, "gp", c_act, mk_rsq[yc])
                    if yc == 0:
                        i = g.tensor_copy(lnacc[:, 0:256], sqt[:, 0:256])
                    else:
                        i = g.tensor_add(lnacc[:, 0:256], lnacc[:, 0:256],
                                         sqt[:, par * 256:par * 256 + 256])
                    mk_rac[yc] = I(i, c_gp)

            for h in range(NH + 1):
                if h < NH:
                    stage_front(h)
                if h >= 1:
                    stage_back(h - 1)
                    if (h - 1) % 2 == 1:
                        stage_gate((h - 1) // 2)
            return {"rac": mk_rac}

    def emit_outproj(blk, ssdm, ln_m=None):
        if True:
            mk_load = {}
            mk_hi = {}
            mk_lo = {}
            mk_mm = {}
            mk_ev = {}
            mr = {}

            @blk.tensor
            def _(t):
                W(t, "pe", c_gp, lambda: ssdm["rac"][31])
                i = t.matmul(pb[3][0:1, 0:256], onesc[:, 0:1], lnacc[:, 0:256],
                             start=True, stop=True)
                mr["mm31"] = I(i, c_pe)

            @blk.scalar
            def _(s):
                W(s, "act", c_pe, mr["mm31"])
                i = s.activation(rowB[0:1, :], pb[3][0:1, 0:256], AT.Sqrt,
                                 bias=epscol[0:1, 0:1], scale=1.0 / 4096)
                mr["sqrt"] = I(i, c_act)

            @blk.vector
            def _(v):
                W(v, "vec", c_act, mr["sqrt"])
                i = v.reciprocal(rowC[0:1, :], rowB[0:1, :])
                mr["recip"] = I(i, c_vec)

            @blk.tensor
            def _(t):
                W(t, "pe", c_vec, mr["recip"])
                i = t.matmul(pb[7][:, 0:256], onesr[0:1, :], rowC[0:1, :],
                             start=True, stop=True)
                mr["bcast"] = I(i, c_pe)

            @blk.vector
            def _(v):
                W(v, "vec", c_pe, mr["bcast"])
                i = v.tensor_copy(sdo[0][:, :], pb[7][:, 0:256])
                mr["rstdb"] = I(i, c_vec)
            def _emit_op_evac(blk, oc):
                @blk.vector
                def _(v, oc=oc):
                    W(v, "vec", c_pe, mk_mm[2 * oc + 1])
                    sl = slice(oc * T, (oc + 1) * T)
                    v.tensor_mul(tu[oc % 2][:, :], pb[4 + oc % 2][:, 0:256],
                                 sdo[0][:, :])
                    i = v.tensor_add(x_fm[:, sl], tu[oc % 2][:, :], x_fm[:, sl])
                    mk_ev[oc] = I(i, c_vec)

                if ln_m is not None:
                    fcw = lambda eng, key, fc: W(eng, key, c_vec, mk_ev[fc])
                    emit_ln_sq_fc(blk, oc, ln_m, fcw)
                    emit_ln_acc_fc(blk, oc, ln_m, fcw)

            for hs in range(32):
                oc, kh = hs // 2, hs % 2

                @blk.sync
                def _(sync, hs=hs, oc=oc, kh=kh):
                    if hs >= 2:
                        W(sync, "sy", c_vec, mk_lo[hs - 2])
                    src = wop.ap()[oc, kh].rearrange("p k m -> p (k m)")
                    mk_load[hs] = I(sync.dma_start(proj32[hs % 2][:, :], src),
                                    c_hw, 16)

                @blk.vector
                def _(v, hs=hs):
                    W(v, "vec", c_hw, mk_load[hs])
                    if hs >= 2:
                        W(v, "vec", c_pe, mk_mm[hs - 2])
                    v.tensor_copy(projh[hs % 2][:, :], proj32[hs % 2][:, :])
                    i = v.tensor_sub(projl[hs % 2][:, :], proj32[hs % 2][:, :],
                                     projh[hs % 2][:, :])
                    mk_lo[hs] = I(i, c_vec)

                if hs >= 1 and hs % 2 == 0:
                    _emit_op_evac(blk, hs // 2 - 1)

                @blk.tensor
                def _(t, hs=hs, oc=oc, kh=kh):
                    W(t, "pe", c_vec, mk_lo[hs])
                    if kh == 0 and oc >= 2:
                        W(t, "pe", c_vec, mk_ev[oc - 2])
                    pd = pb[4 + oc % 2][:, 0:256]
                    for kb in range(16):
                        kg = kh * 16 + kb
                        wh = projh[hs % 2][:, kb * 128:(kb + 1) * 128]
                        wl = projl[hs % 2][:, kb * 128:(kb + 1) * 128]
                        gh = ygp[:, kg * 512:kg * 512 + 256]
                        gl = ygp[:, kg * 512 + 256:kg * 512 + 512]
                        t.matmul(pd, wh, gh, start=(kh == 0 and kb == 0), stop=False)
                        t.matmul(pd, wh, gl, start=False, stop=False)
                        i = t.matmul(pd, wl, gh, start=False,
                                     stop=(kh == 1 and kb == 15))
                    mk_mm[hs] = I(i, c_pe)

            _emit_op_evac(blk, 15)
            return mk_ev, mk_lo, mk_mm, mr

    st_mark = 0
    for ch in range(n_chunks):
        pref_gu1, pref_dn1, pref_gu2, pref_dn2 = {}, {}, {}, {}
        mxbox = {}

        m1, m2, m3 = {}, {}, {}
        with phase("loadln1_c%d" % ch) as blk:
            @blk.sync
            def _(sync, ch=ch):
                W(sync, "sy", c_st, st_mark)
                srcx = xt.ap()[:, ch * T:(ch + 1) * T].rearrange(
                    "(fc p) t -> p fc t", p=128)
                dstx = x_fm[:, :].rearrange("p (fc t) -> p fc t", fc=FCX)
                mxbox["x"] = I(sync.dma_start(dstx, srcx), c_hw, 16)
            if do_ffn1:
                emit_gateup_prefetch(blk, pref_gu1)
            fcw_x = lambda eng, key, fc: W(eng, key, c_hw, lambda: mxbox["x"])
            for fc in range(FCX):
                emit_ln_sq_fc(blk, fc, m1, fcw_x)
                emit_ln_acc_fc(blk, fc, m1, fcw_x)

        if do_ffn1:
            with phase("ffn1_c%d" % ch) as blk:
                emit_ln_stats(blk, m1)
                mk_xlo1 = {}
                emit_ln_norm(blk, m1, mk_xlo1)
                gum1 = {}
                fhi, flo = emit_ffn_gateup(blk, pref_gu1, pref_dn1, gum1,
                                           xnp_marks=mk_xlo1)
                dn_ev = emit_ffn_down(blk, pref_dn1, fhi, flo,
                                      ln_m=(m2 if do_mamba else None))
        if do_mamba:
            with phase("inproj_c%d" % ch) as blk:
                emit_inproj(blk, ch, ln_m=(m2 if do_ffn1 else m1))
            ssdm = {}
            with phase("ssdrms_c%d" % ch) as blk:
                ssdm.update(emit_ssd(blk, ch))
            with phase("outprojln3_c%d" % ch) as blk:
                op_ev, op_lo, op_mm, op_mr = emit_outproj(
                    blk, ssdm, ln_m=(m3 if do_ffn2 else None))
                if do_ffn2:
                    emit_gateup_prefetch(blk, pref_gu2,
                                         waits=((c_vec, lambda: op_lo[31]),
                                                (c_pe, lambda: op_mm[31])))
        if do_ffn2:
            with phase("ffn2_c%d" % ch) as blk:
                emit_ln_stats(blk, m3)
                mk_xlo3 = {}
                emit_ln_norm(blk, m3, mk_xlo3)
                gum2 = {}
                fhi2, flo2 = emit_ffn_gateup(blk, pref_gu2, pref_dn2, gum2,
                                             xnp_marks=mk_xlo3)
                dn_ev2 = emit_ffn_down(blk, pref_dn2, fhi2, flo2)

                @blk.sync
                def _(sync, ch=ch):
                    W(sync, "sy", c_vec, lambda: dn_ev2[15])
                    dsty = yt.ap()[:, ch * T:(ch + 1) * T].rearrange(
                        "(fc p) t -> p fc t", p=128)
                    srcy = x_fm[:, :].rearrange("p (fc t) -> p fc t", fc=FCX)
                    I(sync.dma_start(dsty, srcy), c_st, 16)
        st_mark = c_st.v

    with nc.Block() as blk:
        @blk.sync
        def _(sync):
            sync.wait_ge(c_st.h, c_st.v)

    return nc



def prep_in_maps(L0_attn_hidden_states, L0_attn_in_proj_w, L0_attn_conv_w,
                 L0_attn_conv_b, L0_attn_dt_bias, L0_attn_A_log, L0_attn_D,
                 L0_attn_norm_weight, L0_attn_out_proj_w, L0_ffn_w_gate,
                 L0_ffn_w_up, L0_ffn_w_down, L0_ffn_s_gate, L0_ffn_s_up,
                 L0_ffn_s_down):
    x = np.ascontiguousarray(np.asarray(L0_attn_hidden_states, dtype=np.float32))
    wip = np.asarray(L0_attn_in_proj_w, dtype=np.float32)
    wout = np.asarray(L0_attn_out_proj_w, dtype=np.float32)
    cw = np.asarray(L0_attn_conv_w, dtype=np.float32)
    cb = np.asarray(L0_attn_conv_b, dtype=np.float32)
    dtb = np.asarray(L0_attn_dt_bias, dtype=np.float32)
    alog = np.asarray(L0_attn_A_log, dtype=np.float32)
    dvec = np.asarray(L0_attn_D, dtype=np.float32)
    nw = np.asarray(L0_attn_norm_weight, dtype=np.float32)
    wg = np.asarray(L0_ffn_w_gate)
    wu = np.asarray(L0_ffn_w_up)
    wd = np.asarray(L0_ffn_w_down)
    sgt = np.asarray(L0_ffn_s_gate, dtype=np.float32)
    sut = np.asarray(L0_ffn_s_up, dtype=np.float32)
    sdt = np.asarray(L0_ffn_s_down, dtype=np.float32)

    wipm = np.ascontiguousarray(
        wip[:, :8448].reshape(16, 128, 66, 128).transpose(2, 1, 0, 3))
    wipd = np.ascontiguousarray(
        wip[:, 8448:].reshape(16, 128, 64).transpose(1, 0, 2))
    wopr = np.ascontiguousarray(
        wout.reshape(2, 16, 128, 16, 128).transpose(3, 0, 2, 1, 4))
    wg_r = np.ascontiguousarray(wg.reshape(16, 128, 64, 128).transpose(2, 1, 0, 3))
    wu_r = np.ascontiguousarray(wu.reshape(16, 128, 64, 128).transpose(2, 1, 0, 3))
    wd_r = np.ascontiguousarray(wd.reshape(64, 128, 16, 128).transpose(2, 1, 0, 3))

    shared = {
        "wipm": wipm, "wipd": wipd, "wop": wopr,
        "wg8": wg_r, "wu8": wu_r, "wd8": wd_r,
        "sgate_d": np.ascontiguousarray(sgt.reshape(64, 128).T),
        "sup_d": np.ascontiguousarray(sut.reshape(64, 128).T),
        "sdownh_d": np.ascontiguousarray((0.5 * sdt).reshape(16, 128).T),
        "nw_d": np.ascontiguousarray(nw.reshape(32, 128).T),
        "convb_d": np.ascontiguousarray(cb.reshape(34, 128).T),
        "convw_d": np.ascontiguousarray(
            cw.reshape(34, 128, 4).transpose(1, 0, 2).reshape(128, 136)),
        "dtb_d": dtb.reshape(64, 1),
        "alog_d": alog.reshape(64, 1),
        "dvec_d": dvec.reshape(1, 64),
    }
    in_maps = []
    for i in range(8):
        d = dict(shared)
        d["xt"] = np.ascontiguousarray(x[i].T)
        in_maps.append(d)
    return in_maps


def kernel(**inputs):
    in_maps = prep_in_maps(**inputs)
    nc = build_nc()
    res = run_bass_kernel_spmd(nc, in_maps, core_ids=list(range(8)))
    out = np.empty((8, NTOK, HID), dtype=np.float32)
    for i in range(8):
        out[i] = np.asarray(res.results[i]["yt"]).T
    return out
